# version 01
# baseline (speedup 1.0000x reference)
import numpy as np
from contextlib import ExitStack

import concourse.bass as bass
import concourse.mybir as mybir
from concourse.bass import IndirectOffsetOnAxis
from concourse.bass_utils import run_bass_kernel_spmd

F32 = mybir.dt.float32
BF16 = mybir.dt.bfloat16
I32 = mybir.dt.int32
AF = mybir.ActivationFunctionType
ALU = mybir.AluOpType
AXX = mybir.AxisListType.X

B, N, D = 4, 4096, 1024
H, KVH, DH = 16, 4, 64
NT = N // 128
NSLOT = 16
FF, NE = 4096, 8
EPS1, EPS2 = 1e-5, 1.1920929e-07

ENGS = ("t", "v", "c", "s", "g")


class Bld:

    def __init__(self, nc):
        self.nc = nc
        self.progs = {e: [] for e in ENGS}
        self.sem = {e: nc.alloc_semaphore(f"sem_{e}") for e in ENGS}
        self.cnt = {e: 0 for e in ENGS}
        self.ext = {}
        self.lastw = {}
        self.readers = {}
        self.satisfied = {e: {} for e in ENGS}

    def _wait_tok(self, eng, tok, sem, val):
        if val <= 0:
            return
        if self.satisfied[eng].get(tok, 0) >= val:
            return
        self.progs[eng].append(lambda E, s=sem, c=val: E.wait_ge(s, c))
        self.satisfied[eng][tok] = val

    def _wait_prod(self, eng, prod):
        kind = prod[0]
        if kind == "eng":
            _, peng, pcnt = prod
            if peng == eng:
                if eng == "t":
                    return
                self._wait_tok(eng, ("e", eng), self.sem[eng], pcnt)
                return
            self._wait_tok(eng, ("e", peng), self.sem[peng], pcnt)
        else:
            _, semid, pcnt = prod
            sem = self.ext[semid][0]
            self._wait_tok(eng, ("x", semid), sem, pcnt)

    def op(self, eng, fn, reads=(), writes=(), inc=1, sem=None):
        for k in reads:
            lw = self.lastw.get(k)
            if lw:
                self._wait_prod(eng, lw)
        for k in writes:
            lw = self.lastw.get(k)
            if lw:
                self._wait_prod(eng, lw)
            for pe, pc in self.readers.get(k, {}).items():
                self._wait_prod(eng, ("eng", pe, pc))
        if sem is None:
            self.cnt[eng] += inc
            ss = self.sem[eng]
            prod = ("eng", eng, self.cnt[eng])
        else:
            rec = self.ext.setdefault(id(sem), [sem, 0])
            rec[1] += inc
            ss = sem
            prod = ("ext", id(sem), rec[1])
        self.progs[eng].append(lambda E, f=fn, s2=ss, i=inc: f(E).then_inc(s2, i))
        mycnt = self.cnt[eng]
        for k in reads:
            self.readers.setdefault(k, {})[eng] = mycnt
        for k in writes:
            self.lastw[k] = prod
            self.readers[k] = {}
        return prod

    def raw(self, eng, semh, val):
        self.progs[eng].append(lambda E, s=semh, c=val: E.wait_ge(s, c))

    def barrier_into(self, eng):
        for pe in ENGS:
            if pe != eng:
                self._wait_tok(eng, ("e", pe), self.sem[pe], self.cnt[pe])
        for semid, (sem, val) in self.ext.items():
            self._wait_tok(eng, ("x", semid), sem, val)


def _build(upto="full"):
    LV = {"phase1": 1, "attn": 2, "oproj": 3, "route": 4, "full": 5}[upto]
    nc = bass.Bass("TRN2", target_bir_lowering=False)

    d_x = nc.dram_tensor("x", [N, D], F32, kind="ExternalInput")
    d_wq = nc.dram_tensor("wq", [D, D], F32, kind="ExternalInput")
    d_wk = nc.dram_tensor("wk", [D, 256], F32, kind="ExternalInput")
    d_wv = nc.dram_tensor("wv", [D, 256], F32, kind="ExternalInput")
    d_wo = nc.dram_tensor("wo", [D, D], F32, kind="ExternalInput")
    d_nwT = nc.dram_tensor("nwT", [128, 8], F32, kind="ExternalInput")
    d_rfT = nc.dram_tensor("rfT", [128, N], F32, kind="ExternalInput")
    d_zf = nc.dram_tensor("zf", [128, 128], F32, kind="ExternalInput")
    d_wr = nc.dram_tensor("wr", [128, 64], F32, kind="ExternalInput")
    d_W1 = nc.dram_tensor("W1", [NE, D, FF], F32, kind="ExternalInput")
    d_b1 = nc.dram_tensor("b1", [NE, FF], F32, kind="ExternalInput")
    d_W2 = nc.dram_tensor("W2", [NE, FF, D], F32, kind="ExternalInput")
    d_b2 = nc.dram_tensor("b2", [NE, D], F32, kind="ExternalInput")
    d_out = nc.dram_tensor("out", [NSLOT * 128, D], F32, kind="ExternalOutput")

    tri_np = np.tril(np.ones((128, 128), np.float32)).T
    d_tri = nc.inline_tensor(tri_np, "triconst")
    iotas_np = (np.arange(32)[None, :] * 128 + np.arange(128)[:, None]).astype(np.float32)
    d_iotas = nc.inline_tensor(iotas_np, "iotasconst")
    d_io8 = nc.inline_tensor(np.arange(8, dtype=np.float32).reshape(1, 8), "io8const")
    d_id = nc.inline_tensor(np.eye(128, dtype=np.float32), "idconst")

    cc_in = nc.dram_tensor("cc_in", [1, D], F32)
    cc_out = nc.dram_tensor("cc_out", [1, D], F32)

    dbg = {}
    if LV < 5:
        dbg["r1"] = nc.dram_tensor("dbg_r1", [NSLOT * 128, D], F32, kind="ExternalOutput")
        dbg["qt"] = nc.dram_tensor("dbg_qt", [128, 8, 2048], F32, kind="ExternalOutput")
        dbg["kt"] = nc.dram_tensor("dbg_kt", [128, 2, N], F32, kind="ExternalOutput")
        dbg["va"] = nc.dram_tensor("dbg_va", [128, 32, 260], F32, kind="ExternalOutput")
        dbg["ft"] = nc.dram_tensor("dbg_ft", [128, 8, 2048], F32, kind="ExternalOutput")
        dbg["rt"] = nc.dram_tensor("dbg_rt", [128, 8], F32, kind="ExternalOutput")

    W1v = d_W1.ap().rearrange("e d f -> (e d) f")
    W2v = d_W2.ap().rearrange("e f d -> (e f) d")
    b1v = d_b1.ap().rearrange("e (r p) -> (e r) p", p=128)

    with ExitStack() as ctx:
        e = ctx.enter_context
        r1 = e(nc.sbuf_tensor("sb_r1", [128, NSLOT, D], F32))
        id128 = e(nc.sbuf_tensor("sb_id128", [128, 128], F32))
        tri_b = e(nc.sbuf_tensor("sb_tri_b", [128, 128], BF16))
        zf_b = e(nc.sbuf_tensor("sb_zf_b", [128, 128], BF16))
        iotas = e(nc.sbuf_tensor("sb_iotas", [128, 32], F32))
        io8 = e(nc.sbuf_tensor("sb_io8", [1, 8], F32))
        nwT = e(nc.sbuf_tensor("sb_nwT", [128, 8], F32))
        onesf = e(nc.sbuf_tensor("sb_onesf", [128, 1], F32))
        onesb = e(nc.sbuf_tensor("sb_onesb", [128, 1], BF16))
        ones_r = e(nc.sbuf_tensor("sb_ones_r", [1, 128], F32))
        negM0 = e(nc.sbuf_tensor("sb_negM0", [128, 1], F32))
        rt = e(nc.sbuf_tensor("sb_rt", [128, 8], F32))
        wr_s = e(nc.sbuf_tensor("sb_wr_s", [128, 64], F32))
        b2c = e(nc.sbuf_tensor("sb_b2c", [128, D], F32))
        poolT = e(nc.sbuf_tensor("sb_poolT", [128, 8], F32))
        idxW1 = e(nc.sbuf_tensor("sb_idxW1", [128, 2, 8], I32))
        idxW2 = e(nc.sbuf_tensor("sb_idxW2", [128, 2, 32], I32))
        idxb1 = e(nc.sbuf_tensor("sb_idxb1", [32, 2], I32))
        idxb2 = e(nc.sbuf_tensor("sb_idxb2", [8, 2], I32))
        idxf = e(nc.sbuf_tensor("sb_idxf", [128, 32], F32))
        fT = e(nc.sbuf_tensor("sb_fT", [128, 8, 2048], BF16))

        bld = Bld(nc)
        op, raw = bld.op, bld.raw
        OUTSEM = nc.alloc_semaphore("outsem")
        CCSEM = nc.alloc_semaphore("ccsem")
        XSEM = [nc.alloc_semaphore("xsa"), nc.alloc_semaphore("xsb")]
        RSEM = nc.alloc_semaphore("rsem")
        FSEM = nc.alloc_semaphore("fsem")
        W2SEM = [nc.alloc_semaphore("w2a"), nc.alloc_semaphore("w2b")]
        RFSEM = [nc.alloc_semaphore("rfa"), nc.alloc_semaphore("rfb")]

        with ExitStack() as ctxB:
            eB = ctxB.enter_context
            QT = eB(nc.sbuf_tensor("sb_QT", [128, 8, 2048], BF16))
            KT = eB(nc.sbuf_tensor("sb_KT", [128, 2, N], BF16))
            VA = eB(nc.sbuf_tensor("sb_VA", [128, 32, 260], BF16))

            op("s", lambda E: E.dma_start(out=id128[:], in_=d_id[:, :]), writes=[("id",)], inc=16)
            op("s", lambda E: E.dma_start(out=iotas[:], in_=d_iotas[:, :]), writes=[("iotas",)], inc=16)
            op("s", lambda E: E.dma_start(out=io8[:], in_=d_io8[:, :]), writes=[("io8",)], inc=16)
            op("s", lambda E: E.dma_start(out=nwT[:], in_=d_nwT[:, :]), writes=[("nwT",)], inc=16)
            op("s", lambda E: E.dma_start(out=wr_s[:], in_=d_wr[:, :]), writes=[("wr",)], inc=16)
            op("g", lambda E: E.dma_start(out=tri_b[:], in_=d_tri[:, :]), writes=[("tri",)], inc=16)
            op("g", lambda E: E.dma_start(out=zf_b[:], in_=d_zf[:, :]), writes=[("zfb",)], inc=16)
            op("v", lambda E: E.memset(onesf[:], 1.0), writes=[("onesf",)])
            op("v", lambda E: E.memset(onesb[:], 1.0), writes=[("onesb",)])
            op("v", lambda E: E.memset(ones_r[:], 1.0), writes=[("onesr",)])
            op("v", lambda E: E.memset(VA[:, :, 64:260:65], 1.0), writes=[("vaones",)])

            with ExitStack() as ctxB1:
                eB1 = ctxB1.enter_context
                xw = eB1(nc.sbuf_tensor("sb_xw", [128, 2, D], F32))
                rfs = eB1(nc.sbuf_tensor("sb_rfs", [128, 2, 256], F32))
                xnT = eB1(nc.sbuf_tensor("sb_xnT", [128, 8, 256], BF16))
                wqb = eB1(nc.sbuf_tensor("sb_wqb", [128, 8, D], BF16))
                wkb = eB1(nc.sbuf_tensor("sb_wkb", [128, 8, 256], BF16))
                wvb = eB1(nc.sbuf_tensor("sb_wvb", [128, 8, 256], BF16))
                sc4k = eB1(nc.sbuf_tensor("sb_sc4k", [128, 512], F32))
                stats = eB1(nc.sbuf_tensor("sb_stats", [128, 2, 12], F32))
                m0sc = eB1(nc.sbuf_tensor("sb_m0sc", [1, 8], F32))
                ps1 = eB1(nc.psum_tensor("pm_ps1", [128, 4096], F32))

                for dc in range(8):
                    op("g", lambda E, dc=dc: E.dma_start(out=wqb[:, dc, :], in_=d_wq[dc * 128:(dc + 1) * 128, :]),
                       writes=[("wqb",)], inc=16)
                    op("g", lambda E, dc=dc: E.dma_start(out=wkb[:, dc, :], in_=d_wk[dc * 128:(dc + 1) * 128, :]),
                       writes=[("wkb",)], inc=16)
                    op("g", lambda E, dc=dc: E.dma_start(out=wvb[:, dc, :], in_=d_wv[dc * 128:(dc + 1) * 128, :]),
                       writes=[("wvb",)], inc=16)

                for hw in range(NT // 2):
                    t0 = 2 * hw
                    hb = hw % 2
                    op("s", lambda E, hb=hb, t0=t0: E.dma_start(
                        out=rfs[:, hb, :], in_=d_rfT[:, t0 * 128:t0 * 128 + 256]),
                       writes=[("rfs", hb)], inc=16, sem=RFSEM[hb])
                    for i in range(2):
                        tl = t0 + i
                        xin = d_x[tl * 128:(tl + 1) * 128, :]
                        op("s", lambda E, i=i, xin=xin: E.dma_start(out=xw[:, i, :], in_=xin),
                           writes=[("xw", i)], inc=16, sem=XSEM[i])
                    for i in range(2):
                        S = stats[:, i, :]
                        op("c", lambda E, i=i: E.activation(sc4k[:], xw[:, i, 0:512], AF.Square),
                           reads=[("xw", i)], writes=[("sc4k",)])
                        op("v", lambda E, S=S: E.reduce_sum(S[:, 1:2], sc4k[:], axis=AXX),
                           reads=[("sc4k",)], writes=[("st", i)])
                        op("c", lambda E, i=i: E.activation(sc4k[:], xw[:, i, 512:1024], AF.Square),
                           reads=[("xw", i)], writes=[("sc4k",)])
                        op("v", lambda E, S=S: E.reduce_sum(S[:, 4:5], sc4k[:], axis=AXX),
                           reads=[("sc4k",)], writes=[("st", i)])
                        op("v", lambda E, S=S: E.tensor_add(S[:, 1:2], S[:, 1:2], S[:, 4:5]),
                           reads=[("st", i)], writes=[("st", i)])
                        op("v", lambda E, S=S, i=i: E.reduce_sum(S[:, 0:1], xw[:, i, :], axis=AXX),
                           reads=[("xw", i)], writes=[("st", i)])
                        op("v", lambda E, S=S: E.tensor_scalar_mul(S[:, 2:3], S[:, 0:1], 1.0 / D),
                           reads=[("st", i)], writes=[("st", i)])
                        op("v", lambda E, S=S: E.tensor_scalar_mul(S[:, 3:4], S[:, 1:2], 1.0 / D),
                           reads=[("st", i)], writes=[("st", i)])
                        op("v", lambda E, S=S: E.tensor_mul(S[:, 4:5], S[:, 2:3], S[:, 2:3]),
                           reads=[("st", i)], writes=[("st", i)])
                        op("v", lambda E, S=S: E.tensor_sub(S[:, 3:4], S[:, 3:4], S[:, 4:5]),
                           reads=[("st", i)], writes=[("st", i)])
                        op("v", lambda E, S=S: E.tensor_scalar_add(S[:, 5:6], S[:, 3:4], EPS1),
                           reads=[("st", i)], writes=[("st", i)])
                        op("c", lambda E, S=S: E.activation(S[:, 6:7], S[:, 5:6], AF.Ln),
                           reads=[("st", i)], writes=[("st", i)])
                        op("c", lambda E, S=S: E.activation(S[:, 6:7], S[:, 6:7], AF.Exp, scale=-0.5),
                           reads=[("st", i)], writes=[("st", i)])
                        op("v", lambda E, S=S: E.reciprocal(S[:, 7:8], S[:, 5:6]),
                           reads=[("st", i)], writes=[("st", i)])
                        op("v", lambda E, S=S: E.tensor_mul(S[:, 7:8], S[:, 7:8], S[:, 3:4]),
                           reads=[("st", i)], writes=[("st", i)])
                        op("v", lambda E, S=S: E.tensor_scalar_add(S[:, 8:9], S[:, 7:8], EPS2),
                           reads=[("st", i)], writes=[("st", i)])
                        op("c", lambda E, S=S: E.activation(S[:, 8:9], S[:, 8:9], AF.Ln),
                           reads=[("st", i)], writes=[("st", i)])
                        op("c", lambda E, S=S: E.activation(S[:, 8:9], S[:, 8:9], AF.Exp, scale=-0.5),
                           reads=[("st", i)], writes=[("st", i)])
                        op("v", lambda E, S=S: E.tensor_mul(S[:, 9:10], S[:, 6:7], S[:, 8:9]),
                           reads=[("st", i)], writes=[("st", i)])
                        op("v", lambda E, S=S: E.tensor_scalar_add(S[:, 10:11], S[:, 6:7], 1.0),
                           reads=[("st", i)], writes=[("st", i)])
                        op("v", lambda E, S=S: E.tensor_mul(S[:, 11:12], S[:, 2:3], S[:, 6:7]),
                           reads=[("st", i)], writes=[("st", i)])
                        if i == 0:
                            op("v", lambda E, S=S, i=i, hw=hw: E.tensor_scalar(
                                r1[:, hw, :], xw[:, i, :], S[:, 10:11], S[:, 11:12],
                                ALU.mult, ALU.subtract),
                               reads=[("xw", i), ("st", i)], writes=[("r1", hw)])
                        op("v", lambda E, S=S, i=i: E.tensor_scalar(
                            xw[:, i, :], xw[:, i, :], S[:, 2:3], S[:, 9:10],
                            ALU.subtract, ALU.mult),
                           reads=[("st", i)], writes=[("xw", i)])
                        for c8 in range(8):
                            tb = c8 % 2
                            op("t", lambda E, c8=c8, tb=tb, i=i: E.transpose(
                                ps1[:, tb * 512:tb * 512 + 128], xw[:, i, c8 * 128:(c8 + 1) * 128], id128[:]),
                               reads=[("xw", i), ("id",)], writes=[("ps1", tb)])
                            op("c", lambda E, c8=c8, tb=tb, i=i: E.mul(
                                xnT[:, c8, i * 128:(i + 1) * 128], ps1[:, tb * 512:tb * 512 + 128],
                                nwT[:, c8:c8 + 1]),
                               reads=[("ps1", tb), ("nwT",)], writes=[("xnT", c8)])
                    for qc in range(8):
                        qb = 2 + qc % 2
                        for dc in range(8):
                            op("t", lambda E, qc=qc, dc=dc, qb=qb: E.matmul(
                                ps1[:, qb * 512:qb * 512 + 128], wqb[:, dc, qc * 128:(qc + 1) * 128],
                                xnT[:, dc, 0:128], start=(dc == 0), stop=(dc == 7)),
                               reads=[("wqb",), ("xnT", dc)], writes=[("ps1", qb)])
                        op("v", lambda E, qc=qc, qb=qb, hw=hw, hb=hb: E.tensor_mul(
                            QT[:, qc, hw * 128:(hw + 1) * 128], ps1[:, qb * 512:qb * 512 + 128],
                            rfs[:, hb, 0:128]),
                           reads=[("ps1", qb), ("rfs", hb)], writes=[("QT",)])
                    for kc2 in range(2):
                        kb = 4 + kc2
                        for dc in range(8):
                            op("t", lambda E, kc2=kc2, dc=dc, kb=kb: E.matmul(
                                ps1[:, kb * 512:kb * 512 + 256], wkb[:, dc, kc2 * 128:(kc2 + 1) * 128],
                                xnT[:, dc, 0:256], start=(dc == 0), stop=(dc == 7)),
                               reads=[("wkb",), ("xnT", dc)], writes=[("ps1", kb)])
                        op("v", lambda E, kc2=kc2, kb=kb, t0=t0, hb=hb: E.tensor_mul(
                            KT[:, kc2, t0 * 128:t0 * 128 + 256], ps1[:, kb * 512:kb * 512 + 256],
                            rfs[:, hb, 0:256]),
                           reads=[("ps1", kb), ("rfs", hb)], writes=[("KT",)])
                    for i in range(2):
                        vb = 6 + i
                        tl = t0 + i
                        for dc in range(8):
                            op("t", lambda E, i=i, dc=dc, vb=vb: E.matmul(
                                ps1[:, vb * 512:vb * 512 + 256], xnT[:, dc, i * 128:(i + 1) * 128],
                                wvb[:, dc, :], start=(dc == 0), stop=(dc == 7)),
                               reads=[("wvb",), ("xnT", dc)], writes=[("ps1", vb)])
                        op("v", lambda E, vb=vb, tl=tl: E.tensor_copy(
                            VA[:, tl, :].rearrange("p (k s) -> p k s", s=65)[:, :, 0:64],
                            ps1[:, vb * 512:vb * 512 + 256].rearrange("p (k s) -> p k s", s=64)),
                           reads=[("ps1", vb), ("vaones",)], writes=[("VA", tl)])

                msc = wqb
                op("v", lambda E: E.memset(m0sc[0:1, 0:1], 0.0), writes=[("m0",)])
                op("v", lambda E: E.memset(m0sc[0:1, 2:3], 0.0), writes=[("m0",)])
                for qc in range(8):
                    for half in range(2):
                        op("v", lambda E, qc=qc, half=half: E.tensor_mul(
                            msc[:, half, :], QT[:, qc, half * 1024:(half + 1) * 1024],
                            QT[:, qc, half * 1024:(half + 1) * 1024]),
                           reads=[("QT",)], writes=[("msc",), ("wqb",)])
                    for pc in range(4):
                        op("t", lambda E, pc=pc: E.matmul(
                            ps1[0:1, 0:512], onesb[:], msc[:, pc // 2, (pc % 2) * 512:(pc % 2) * 512 + 512],
                            start=True, stop=True),
                           reads=[("msc",), ("onesb",)], writes=[("ps1", 0)])
                        op("v", lambda E: E.reduce_max(m0sc[0:1, 1:2], ps1[0:1, 0:512], axis=AXX),
                           reads=[("ps1", 0)], writes=[("m0",)])
                        op("v", lambda E: E.tensor_max(m0sc[0:1, 0:1], m0sc[0:1, 0:1], m0sc[0:1, 1:2]),
                           reads=[("m0",)], writes=[("m0",)])
                for kc2 in range(2):
                    for half in range(4):
                        op("v", lambda E, kc2=kc2, half=half: E.tensor_mul(
                            msc[:, half % 2, :], KT[:, kc2, half * 1024:(half + 1) * 1024],
                            KT[:, kc2, half * 1024:(half + 1) * 1024]),
                           reads=[("KT",)], writes=[("msc",), ("wqb",)])
                        for pc in range(2):
                            op("t", lambda E, half=half, pc=pc: E.matmul(
                                ps1[0:1, 0:512], onesb[:], msc[:, half % 2, pc * 512:(pc + 1) * 512],
                                start=True, stop=True),
                               reads=[("msc",), ("onesb",)], writes=[("ps1", 0)])
                            op("v", lambda E: E.reduce_max(m0sc[0:1, 3:4], ps1[0:1, 0:512], axis=AXX),
                               reads=[("ps1", 0)], writes=[("m0",)])
                            op("v", lambda E: E.tensor_max(m0sc[0:1, 2:3], m0sc[0:1, 2:3], m0sc[0:1, 3:4]),
                               reads=[("m0",)], writes=[("m0",)])
                op("v", lambda E: E.tensor_add(m0sc[0:1, 4:5], m0sc[0:1, 0:1], m0sc[0:1, 2:3]),
                   reads=[("m0",)], writes=[("m0",)])
                op("v", lambda E: E.tensor_scalar_mul(m0sc[0:1, 4:5], m0sc[0:1, 4:5], -0.0625),
                   reads=[("m0",)], writes=[("m0",)])
                op("t", lambda E: E.matmul(ps1[:, 512:513], ones_r[:], m0sc[0:1, 4:5], start=True, stop=True),
                   reads=[("m0",), ("onesr",)], writes=[("ps1", 1)])
                op("v", lambda E: E.tensor_copy(negM0[:], ps1[:, 512:513]),
                   reads=[("ps1", 1)], writes=[("negM0",)])

                for ee in ENGS:
                    bld.barrier_into(ee)

            if LV >= 2:
                with ExitStack() as ctxB2:
                    eB2 = ctxB2.enter_context
                    Psl = eB2(nc.sbuf_tensor("sb_Psl", [128, 2, 4, 512], BF16))
                    lrow = eB2(nc.sbuf_tensor("sb_lrow", [1, 1024], F32))
                    linv = eB2(nc.sbuf_tensor("sb_linv", [1, 1024], F32))
                    lrep = eB2(nc.sbuf_tensor("sb_lrep", [128, 512], F32))
                    psA = eB2(nc.psum_tensor("pm_psA", [128, 3072], F32))
                    psO = eB2(nc.psum_tensor("pm_psO", [128, 1024], F32))

                    for g in range(4):
                        ext = 8 * g + 8
                        for c in range(8):
                            kch = c % 2
                            kv_e = (2 * c) % 4
                            batches, k0 = [], 0
                            while k0 < ext:
                                sz = 2 if (len(batches) % 2 == 0) else 1
                                sz = min(sz, ext - k0)
                                batches.append(list(range(k0, k0 + sz)))
                                k0 += sz
                            for kcs in batches:
                                slot = 0 if len(kcs) == 2 else 1
                                bank0 = 0 if slot == 0 else 4
                                for ki, kc in enumerate(kcs):
                                    for h2 in range(2):
                                        bk = bank0 + ki * 2 + h2
                                        op("t", lambda E, kch=kch, kc=kc, h2=h2, c=c, g=g, bk=bk: E.matmul(
                                            psA[:, bk * 512:(bk + 1) * 512],
                                            KT[h2 * 64:(h2 + 1) * 64, kch, kc * 128:(kc + 1) * 128],
                                            QT[h2 * 64:(h2 + 1) * 64, c, g * 512:(g + 1) * 512],
                                            start=True, stop=True, tile_position=(h2 * 64, 0)),
                                           reads=[("KT",), ("QT",)], writes=[("psA", bk)])
                                nb = len(kcs) * 2
                                op("c", lambda E, bank0=bank0, nb=nb, slot=slot: E.activation(
                                    Psl[:, slot, 0:nb, :], psA[:, bank0 * 512:(bank0 + nb) * 512],
                                    AF.Exp, bias=negM0[:, 0:1], scale=0.125),
                                   reads=[("negM0",)] + [("psA", bank0 + j) for j in range(nb)],
                                   writes=[("P", slot)])
                                for ki, kc in enumerate(kcs):
                                    if kc < 8 * g:
                                        continue
                                    n = kc - 8 * g
                                    m = n // 2
                                    for h2 in range(2):
                                        ti = ki * 2 + h2
                                        if m >= 1:
                                            op("v", lambda E, slot=slot, ti=ti, m=m: E.memset(
                                                Psl[:, slot, ti, 0:m * 128], 0.0),
                                               writes=[("P", slot)])
                                        if n % 2 == 0:
                                            msk, mk = tri_b, ("tri",)
                                        else:
                                            msk, mk = zf_b, ("zfb",)
                                        op("v", lambda E, slot=slot, ti=ti, m=m, msk=msk: E.tensor_tensor(
                                            Psl[:, slot, ti, m * 128:(m + 1) * 128],
                                            Psl[:, slot, ti, m * 128:(m + 1) * 128],
                                            msk[:, :], ALU.mult),
                                           reads=[mk], writes=[("P", slot)])
                                for ki, kc in enumerate(kcs):
                                    for h2 in range(2):
                                        kv = kv_e + h2
                                        op("t", lambda E, kc=kc, h2=h2, kv=kv, slot=slot, ki=ki, ext=ext: E.matmul(
                                            psO[0:65, h2 * 512:(h2 + 1) * 512],
                                            VA[:, kc, kv * 65:kv * 65 + 65],
                                            Psl[:, slot, ki * 2 + h2, :],
                                            start=(kc == 0), stop=(kc == ext - 1)),
                                           reads=[("P", slot), ("VA", kc)], writes=[("psO", h2)])
                            op("v", lambda E: E.tensor_copy(lrow[0:1, :], psO[64:65, :]),
                               reads=[("psO", 0), ("psO", 1)], writes=[("lrow",)])
                            op("v", lambda E: E.reciprocal(linv[0:1, :], lrow[0:1, :]),
                               reads=[("lrow",)], writes=[("linv",)])
                            for h2 in range(2):
                                op("t", lambda E, h2=h2: E.matmul(
                                    psA[:, 0:512], ones_r[:], linv[0:1, h2 * 512:(h2 + 1) * 512],
                                    start=True, stop=True),
                                   reads=[("linv",), ("onesr",)], writes=[("psA", 0)])
                                op("v", lambda E: E.tensor_copy(lrep[:], psA[:, 0:512]),
                                   reads=[("psA", 0)], writes=[("lrep",)])
                                op("v", lambda E, h2=h2, c=c, g=g: E.tensor_mul(
                                    fT[h2 * 64:h2 * 64 + 64, c, g * 512:(g + 1) * 512],
                                    psO[0:64, h2 * 512:(h2 + 1) * 512], lrep[0:64, :]),
                                   reads=[("psO", h2), ("lrep",)],
                                   writes=[("fT", 4 * g + mm) for mm in range(4)])

            if LV < 5:
                for ee in ENGS:
                    bld.barrier_into(ee)
                ns = 0
                op("g", lambda E: E.dma_start(out=dbg["qt"][:, :, :], in_=QT[:]), inc=16, sem=OUTSEM); ns += 16
                op("g", lambda E: E.dma_start(out=dbg["kt"][:, :, :], in_=KT[:]), inc=16, sem=OUTSEM); ns += 16
                op("g", lambda E: E.dma_start(out=dbg["va"][:, :, :], in_=VA[:]), inc=16, sem=OUTSEM); ns += 16
                dbg_ns = ns

            for ee in ENGS:
                bld.barrier_into(ee)
            ctxB.close()

            if LV >= 3:
                with ExitStack() as ctxC:
                    eC = ctxC.enter_context
                    wob = eC(nc.sbuf_tensor("sb_wob", [128, 8, D], BF16))
                    ln2sc = eC(nc.sbuf_tensor("sb_ln2sc", [128, D], F32))
                    st2 = eC(nc.sbuf_tensor("sb_st2", [128, 8], F32))
                    rsc = eC(nc.sbuf_tensor("sb_rsc", [1, 64], F32))
                    poolrow = eC(nc.sbuf_tensor("sb_poolrow", [1, D], F32))
                    b2r = eC(nc.sbuf_tensor("sb_b2r", [1, 2, D], F32))
                    b2g = eC(nc.sbuf_tensor("sb_b2g", [8, D], F32))
                    psC = eC(nc.psum_tensor("pm_psC", [128, 4096], F32))

                    for dc in range(8):
                        op("g", lambda E, dc=dc: E.dma_start(out=wob[:, dc, :], in_=d_wo[dc * 128:(dc + 1) * 128, :]),
                           writes=[("wob",)], inc=16)

                    for j in range(NSLOT):
                        for dh in range(2):
                            for cc in range(8):
                                op("t", lambda E, j=j, dh=dh, cc=cc: E.matmul(
                                    psC[:, dh * 512:(dh + 1) * 512],
                                    fT[:, cc, j * 128:(j + 1) * 128], wob[:, cc, dh * 512:(dh + 1) * 512],
                                    start=(cc == 0), stop=(cc == 7)),
                                   reads=[("fT", j), ("wob",)], writes=[("psC", dh)])
                        for dh in range(2):
                            op("v", lambda E, j=j, dh=dh: E.tensor_add(
                                r1[:, j, dh * 512:(dh + 1) * 512], r1[:, j, dh * 512:(dh + 1) * 512],
                                psC[:, dh * 512:(dh + 1) * 512]),
                               reads=[("psC", dh), ("r1", j)], writes=[("r1", j)])
                        op("c", lambda E, j=j: E.activation(ln2sc[:], r1[:, j, :], AF.Square),
                           reads=[("r1", j)], writes=[("ln2sc",)])
                        op("v", lambda E: E.reduce_sum(st2[:, 1:2], ln2sc[:], axis=AXX),
                           reads=[("ln2sc",)], writes=[("st2",)])
                        op("v", lambda E, j=j: E.reduce_sum(st2[:, 0:1], r1[:, j, :], axis=AXX),
                           reads=[("r1", j)], writes=[("st2",)])
                        op("v", lambda E: E.tensor_scalar_mul(st2[:, 2:3], st2[:, 0:1], 1.0 / D),
                           reads=[("st2",)], writes=[("st2",)])
                        op("v", lambda E: E.tensor_scalar_mul(st2[:, 3:4], st2[:, 1:2], 1.0 / D),
                           reads=[("st2",)], writes=[("st2",)])
                        op("v", lambda E: E.tensor_mul(st2[:, 4:5], st2[:, 2:3], st2[:, 2:3]),
                           reads=[("st2",)], writes=[("st2",)])
                        op("v", lambda E: E.tensor_sub(st2[:, 3:4], st2[:, 3:4], st2[:, 4:5]),
                           reads=[("st2",)], writes=[("st2",)])
                        op("v", lambda E: E.tensor_scalar_add(st2[:, 5:6], st2[:, 3:4], EPS1),
                           reads=[("st2",)], writes=[("st2",)])
                        op("c", lambda E: E.activation(st2[:, 6:7], st2[:, 5:6], AF.Ln),
                           reads=[("st2",)], writes=[("st2",)])
                        op("c", lambda E: E.activation(st2[:, 6:7], st2[:, 6:7], AF.Exp, scale=-0.5),
                           reads=[("st2",)], writes=[("st2",)])
                        op("v", lambda E, j=j: E.tensor_scalar(
                            ln2sc[:], r1[:, j, :], st2[:, 2:3], st2[:, 6:7], ALU.subtract, ALU.mult),
                           reads=[("r1", j), ("st2",)], writes=[("ln2sc",)])
                        for c8 in range(8):
                            tb = 2 + c8 % 2
                            op("t", lambda E, c8=c8, tb=tb: E.transpose(
                                psC[:, tb * 512:tb * 512 + 128], ln2sc[:, c8 * 128:(c8 + 1) * 128], id128[:]),
                               reads=[("ln2sc",), ("id",)], writes=[("psC", tb)])
                            op("c", lambda E, c8=c8, tb=tb, j=j: E.copy(
                                fT[:, c8, j * 128:(j + 1) * 128], psC[:, tb * 512:tb * 512 + 128]),
                               reads=[("psC", tb)], writes=[("fT", j)])
                        for dh in range(2):
                            op("t", lambda E, j=j, dh=dh: E.matmul(
                                psC[0:1, (4 + dh) * 512:(4 + dh) * 512 + 512], onesf[:],
                                ln2sc[:, dh * 512:(dh + 1) * 512], start=(j == 0), stop=(j == NSLOT - 1)),
                               reads=[("ln2sc",), ("onesf",)], writes=[("psC", 4 + dh)] if j == NSLOT - 1 else [("psCpool", 4 + dh)])

                    if LV >= 4:
                        op("v", lambda E: E.tensor_copy(poolrow[0:1, 0:512], psC[0:1, 4 * 512:4 * 512 + 512]),
                           reads=[("psC", 4)], writes=[("poolrow",)])
                        op("v", lambda E: E.tensor_copy(poolrow[0:1, 512:1024], psC[0:1, 5 * 512:5 * 512 + 512]),
                           reads=[("psC", 5)], writes=[("poolrow",)])
                        op("s", lambda E: E.dma_start(out=cc_in[:, :], in_=poolrow[:]),
                           reads=[("poolrow",)], writes=[("ccin",)], inc=16)
                        op("g", lambda E: E.collective_compute(
                            "AllReduce", ALU.add,
                            replica_groups=[[0, 1], [2, 3], [4, 5], [6, 7]],
                            ins=[cc_in.ap().opt()], outs=[cc_out.ap().opt()]),
                           reads=[("ccin",)], writes=[("ccout",)], inc=1, sem=CCSEM)
                        op("g", lambda E: E.dma_start(
                            out=poolT[:], in_=cc_out.ap().rearrange("one (p c) -> (one p) c", c=8)),
                           reads=[("ccout",)], writes=[("poolT",)], inc=16, sem=RSEM)
                        op("v", lambda E: E.tensor_scalar_mul(poolT[:], poolT[:], 1.0 / N),
                           reads=[("poolT",)], writes=[("poolT",)])
                        for cb in range(8):
                            op("t", lambda E, cb=cb: E.matmul(
                                psC[0:1, 6 * 512:6 * 512 + 8], poolT[:, cb:cb + 1], wr_s[:, cb * 8:(cb + 1) * 8],
                                start=(cb == 0), stop=(cb == 7)),
                               reads=[("poolT",), ("wr",)], writes=[("psC", 6)] if cb == 7 else [("psClg",)])
                        op("v", lambda E: E.tensor_copy(rsc[0:1, 0:8], psC[0:1, 6 * 512:6 * 512 + 8]),
                           reads=[("psC", 6)], writes=[("rsc",)])
                        op("v", lambda E: E.reduce_max(rsc[0:1, 8:9], rsc[0:1, 0:8], axis=AXX),
                           reads=[("rsc",)], writes=[("rsc",)])
                        op("v", lambda E: E.tensor_single_scalar(rsc[0:1, 16:24], rsc[0:1, 0:8], rsc[0:1, 8:9], ALU.is_ge),
                           reads=[("rsc",)], writes=[("rsc",)])
                        op("v", lambda E: E.tensor_mul(rsc[0:1, 24:32], rsc[0:1, 16:24], io8[0:1, :]),
                           reads=[("rsc",), ("io8",)], writes=[("rsc",)])
                        op("v", lambda E: E.reduce_sum(rsc[0:1, 56:57], rsc[0:1, 24:32], axis=AXX),
                           reads=[("rsc",)], writes=[("rsc",)])
                        op("v", lambda E: E.scalar_tensor_tensor(
                            rsc[0:1, 32:40], rsc[0:1, 16:24], -1e30, rsc[0:1, 0:8], ALU.mult, ALU.add),
                           reads=[("rsc",)], writes=[("rsc",)])
                        op("v", lambda E: E.reduce_max(rsc[0:1, 10:11], rsc[0:1, 32:40], axis=AXX),
                           reads=[("rsc",)], writes=[("rsc",)])
                        op("v", lambda E: E.tensor_single_scalar(rsc[0:1, 40:48], rsc[0:1, 32:40], rsc[0:1, 10:11], ALU.is_ge),
                           reads=[("rsc",)], writes=[("rsc",)])
                        op("v", lambda E: E.tensor_mul(rsc[0:1, 48:56], rsc[0:1, 40:48], io8[0:1, :]),
                           reads=[("rsc",), ("io8",)], writes=[("rsc",)])
                        op("v", lambda E: E.reduce_sum(rsc[0:1, 57:58], rsc[0:1, 48:56], axis=AXX),
                           reads=[("rsc",)], writes=[("rsc",)])
                        op("v", lambda E: E.tensor_sub(rsc[0:1, 12:13], rsc[0:1, 10:11], rsc[0:1, 8:9]),
                           reads=[("rsc",)], writes=[("rsc",)])
                        op("c", lambda E: E.activation(rsc[0:1, 13:14], rsc[0:1, 12:13], AF.Exp),
                           reads=[("rsc",)], writes=[("rsc",)])
                        op("v", lambda E: E.tensor_scalar_add(rsc[0:1, 14:15], rsc[0:1, 13:14], 1.0),
                           reads=[("rsc",)], writes=[("rsc",)])
                        op("v", lambda E: E.reciprocal(rsc[0:1, 58:59], rsc[0:1, 14:15]),
                           reads=[("rsc",)], writes=[("rsc",)])
                        op("v", lambda E: E.tensor_mul(rsc[0:1, 59:60], rsc[0:1, 13:14], rsc[0:1, 58:59]),
                           reads=[("rsc",)], writes=[("rsc",)])
                        op("t", lambda E: E.matmul(psC[:, 7 * 512:7 * 512 + 4], ones_r[:], rsc[0:1, 56:60],
                                                   start=True, stop=True),
                           reads=[("rsc",), ("onesr",)], writes=[("psC", 7)])
                        op("v", lambda E: E.tensor_copy(rt[:, 0:4], psC[:, 7 * 512:7 * 512 + 4]),
                           reads=[("psC", 7)], writes=[("rt",)])
                        for sx in range(2):
                            op("v", lambda E, sx=sx: E.tensor_scalar_mul(rt[:, 4:5], rt[:, sx:sx + 1], 1024.0),
                               reads=[("rt",)], writes=[("rt",)])
                            op("v", lambda E: E.tensor_single_scalar(idxf[:, 0:8], iotas[:, 0:8], rt[:, 4:5], ALU.add),
                               reads=[("rt",), ("iotas",)], writes=[("idxf",)])
                            op("v", lambda E, sx=sx: E.tensor_copy(idxW1[:, sx, :], idxf[:, 0:8]),
                               reads=[("idxf",)], writes=[("idxW1",)])
                            op("v", lambda E, sx=sx: E.tensor_scalar_mul(rt[:, 5:6], rt[:, sx:sx + 1], 4096.0),
                               reads=[("rt",)], writes=[("rt",)])
                            op("v", lambda E: E.tensor_single_scalar(idxf[:, 0:32], iotas[:, 0:32], rt[:, 5:6], ALU.add),
                               reads=[("rt",), ("iotas",)], writes=[("idxf",)])
                            op("v", lambda E, sx=sx: E.tensor_copy(idxW2[:, sx, :], idxf[:, 0:32]),
                               reads=[("idxf",)], writes=[("idxW2",)])
                            op("v", lambda E, sx=sx: E.tensor_scalar_mul(rt[:, 6:7], rt[:, sx:sx + 1], 32.0),
                               reads=[("rt",)], writes=[("rt",)])
                            op("v", lambda E: E.tensor_single_scalar(idxf[0:32, 0:1], iotas[0:32, 0:1], rt[0:32, 6:7], ALU.add),
                               reads=[("rt",), ("iotas",)], writes=[("idxf",)])
                            op("v", lambda E, sx=sx: E.tensor_copy(idxb1[:, sx:sx + 1], idxf[0:32, 0:1]),
                               reads=[("idxf",)], writes=[("idxb1",)])
                            op("v", lambda E: E.tensor_scalar_mul(idxf[0:8, 1:2], iotas[0:8, 0:1], 1.0e6),
                               reads=[("iotas",)], writes=[("idxf",)])
                            op("v", lambda E, sx=sx: E.tensor_single_scalar(
                                idxf[0:8, 1:2], idxf[0:8, 1:2], rt[0:8, sx:sx + 1], ALU.add),
                               reads=[("rt",)], writes=[("idxf",)])
                            op("v", lambda E, sx=sx: E.tensor_copy(idxb2[:, sx:sx + 1], idxf[0:8, 1:2]),
                               reads=[("idxf",)], writes=[("idxb2",)])
                        for sx in range(2):
                            op("g", lambda E, sx=sx: E.indirect_dma_start(
                                b2g[:, :], None, d_b2.ap(), IndirectOffsetOnAxis(idxb2[:, sx:sx + 1], 0),
                                bounds_check=NE - 1, oob_is_err=False),
                               reads=[("idxb2",)], writes=[("b2g",)], inc=16, sem=RSEM)
                            wcol = 58 + sx
                            op("v", lambda E, sx=sx, wcol=wcol: E.tensor_single_scalar(
                                b2r[0:1, sx, :], b2g[0:1, :], rsc[0:1, wcol:wcol + 1], ALU.mult),
                               reads=[("b2g",), ("rsc",)], writes=[("b2r",)])
                        op("v", lambda E: E.tensor_add(b2r[0:1, 0, :], b2r[0:1, 0, :], b2r[0:1, 1, :]),
                           reads=[("b2r",)], writes=[("b2r",)])
                        for dh in range(2):
                            op("t", lambda E, dh=dh: E.matmul(
                                psC[:, (6 + dh) * 512:(6 + dh) * 512 + 512], ones_r[:],
                                b2r[0:1, 0, dh * 512:(dh + 1) * 512], start=True, stop=True),
                               reads=[("b2r",), ("onesr",)], writes=[("psC", 6 + dh)])
                            op("v", lambda E, dh=dh: E.tensor_copy(
                                b2c[:, dh * 512:(dh + 1) * 512], psC[:, (6 + dh) * 512:(6 + dh) * 512 + 512]),
                               reads=[("psC", 6 + dh)], writes=[("b2c",)])

            if LV >= 5:
                for ee in ENGS:
                    bld.barrier_into(ee)
                with ExitStack() as ctxD:
                    eD = ctxD.enter_context
                    W1sb = eD(nc.sbuf_tensor("sb_W1sb", [128, 8, FF], BF16))
                    hT = eD(nc.sbuf_tensor("sb_hT", [128, 32, 512], BF16))
                    W2sb = eD(nc.sbuf_tensor("sb_W2sb", [128, 2, D], BF16))
                    b1g = eD(nc.sbuf_tensor("sb_b1g", [32, 128], F32))
                    b1T = eD(nc.sbuf_tensor("sb_b1T", [128, 32], F32))
                    psF = eD(nc.psum_tensor("pm_psF", [128, 4096], F32))

                    for sx in range(2):
                        gatecol = rt[:, 2 + sx:3 + sx]
                        for dc in range(8):
                            op("g", lambda E, sx=sx, dc=dc: E.indirect_dma_start(
                                W1sb[:, dc, :], None, W1v, IndirectOffsetOnAxis(idxW1[:, sx, dc:dc + 1], 0)),
                               reads=[("idxW1",)], writes=[("W1sb",)], inc=16, sem=FSEM)
                        op("g", lambda E, sx=sx: E.indirect_dma_start(
                            b1g[:, :], None, b1v, IndirectOffsetOnAxis(idxb1[:, sx:sx + 1], 0)),
                           reads=[("idxb1",)], writes=[("b1g",)], inc=16, sem=RSEM)
                        op("t", lambda E: E.transpose(psF[:, 7 * 512:7 * 512 + 32], b1g[:, :], id128[0:32, 0:32]),
                           reads=[("b1g",), ("id",)], writes=[("psF", 7)])
                        op("v", lambda E: E.tensor_copy(b1T[:], psF[:, 7 * 512:7 * 512 + 32]),
                           reads=[("psF", 7)], writes=[("b1T",)])

                        for v4 in range(4):
                            for ft in range(32):
                                fb = ft % 2
                                for dc in range(8):
                                    op("t", lambda E, ft=ft, dc=dc, v4=v4, fb=fb: E.matmul(
                                        psF[:, fb * 512:(fb + 1) * 512], W1sb[:, dc, ft * 128:(ft + 1) * 128],
                                        fT[:, dc, v4 * 512:(v4 + 1) * 512], start=(dc == 0), stop=(dc == 7)),
                                       reads=[("W1sb",)] + [("fT", v4 * 4 + q) for q in range(4)],
                                       writes=[("psF", fb)])
                                op("c", lambda E, ft=ft, fb=fb: E.activation(
                                    hT[:, ft, :], psF[:, fb * 512:(fb + 1) * 512], AF.Gelu,
                                    bias=b1T[:, ft:ft + 1]),
                                   reads=[("psF", fb), ("b1T",)], writes=[("hT", ft)])
                            for fc in range(32):
                                par2 = fc % 2
                                op("g", lambda E, sx=sx, fc=fc, par2=par2: E.indirect_dma_start(
                                    W2sb[:, par2, :], None, W2v, IndirectOffsetOnAxis(idxW2[:, sx, fc:fc + 1], 0)),
                                   reads=[("idxW2",)], writes=[("W2sb", par2)], inc=16, sem=W2SEM[par2])
                                for tt in range(4):
                                    for dh in range(2):
                                        bk = tt * 2 + dh
                                        op("t", lambda E, fc=fc, tt=tt, dh=dh, par2=par2, bk=bk: E.matmul(
                                            psF[:, bk * 512:(bk + 1) * 512],
                                            hT[:, fc, tt * 128:(tt + 1) * 128],
                                            W2sb[:, par2, dh * 512:(dh + 1) * 512],
                                            start=(fc == 0), stop=(fc == 31)),
                                           reads=[("hT", fc), ("W2sb", par2)], writes=[("psF", bk)])
                            for tt in range(4):
                                for dh in range(2):
                                    bk = tt * 2 + dh
                                    jj = v4 * 4 + tt
                                    op("v", lambda E, bk=bk, jj=jj, dh=dh, gatecol=gatecol: E.scalar_tensor_tensor(
                                        r1[:, jj, dh * 512:(dh + 1) * 512], psF[:, bk * 512:(bk + 1) * 512],
                                        gatecol, r1[:, jj, dh * 512:(dh + 1) * 512], ALU.mult, ALU.add),
                                       reads=[("psF", bk), ("rt",), ("r1", jj)], writes=[("r1", jj)])

                    for j in range(NSLOT):
                        op("v", lambda E, j=j: E.tensor_add(r1[:, j, :], r1[:, j, :], b2c[:]),
                           reads=[("b2c",), ("r1", j)], writes=[("r1", j)])
                        op("s", lambda E, j=j: E.dma_start(out=d_out[j * 128:(j + 1) * 128, :], in_=r1[:, j, :]),
                           reads=[("r1", j)], inc=16, sem=OUTSEM)
                    raw("s", OUTSEM, 16 * NSLOT)

            if LV < 5:
                for ee in ENGS:
                    bld.barrier_into(ee)
                ns = dbg_ns
                for j in range(NSLOT):
                    op("s", lambda E, j=j: E.dma_start(out=dbg["r1"][j * 128:(j + 1) * 128, :], in_=r1[:, j, :]),
                       inc=16, sem=OUTSEM)
                    ns += 16
                if LV >= 2:
                    op("g", lambda E: E.dma_start(out=dbg["ft"][:, :, :], in_=fT[:]), inc=16, sem=OUTSEM); ns += 16
                if LV >= 4:
                    op("s", lambda E: E.dma_start(out=dbg["rt"][:, 0:4], in_=rt[:, 0:4]), inc=16, sem=OUTSEM); ns += 16
                    op("s", lambda E: E.dma_start(out=dbg["rt"][:, 4:5], in_=poolT[:, 0:1], allow_slow_non_contiguous=True), inc=16, sem=OUTSEM); ns += 16
                raw("s", OUTSEM, ns)

            with nc.Block() as block:
                @block.tensor
                def _(E):
                    for f in bld.progs["t"]:
                        f(E)

                @block.vector
                def _(E):
                    for f in bld.progs["v"]:
                        f(E)

                @block.scalar
                def _(E):
                    for f in bld.progs["c"]:
                        f(E)

                @block.sync
                def _(E):
                    for f in bld.progs["s"]:
                        f(E)

                @block.gpsimd
                def _(E):
                    for f in bld.progs["g"]:
                        f(E)

    return nc


_NC_CACHE = {}


def _get_nc(upto="full"):
    if upto not in _NC_CACHE:
        _NC_CACHE[upto] = _build(upto)
    return _NC_CACHE[upto]


def _host_prep(inputs):
    x = np.ascontiguousarray(np.asarray(inputs["L0_attn_x"], dtype=np.float32))
    wq = np.ascontiguousarray(np.asarray(inputs["L0_attn_w_q"], dtype=np.float32))
    wk = np.ascontiguousarray(np.asarray(inputs["L0_attn_w_k"], dtype=np.float32))
    wv = np.ascontiguousarray(np.asarray(inputs["L0_attn_w_v"], dtype=np.float32))
    wo = np.ascontiguousarray(np.asarray(inputs["L0_attn_w_o"], dtype=np.float32))
    nw = np.asarray(inputs["L0_attn_norm_w"], dtype=np.float32)
    Wr = np.ascontiguousarray(np.asarray(inputs["L0_ffn_Wr"], dtype=np.float32))
    W1 = np.ascontiguousarray(np.asarray(inputs["L0_ffn_W1"], dtype=np.float32))
    b1 = np.ascontiguousarray(np.asarray(inputs["L0_ffn_b1"], dtype=np.float32))
    W2 = np.ascontiguousarray(np.asarray(inputs["L0_ffn_W2"], dtype=np.float32))
    b2 = np.ascontiguousarray(np.asarray(inputs["L0_ffn_b2"], dtype=np.float32))

    i = np.arange(DH)
    theta = np.power(10000.0, -2.0 * (i // 2).astype(np.float64) / DH)
    pos = np.arange(N, dtype=np.float64)
    ang = pos[:, None] * theta[None, :]
    sign = np.where(np.arange(N) % 2 == 0, -1.0, 1.0)
    rf = (np.cos(ang) + np.sin(ang) * sign[:, None]).astype(np.float32)
    rfT_base = np.concatenate([rf.T, rf.T], axis=0)

    nwT = np.ascontiguousarray(nw.reshape(8, 128).T)
    wr_r = np.ascontiguousarray(Wr.reshape(128, 64))

    in_maps = []
    for core in range(8):
        b, par = core // 2, core % 2
        xl = x[b]
        rfT = rfT_base
        if par:
            perm = np.arange(NT).reshape(NT // 2, 2)[:, ::-1].reshape(NT)
            xl = xl.reshape(NT, 128, D)[perm].reshape(N, D)
            rfT = rfT_base.reshape(128, NT, 128)[:, perm].reshape(128, N)
        zf = np.zeros((128, 128), np.float32) if par == 0 else np.ones((128, 128), np.float32)
        in_maps.append({
            "x": np.ascontiguousarray(xl), "wq": wq, "wk": wk, "wv": wv, "wo": wo,
            "nwT": nwT, "rfT": np.ascontiguousarray(rfT), "zf": zf, "wr": wr_r,
            "W1": W1, "b1": b1, "W2": W2, "b2": b2,
        })
    return in_maps


def _assemble(results, key="out"):
    out = np.empty((B, N, D), dtype=np.float32)
    for core in range(8):
        b, par = core // 2, core % 2
        ol = results[core][key]
        for j in range(NSLOT):
            at = 2 * j + par
            out[b, at * 128:(at + 1) * 128, :] = ol[j * 128:(j + 1) * 128, :]
    return out


def kernel(**inputs):
    nc = _get_nc("full")
    in_maps = _host_prep(inputs)
    res = run_bass_kernel_spmd(nc, in_maps, core_ids=list(range(8)))
    return _assemble(res.results)


# version 02
# speedup vs baseline: 1.1512x; 1.1512x over previous
import numpy as np
from contextlib import ExitStack

import concourse.bass as bass
import concourse.mybir as mybir
from concourse.bass import IndirectOffsetOnAxis
from concourse.bass_utils import run_bass_kernel_spmd

F32 = mybir.dt.float32
BF16 = mybir.dt.bfloat16
I32 = mybir.dt.int32
AF = mybir.ActivationFunctionType
ALU = mybir.AluOpType
AXX = mybir.AxisListType.X

B, N, D = 4, 4096, 1024
H, KVH, DH = 16, 4, 64
NT = N // 128
NSLOT = 16
FF, NE = 4096, 8
EPS1, EPS2 = 1e-5, 1.1920929e-07

ENGS = ("t", "v", "c", "s", "g")


class Bld:

    def __init__(self, nc):
        self.nc = nc
        self.progs = {e: [] for e in ENGS}
        self.sem = {e: nc.alloc_semaphore(f"sem_{e}") for e in ENGS}
        self.cnt = {e: 0 for e in ENGS}
        self.ext = {}
        self.lastw = {}
        self.readers = {}
        self.satisfied = {e: {} for e in ENGS}
        self.marks = [("init", {e: 0 for e in ENGS})]

    def _wait_tok(self, eng, tok, sem, val):
        if val <= 0:
            return
        if self.satisfied[eng].get(tok, 0) >= val:
            return
        self.progs[eng].append(lambda E, s=sem, c=val: E.wait_ge(s, c))
        self.satisfied[eng][tok] = val

    def _wait_prod(self, eng, prod):
        kind = prod[0]
        if kind == "eng":
            _, peng, pcnt = prod
            if peng == eng:
                if eng == "t":
                    return
                self._wait_tok(eng, ("e", eng), self.sem[eng], pcnt)
                return
            self._wait_tok(eng, ("e", peng), self.sem[peng], pcnt)
        else:
            _, semid, pcnt = prod
            sem = self.ext[semid][0]
            self._wait_tok(eng, ("x", semid), sem, pcnt)

    def op(self, eng, fn, reads=(), writes=(), inc=1, sem=None):
        for k in reads:
            lw = self.lastw.get(k)
            if lw:
                self._wait_prod(eng, lw)
        for k in writes:
            lw = self.lastw.get(k)
            if lw:
                self._wait_prod(eng, lw)
            for pe, pc in self.readers.get(k, {}).items():
                self._wait_prod(eng, ("eng", pe, pc))
        if sem is None:
            self.cnt[eng] += inc
            ss = self.sem[eng]
            prod = ("eng", eng, self.cnt[eng])
        else:
            rec = self.ext.setdefault(id(sem), [sem, 0])
            rec[1] += inc
            ss = sem
            prod = ("ext", id(sem), rec[1])
        self.progs[eng].append(lambda E, f=fn, s2=ss, i=inc: f(E).then_inc(s2, i))
        mycnt = self.cnt[eng]
        for k in reads:
            self.readers.setdefault(k, {})[eng] = mycnt
        for k in writes:
            self.lastw[k] = prod
            self.readers[k] = {}
        return prod

    def raw(self, eng, semh, val):
        self.progs[eng].append(lambda E, s=semh, c=val: E.wait_ge(s, c))

    def mark(self, name):
        self.marks.append((name, {e: len(self.progs[e]) for e in ENGS}))

    def barrier_into(self, eng):
        for pe in ENGS:
            if pe != eng:
                self._wait_tok(eng, ("e", pe), self.sem[pe], self.cnt[pe])
        for semid, (sem, val) in self.ext.items():
            self._wait_tok(eng, ("x", semid), sem, val)


def _build(upto="full"):
    LV = {"phase1": 1, "attn": 2, "oproj": 3, "route": 4, "full": 5}[upto]
    nc = bass.Bass("TRN2", target_bir_lowering=False)

    d_x = nc.dram_tensor("x", [N, D], F32, kind="ExternalInput")
    d_wq = nc.dram_tensor("wq", [D, D], F32, kind="ExternalInput")
    d_wk = nc.dram_tensor("wk", [D, 256], F32, kind="ExternalInput")
    d_wv = nc.dram_tensor("wv", [D, 256], F32, kind="ExternalInput")
    d_wo = nc.dram_tensor("wo", [D, D], F32, kind="ExternalInput")
    d_nwT = nc.dram_tensor("nwT", [128, 8], F32, kind="ExternalInput")
    d_rfT = nc.dram_tensor("rfT", [128, N], F32, kind="ExternalInput")
    d_zf = nc.dram_tensor("zf", [128, 128], F32, kind="ExternalInput")
    d_wr = nc.dram_tensor("wr", [128, 64], F32, kind="ExternalInput")
    d_W1 = nc.dram_tensor("W1", [NE, D, FF], F32, kind="ExternalInput")
    d_b1 = nc.dram_tensor("b1", [NE, FF], F32, kind="ExternalInput")
    d_W2 = nc.dram_tensor("W2", [NE, FF, D], F32, kind="ExternalInput")
    d_b2 = nc.dram_tensor("b2", [NE, D], F32, kind="ExternalInput")
    d_out = nc.dram_tensor("out", [NSLOT * 128, D], F32, kind="ExternalOutput")

    tri_np = np.tril(np.ones((128, 128), np.float32)).T
    d_tri = nc.inline_tensor(tri_np, "triconst")
    iotas_np = (np.arange(32)[None, :] * 128 + np.arange(128)[:, None]).astype(np.float32)
    d_iotas = nc.inline_tensor(iotas_np, "iotasconst")
    d_io8 = nc.inline_tensor(np.arange(8, dtype=np.float32).reshape(1, 8), "io8const")
    d_id = nc.inline_tensor(np.eye(128, dtype=np.float32), "idconst")

    cc_in = nc.dram_tensor("cc_in", [1, D], F32)
    cc_out = nc.dram_tensor("cc_out", [1, D], F32)

    dbg = {}
    if LV < 5:
        dbg["r1"] = nc.dram_tensor("dbg_r1", [NSLOT * 128, D], F32, kind="ExternalOutput")
        dbg["qt"] = nc.dram_tensor("dbg_qt", [128, 8, 2048], F32, kind="ExternalOutput")
        dbg["kt"] = nc.dram_tensor("dbg_kt", [128, 2, N], F32, kind="ExternalOutput")
        dbg["va"] = nc.dram_tensor("dbg_va", [128, 32, 260], F32, kind="ExternalOutput")
        dbg["ft"] = nc.dram_tensor("dbg_ft", [128, 8, 2048], F32, kind="ExternalOutput")
        dbg["rt"] = nc.dram_tensor("dbg_rt", [128, 8], F32, kind="ExternalOutput")

    W1v = d_W1.ap().rearrange("e d f -> (e d) f")
    W2v = d_W2.ap().rearrange("e f d -> (e f) d")
    b1v = d_b1.ap().rearrange("e (r p) -> (e r) p", p=128)

    with ExitStack() as ctx:
        e = ctx.enter_context
        r1 = e(nc.sbuf_tensor("sb_r1", [128, NSLOT, D], F32))
        id128 = e(nc.sbuf_tensor("sb_id128", [128, 128], F32))
        tri_b = e(nc.sbuf_tensor("sb_tri_b", [128, 128], BF16))
        zf_b = e(nc.sbuf_tensor("sb_zf_b", [128, 128], BF16))
        iotas = e(nc.sbuf_tensor("sb_iotas", [128, 32], F32))
        io8 = e(nc.sbuf_tensor("sb_io8", [1, 8], F32))
        nwT = e(nc.sbuf_tensor("sb_nwT", [128, 8], F32))
        onesf = e(nc.sbuf_tensor("sb_onesf", [128, 1], F32))
        onesb = e(nc.sbuf_tensor("sb_onesb", [128, 1], BF16))
        ones_r = e(nc.sbuf_tensor("sb_ones_r", [1, 128], F32))
        negM0 = e(nc.sbuf_tensor("sb_negM0", [128, 1], F32))
        rt = e(nc.sbuf_tensor("sb_rt", [128, 8], F32))
        wr_s = e(nc.sbuf_tensor("sb_wr_s", [128, 64], F32))
        b2c = e(nc.sbuf_tensor("sb_b2c", [128, D], F32))
        poolT = e(nc.sbuf_tensor("sb_poolT", [128, 8], F32))
        idxW1 = e(nc.sbuf_tensor("sb_idxW1", [128, 2, 8], I32))
        idxW2 = e(nc.sbuf_tensor("sb_idxW2", [128, 2, 32], I32))
        idxb1 = e(nc.sbuf_tensor("sb_idxb1", [32, 2], I32))
        idxb2 = e(nc.sbuf_tensor("sb_idxb2", [8, 2], I32))
        idxf = e(nc.sbuf_tensor("sb_idxf", [128, 32], F32))
        fT = e(nc.sbuf_tensor("sb_fT", [128, 8, 2048], BF16))

        bld = Bld(nc)
        op, raw = bld.op, bld.raw
        OUTSEM = nc.alloc_semaphore("outsem")
        CCSEM = nc.alloc_semaphore("ccsem")
        XSEM = [nc.alloc_semaphore("xsa"), nc.alloc_semaphore("xsb")]
        RSEM = nc.alloc_semaphore("rsem")
        FSEM = nc.alloc_semaphore("fsem")
        W2SEM = [nc.alloc_semaphore("w2a"), nc.alloc_semaphore("w2b")]
        RFSEM = [nc.alloc_semaphore("rfa"), nc.alloc_semaphore("rfb")]

        with ExitStack() as ctxB:
            eB = ctxB.enter_context
            QT = eB(nc.sbuf_tensor("sb_QT", [128, 8, 2048], BF16))
            KT = eB(nc.sbuf_tensor("sb_KT", [128, 2, N], BF16))
            VA = eB(nc.sbuf_tensor("sb_VA", [128, 32, 260], BF16))

            op("s", lambda E: E.dma_start(out=id128[:], in_=d_id[:, :]), writes=[("id",)], inc=16)
            op("s", lambda E: E.dma_start(out=iotas[:], in_=d_iotas[:, :]), writes=[("iotas",)], inc=16)
            op("s", lambda E: E.dma_start(out=io8[:], in_=d_io8[:, :]), writes=[("io8",)], inc=16)
            op("s", lambda E: E.dma_start(out=nwT[:], in_=d_nwT[:, :]), writes=[("nwT",)], inc=16)
            op("s", lambda E: E.dma_start(out=wr_s[:], in_=d_wr[:, :]), writes=[("wr",)], inc=16)
            op("g", lambda E: E.dma_start(out=tri_b[:], in_=d_tri[:, :]), writes=[("tri",)], inc=16)
            op("g", lambda E: E.dma_start(out=zf_b[:], in_=d_zf[:, :]), writes=[("zfb",)], inc=16)
            op("v", lambda E: E.memset(onesf[:], 1.0), writes=[("onesf",)])
            op("v", lambda E: E.memset(onesb[:], 1.0), writes=[("onesb",)])
            op("v", lambda E: E.memset(ones_r[:], 1.0), writes=[("onesr",)])
            op("v", lambda E: E.memset(VA[:, :, 64:260:65], 1.0), writes=[("vaones",)])

            with ExitStack() as ctxB1:
                eB1 = ctxB1.enter_context
                xw = eB1(nc.sbuf_tensor("sb_xw", [128, 2, D], F32))
                rfs = eB1(nc.sbuf_tensor("sb_rfs", [128, 2, 256], F32))
                xnT = eB1(nc.sbuf_tensor("sb_xnT", [128, 8, 256], BF16))
                wqb = eB1(nc.sbuf_tensor("sb_wqb", [128, 8, D], BF16))
                wkb = eB1(nc.sbuf_tensor("sb_wkb", [128, 8, 256], BF16))
                wvb = eB1(nc.sbuf_tensor("sb_wvb", [128, 8, 256], BF16))
                sc4k = eB1(nc.sbuf_tensor("sb_sc4k", [128, 512], F32))
                stats = eB1(nc.sbuf_tensor("sb_stats", [128, 2, 12], F32))
                m0sc = eB1(nc.sbuf_tensor("sb_m0sc", [1, 8], F32))
                ps1 = eB1(nc.psum_tensor("pm_ps1", [128, 4096], F32))

                for dc in range(8):
                    op("g", lambda E, dc=dc: E.dma_start(out=wqb[:, dc, :], in_=d_wq[dc * 128:(dc + 1) * 128, :]),
                       writes=[("wqb",)], inc=16)
                    op("g", lambda E, dc=dc: E.dma_start(out=wkb[:, dc, :], in_=d_wk[dc * 128:(dc + 1) * 128, :]),
                       writes=[("wkb",)], inc=16)
                    op("g", lambda E, dc=dc: E.dma_start(out=wvb[:, dc, :], in_=d_wv[dc * 128:(dc + 1) * 128, :]),
                       writes=[("wvb",)], inc=16)

                bld.mark("phase1")
                for hw in range(NT // 2):
                    t0 = 2 * hw
                    hb = hw % 2
                    op("s", lambda E, hb=hb, t0=t0: E.dma_start(
                        out=rfs[:, hb, :], in_=d_rfT[:, t0 * 128:t0 * 128 + 256]),
                       writes=[("rfs", hb)], inc=16, sem=RFSEM[hb])
                    for i in range(2):
                        tl = t0 + i
                        xin = d_x[tl * 128:(tl + 1) * 128, :]
                        op("s", lambda E, i=i, xin=xin: E.dma_start(out=xw[:, i, :], in_=xin),
                           writes=[("xw", i)], inc=16, sem=XSEM[i])
                    for i in range(2):
                        S = stats[:, i, :]
                        op("c", lambda E, i=i: E.activation(sc4k[:], xw[:, i, 0:512], AF.Square),
                           reads=[("xw", i)], writes=[("sc4k",)])
                        op("v", lambda E, S=S: E.reduce_sum(S[:, 1:2], sc4k[:], axis=AXX),
                           reads=[("sc4k",)], writes=[("st", i)])
                        op("c", lambda E, i=i: E.activation(sc4k[:], xw[:, i, 512:1024], AF.Square),
                           reads=[("xw", i)], writes=[("sc4k",)])
                        op("v", lambda E, S=S: E.reduce_sum(S[:, 4:5], sc4k[:], axis=AXX),
                           reads=[("sc4k",)], writes=[("st", i)])
                        op("v", lambda E, S=S: E.tensor_add(S[:, 1:2], S[:, 1:2], S[:, 4:5]),
                           reads=[("st", i)], writes=[("st", i)])
                        op("v", lambda E, S=S, i=i: E.reduce_sum(S[:, 0:1], xw[:, i, :], axis=AXX),
                           reads=[("xw", i)], writes=[("st", i)])
                        op("v", lambda E, S=S: E.tensor_scalar_mul(S[:, 2:3], S[:, 0:1], 1.0 / D),
                           reads=[("st", i)], writes=[("st", i)])
                        op("v", lambda E, S=S: E.tensor_scalar_mul(S[:, 3:4], S[:, 1:2], 1.0 / D),
                           reads=[("st", i)], writes=[("st", i)])
                        op("v", lambda E, S=S: E.tensor_mul(S[:, 4:5], S[:, 2:3], S[:, 2:3]),
                           reads=[("st", i)], writes=[("st", i)])
                        op("v", lambda E, S=S: E.tensor_sub(S[:, 3:4], S[:, 3:4], S[:, 4:5]),
                           reads=[("st", i)], writes=[("st", i)])
                        op("v", lambda E, S=S: E.tensor_scalar_add(S[:, 5:6], S[:, 3:4], EPS1),
                           reads=[("st", i)], writes=[("st", i)])
                        op("c", lambda E, S=S: E.activation(S[:, 6:7], S[:, 5:6], AF.Ln),
                           reads=[("st", i)], writes=[("st", i)])
                        op("c", lambda E, S=S: E.activation(S[:, 6:7], S[:, 6:7], AF.Exp, scale=-0.5),
                           reads=[("st", i)], writes=[("st", i)])
                        op("v", lambda E, S=S: E.reciprocal(S[:, 7:8], S[:, 5:6]),
                           reads=[("st", i)], writes=[("st", i)])
                        op("v", lambda E, S=S: E.tensor_mul(S[:, 7:8], S[:, 7:8], S[:, 3:4]),
                           reads=[("st", i)], writes=[("st", i)])
                        op("v", lambda E, S=S: E.tensor_scalar_add(S[:, 8:9], S[:, 7:8], EPS2),
                           reads=[("st", i)], writes=[("st", i)])
                        op("c", lambda E, S=S: E.activation(S[:, 8:9], S[:, 8:9], AF.Ln),
                           reads=[("st", i)], writes=[("st", i)])
                        op("c", lambda E, S=S: E.activation(S[:, 8:9], S[:, 8:9], AF.Exp, scale=-0.5),
                           reads=[("st", i)], writes=[("st", i)])
                        op("v", lambda E, S=S: E.tensor_mul(S[:, 9:10], S[:, 6:7], S[:, 8:9]),
                           reads=[("st", i)], writes=[("st", i)])
                        op("v", lambda E, S=S: E.tensor_scalar_add(S[:, 10:11], S[:, 6:7], 1.0),
                           reads=[("st", i)], writes=[("st", i)])
                        op("v", lambda E, S=S: E.tensor_mul(S[:, 11:12], S[:, 2:3], S[:, 6:7]),
                           reads=[("st", i)], writes=[("st", i)])
                        if i == 0:
                            op("v", lambda E, S=S, i=i, hw=hw: E.tensor_scalar(
                                r1[:, hw, :], xw[:, i, :], S[:, 10:11], S[:, 11:12],
                                ALU.mult, ALU.subtract),
                               reads=[("xw", i), ("st", i)], writes=[("r1", hw)])
                        op("v", lambda E, S=S, i=i: E.tensor_scalar(
                            xw[:, i, :], xw[:, i, :], S[:, 2:3], S[:, 9:10],
                            ALU.subtract, ALU.mult),
                           reads=[("st", i)], writes=[("xw", i)])
                        for c8 in range(8):
                            tb = c8 % 2
                            op("t", lambda E, c8=c8, tb=tb, i=i: E.transpose(
                                ps1[:, tb * 512:tb * 512 + 128], xw[:, i, c8 * 128:(c8 + 1) * 128], id128[:]),
                               reads=[("xw", i), ("id",)], writes=[("ps1", tb)])
                            op("c", lambda E, c8=c8, tb=tb, i=i: E.mul(
                                xnT[:, c8, i * 128:(i + 1) * 128], ps1[:, tb * 512:tb * 512 + 128],
                                nwT[:, c8:c8 + 1]),
                               reads=[("ps1", tb), ("nwT",)], writes=[("xnT", c8)])
                    for qc in range(8):
                        qb = 2 + qc % 2
                        for dc in range(8):
                            op("t", lambda E, qc=qc, dc=dc, qb=qb: E.matmul(
                                ps1[:, qb * 512:qb * 512 + 128], wqb[:, dc, qc * 128:(qc + 1) * 128],
                                xnT[:, dc, 0:128], start=(dc == 0), stop=(dc == 7)),
                               reads=[("wqb",), ("xnT", dc)], writes=[("ps1", qb)])
                        op("v", lambda E, qc=qc, qb=qb, hw=hw, hb=hb: E.tensor_mul(
                            QT[:, qc, hw * 128:(hw + 1) * 128], ps1[:, qb * 512:qb * 512 + 128],
                            rfs[:, hb, 0:128]),
                           reads=[("ps1", qb), ("rfs", hb)], writes=[("QT",)])
                    for kc2 in range(2):
                        kb = 4 + kc2
                        for dc in range(8):
                            op("t", lambda E, kc2=kc2, dc=dc, kb=kb: E.matmul(
                                ps1[:, kb * 512:kb * 512 + 256], wkb[:, dc, kc2 * 128:(kc2 + 1) * 128],
                                xnT[:, dc, 0:256], start=(dc == 0), stop=(dc == 7)),
                               reads=[("wkb",), ("xnT", dc)], writes=[("ps1", kb)])
                        op("v", lambda E, kc2=kc2, kb=kb, t0=t0, hb=hb: E.tensor_mul(
                            KT[:, kc2, t0 * 128:t0 * 128 + 256], ps1[:, kb * 512:kb * 512 + 256],
                            rfs[:, hb, 0:256]),
                           reads=[("ps1", kb), ("rfs", hb)], writes=[("KT",)])
                    for i in range(2):
                        vb = 6 + i
                        tl = t0 + i
                        for dc in range(8):
                            op("t", lambda E, i=i, dc=dc, vb=vb: E.matmul(
                                ps1[:, vb * 512:vb * 512 + 256], xnT[:, dc, i * 128:(i + 1) * 128],
                                wvb[:, dc, :], start=(dc == 0), stop=(dc == 7)),
                               reads=[("wvb",), ("xnT", dc)], writes=[("ps1", vb)])
                        op("v", lambda E, vb=vb, tl=tl: E.tensor_copy(
                            VA[:, tl, :].rearrange("p (k s) -> p k s", s=65)[:, :, 0:64],
                            ps1[:, vb * 512:vb * 512 + 256].rearrange("p (k s) -> p k s", s=64)),
                           reads=[("ps1", vb), ("vaones",)], writes=[("VA", tl)])

                bld.mark("m0")
                msc = wqb
                op("v", lambda E: E.memset(m0sc[0:1, 0:1], 0.0), writes=[("m0",)])
                op("v", lambda E: E.memset(m0sc[0:1, 2:3], 0.0), writes=[("m0",)])
                for qc in range(8):
                    for half in range(2):
                        op("v", lambda E, qc=qc, half=half: E.tensor_mul(
                            msc[:, half, :], QT[:, qc, half * 1024:(half + 1) * 1024],
                            QT[:, qc, half * 1024:(half + 1) * 1024]),
                           reads=[("QT",)], writes=[("msc",), ("wqb",)])
                    for pc in range(4):
                        op("t", lambda E, pc=pc: E.matmul(
                            ps1[0:1, 0:512], onesb[:], msc[:, pc // 2, (pc % 2) * 512:(pc % 2) * 512 + 512],
                            start=True, stop=True),
                           reads=[("msc",), ("onesb",)], writes=[("ps1", 0)])
                        op("v", lambda E: E.reduce_max(m0sc[0:1, 1:2], ps1[0:1, 0:512], axis=AXX),
                           reads=[("ps1", 0)], writes=[("m0",)])
                        op("v", lambda E: E.tensor_max(m0sc[0:1, 0:1], m0sc[0:1, 0:1], m0sc[0:1, 1:2]),
                           reads=[("m0",)], writes=[("m0",)])
                for kc2 in range(2):
                    for half in range(4):
                        op("v", lambda E, kc2=kc2, half=half: E.tensor_mul(
                            msc[:, half % 2, :], KT[:, kc2, half * 1024:(half + 1) * 1024],
                            KT[:, kc2, half * 1024:(half + 1) * 1024]),
                           reads=[("KT",)], writes=[("msc",), ("wqb",)])
                        for pc in range(2):
                            op("t", lambda E, half=half, pc=pc: E.matmul(
                                ps1[0:1, 0:512], onesb[:], msc[:, half % 2, pc * 512:(pc + 1) * 512],
                                start=True, stop=True),
                               reads=[("msc",), ("onesb",)], writes=[("ps1", 0)])
                            op("v", lambda E: E.reduce_max(m0sc[0:1, 3:4], ps1[0:1, 0:512], axis=AXX),
                               reads=[("ps1", 0)], writes=[("m0",)])
                            op("v", lambda E: E.tensor_max(m0sc[0:1, 2:3], m0sc[0:1, 2:3], m0sc[0:1, 3:4]),
                               reads=[("m0",)], writes=[("m0",)])
                op("v", lambda E: E.tensor_add(m0sc[0:1, 4:5], m0sc[0:1, 0:1], m0sc[0:1, 2:3]),
                   reads=[("m0",)], writes=[("m0",)])
                op("v", lambda E: E.tensor_scalar_mul(m0sc[0:1, 4:5], m0sc[0:1, 4:5], -0.0625),
                   reads=[("m0",)], writes=[("m0",)])
                op("t", lambda E: E.matmul(ps1[:, 512:513], ones_r[:], m0sc[0:1, 4:5], start=True, stop=True),
                   reads=[("m0",), ("onesr",)], writes=[("ps1", 1)])
                op("v", lambda E: E.tensor_copy(negM0[:], ps1[:, 512:513]),
                   reads=[("ps1", 1)], writes=[("negM0",)])

                for ee in ENGS:
                    bld.barrier_into(ee)

            if LV >= 2:
                bld.mark("attn")
                with ExitStack() as ctxB2:
                    eB2 = ctxB2.enter_context
                    Psl = eB2(nc.sbuf_tensor("sb_Psl", [128, 2, 4, 512], BF16))
                    lrow = eB2(nc.sbuf_tensor("sb_lrow", [1, 1024], F32))
                    linv = eB2(nc.sbuf_tensor("sb_linv", [1, 1024], F32))
                    lrep = eB2(nc.sbuf_tensor("sb_lrep", [128, 512], F32))
                    psA = eB2(nc.psum_tensor("pm_psA", [128, 3072], F32))
                    psO = eB2(nc.psum_tensor("pm_psO", [128, 1024], F32))

                    for g in range(4):
                        ext = 8 * g + 8
                        for c in range(8):
                            kch = c % 2
                            kv_e = (2 * c) % 4
                            batches, k0 = [], 0
                            while k0 < ext:
                                sz = 2 if (len(batches) % 2 == 0) else 1
                                sz = min(sz, ext - k0)
                                batches.append(list(range(k0, k0 + sz)))
                                k0 += sz
                            for kcs in batches:
                                slot = 0 if len(kcs) == 2 else 1
                                bank0 = 0 if slot == 0 else 4
                                for ki, kc in enumerate(kcs):
                                    for h2 in range(2):
                                        bk = bank0 + ki * 2 + h2
                                        op("t", lambda E, kch=kch, kc=kc, h2=h2, c=c, g=g, bk=bk: E.matmul(
                                            psA[:, bk * 512:(bk + 1) * 512],
                                            KT[h2 * 64:(h2 + 1) * 64, kch, kc * 128:(kc + 1) * 128],
                                            QT[h2 * 64:(h2 + 1) * 64, c, g * 512:(g + 1) * 512],
                                            start=True, stop=True, tile_position=(h2 * 64, 0)),
                                           reads=[("KT",), ("QT",)], writes=[("psA", bk)])
                                nb = len(kcs) * 2
                                op("c", lambda E, bank0=bank0, nb=nb, slot=slot: E.activation(
                                    Psl[:, slot, 0:nb, :], psA[:, bank0 * 512:(bank0 + nb) * 512],
                                    AF.Exp, bias=negM0[:, 0:1], scale=0.125),
                                   reads=[("negM0",)] + [("psA", bank0 + j) for j in range(nb)],
                                   writes=[("P", slot)])
                                for ki, kc in enumerate(kcs):
                                    if kc < 8 * g:
                                        continue
                                    n = kc - 8 * g
                                    m = n // 2
                                    for h2 in range(2):
                                        ti = ki * 2 + h2
                                        if m >= 1:
                                            op("v", lambda E, slot=slot, ti=ti, m=m: E.memset(
                                                Psl[:, slot, ti, 0:m * 128], 0.0),
                                               writes=[("P", slot)])
                                        if n % 2 == 0:
                                            msk, mk = tri_b, ("tri",)
                                        else:
                                            msk, mk = zf_b, ("zfb",)
                                        op("v", lambda E, slot=slot, ti=ti, m=m, msk=msk: E.tensor_tensor(
                                            Psl[:, slot, ti, m * 128:(m + 1) * 128],
                                            Psl[:, slot, ti, m * 128:(m + 1) * 128],
                                            msk[:, :], ALU.mult),
                                           reads=[mk], writes=[("P", slot)])
                                for ki, kc in enumerate(kcs):
                                    for h2 in range(2):
                                        kv = kv_e + h2
                                        op("t", lambda E, kc=kc, h2=h2, kv=kv, slot=slot, ki=ki, ext=ext: E.matmul(
                                            psO[0:65, h2 * 512:(h2 + 1) * 512],
                                            VA[:, kc, kv * 65:kv * 65 + 65],
                                            Psl[:, slot, ki * 2 + h2, :],
                                            start=(kc == 0), stop=(kc == ext - 1)),
                                           reads=[("P", slot), ("VA", kc)], writes=[("psO", h2)])
                            op("v", lambda E: E.tensor_copy(lrow[0:1, :], psO[64:65, :]),
                               reads=[("psO", 0), ("psO", 1)], writes=[("lrow",)])
                            op("v", lambda E: E.reciprocal(linv[0:1, :], lrow[0:1, :]),
                               reads=[("lrow",)], writes=[("linv",)])
                            for h2 in range(2):
                                op("t", lambda E, h2=h2: E.matmul(
                                    psA[:, 0:512], ones_r[:], linv[0:1, h2 * 512:(h2 + 1) * 512],
                                    start=True, stop=True),
                                   reads=[("linv",), ("onesr",)], writes=[("psA", 0)])
                                op("v", lambda E: E.tensor_copy(lrep[:], psA[:, 0:512]),
                                   reads=[("psA", 0)], writes=[("lrep",)])
                                op("v", lambda E, h2=h2, c=c, g=g: E.tensor_mul(
                                    fT[h2 * 64:h2 * 64 + 64, c, g * 512:(g + 1) * 512],
                                    psO[0:64, h2 * 512:(h2 + 1) * 512], lrep[0:64, :]),
                                   reads=[("psO", h2), ("lrep",)],
                                   writes=[("fT", 4 * g + mm) for mm in range(4)])

            if LV < 5:
                for ee in ENGS:
                    bld.barrier_into(ee)
                ns = 0
                op("g", lambda E: E.dma_start(out=dbg["qt"][:, :, :], in_=QT[:]), inc=16, sem=OUTSEM); ns += 16
                op("g", lambda E: E.dma_start(out=dbg["kt"][:, :, :], in_=KT[:]), inc=16, sem=OUTSEM); ns += 16
                op("g", lambda E: E.dma_start(out=dbg["va"][:, :, :], in_=VA[:]), inc=16, sem=OUTSEM); ns += 16
                dbg_ns = ns

            for ee in ENGS:
                bld.barrier_into(ee)
            ctxB.close()

            if LV >= 3:
                bld.mark("oproj")
                with ExitStack() as ctxC:
                    eC = ctxC.enter_context
                    wob = eC(nc.sbuf_tensor("sb_wob", [128, 8, D], BF16))
                    ln2sc = eC(nc.sbuf_tensor("sb_ln2sc", [128, D], F32))
                    st2 = eC(nc.sbuf_tensor("sb_st2", [128, 8], F32))
                    rsc = eC(nc.sbuf_tensor("sb_rsc", [1, 64], F32))
                    poolrow = eC(nc.sbuf_tensor("sb_poolrow", [1, D], F32))
                    b2r = eC(nc.sbuf_tensor("sb_b2r", [1, 2, D], F32))
                    b2g = eC(nc.sbuf_tensor("sb_b2g", [8, D], F32))
                    psC = eC(nc.psum_tensor("pm_psC", [128, 4096], F32))

                    for dc in range(8):
                        op("g", lambda E, dc=dc: E.dma_start(out=wob[:, dc, :], in_=d_wo[dc * 128:(dc + 1) * 128, :]),
                           writes=[("wob",)], inc=16)

                    for j in range(NSLOT):
                        for dh in range(2):
                            for cc in range(8):
                                op("t", lambda E, j=j, dh=dh, cc=cc: E.matmul(
                                    psC[:, dh * 512:(dh + 1) * 512],
                                    fT[:, cc, j * 128:(j + 1) * 128], wob[:, cc, dh * 512:(dh + 1) * 512],
                                    start=(cc == 0), stop=(cc == 7)),
                                   reads=[("fT", j), ("wob",)], writes=[("psC", dh)])
                        for dh in range(2):
                            op("v", lambda E, j=j, dh=dh: E.tensor_add(
                                r1[:, j, dh * 512:(dh + 1) * 512], r1[:, j, dh * 512:(dh + 1) * 512],
                                psC[:, dh * 512:(dh + 1) * 512]),
                               reads=[("psC", dh), ("r1", j)], writes=[("r1", j)])
                        op("c", lambda E, j=j: E.activation(ln2sc[:], r1[:, j, :], AF.Square),
                           reads=[("r1", j)], writes=[("ln2sc",)])
                        op("v", lambda E: E.reduce_sum(st2[:, 1:2], ln2sc[:], axis=AXX),
                           reads=[("ln2sc",)], writes=[("st2",)])
                        op("v", lambda E, j=j: E.reduce_sum(st2[:, 0:1], r1[:, j, :], axis=AXX),
                           reads=[("r1", j)], writes=[("st2",)])
                        op("v", lambda E: E.tensor_scalar_mul(st2[:, 2:3], st2[:, 0:1], 1.0 / D),
                           reads=[("st2",)], writes=[("st2",)])
                        op("v", lambda E: E.tensor_scalar_mul(st2[:, 3:4], st2[:, 1:2], 1.0 / D),
                           reads=[("st2",)], writes=[("st2",)])
                        op("v", lambda E: E.tensor_mul(st2[:, 4:5], st2[:, 2:3], st2[:, 2:3]),
                           reads=[("st2",)], writes=[("st2",)])
                        op("v", lambda E: E.tensor_sub(st2[:, 3:4], st2[:, 3:4], st2[:, 4:5]),
                           reads=[("st2",)], writes=[("st2",)])
                        op("v", lambda E: E.tensor_scalar_add(st2[:, 5:6], st2[:, 3:4], EPS1),
                           reads=[("st2",)], writes=[("st2",)])
                        op("c", lambda E: E.activation(st2[:, 6:7], st2[:, 5:6], AF.Ln),
                           reads=[("st2",)], writes=[("st2",)])
                        op("c", lambda E: E.activation(st2[:, 6:7], st2[:, 6:7], AF.Exp, scale=-0.5),
                           reads=[("st2",)], writes=[("st2",)])
                        op("v", lambda E, j=j: E.tensor_scalar(
                            ln2sc[:], r1[:, j, :], st2[:, 2:3], st2[:, 6:7], ALU.subtract, ALU.mult),
                           reads=[("r1", j), ("st2",)], writes=[("ln2sc",)])
                        for c8 in range(8):
                            tb = 2 + c8 % 2
                            op("t", lambda E, c8=c8, tb=tb: E.transpose(
                                psC[:, tb * 512:tb * 512 + 128], ln2sc[:, c8 * 128:(c8 + 1) * 128], id128[:]),
                               reads=[("ln2sc",), ("id",)], writes=[("psC", tb)])
                            op("c", lambda E, c8=c8, tb=tb, j=j: E.copy(
                                fT[:, c8, j * 128:(j + 1) * 128], psC[:, tb * 512:tb * 512 + 128]),
                               reads=[("psC", tb)], writes=[("fT", j)])
                        for dh in range(2):
                            op("t", lambda E, j=j, dh=dh: E.matmul(
                                psC[0:1, (4 + dh) * 512:(4 + dh) * 512 + 512], onesf[:],
                                ln2sc[:, dh * 512:(dh + 1) * 512], start=(j == 0), stop=(j == NSLOT - 1)),
                               reads=[("ln2sc",), ("onesf",)], writes=[("psC", 4 + dh)] if j == NSLOT - 1 else [("psCpool", 4 + dh)])

                    if LV >= 4:
                        bld.mark("route")
                        op("v", lambda E: E.tensor_copy(poolrow[0:1, 0:512], psC[0:1, 4 * 512:4 * 512 + 512]),
                           reads=[("psC", 4)], writes=[("poolrow",)])
                        op("v", lambda E: E.tensor_copy(poolrow[0:1, 512:1024], psC[0:1, 5 * 512:5 * 512 + 512]),
                           reads=[("psC", 5)], writes=[("poolrow",)])
                        op("s", lambda E: E.dma_start(out=cc_in[:, :], in_=poolrow[:]),
                           reads=[("poolrow",)], writes=[("ccin",)], inc=16)
                        op("g", lambda E: E.collective_compute(
                            "AllReduce", ALU.add,
                            replica_groups=[[0, 1], [2, 3], [4, 5], [6, 7]],
                            ins=[cc_in.ap().opt()], outs=[cc_out.ap().opt()]),
                           reads=[("ccin",)], writes=[("ccout",)], inc=1, sem=CCSEM)
                        op("g", lambda E: E.dma_start(
                            out=poolT[:], in_=cc_out.ap().rearrange("one (p c) -> (one p) c", c=8)),
                           reads=[("ccout",)], writes=[("poolT",)], inc=16, sem=RSEM)
                        op("v", lambda E: E.tensor_scalar_mul(poolT[:], poolT[:], 1.0 / N),
                           reads=[("poolT",)], writes=[("poolT",)])
                        for cb in range(8):
                            op("t", lambda E, cb=cb: E.matmul(
                                psC[0:1, 6 * 512:6 * 512 + 8], poolT[:, cb:cb + 1], wr_s[:, cb * 8:(cb + 1) * 8],
                                start=(cb == 0), stop=(cb == 7)),
                               reads=[("poolT",), ("wr",)], writes=[("psC", 6)] if cb == 7 else [("psClg",)])
                        op("v", lambda E: E.tensor_copy(rsc[0:1, 0:8], psC[0:1, 6 * 512:6 * 512 + 8]),
                           reads=[("psC", 6)], writes=[("rsc",)])
                        op("v", lambda E: E.reduce_max(rsc[0:1, 8:9], rsc[0:1, 0:8], axis=AXX),
                           reads=[("rsc",)], writes=[("rsc",)])
                        op("v", lambda E: E.tensor_single_scalar(rsc[0:1, 16:24], rsc[0:1, 0:8], rsc[0:1, 8:9], ALU.is_ge),
                           reads=[("rsc",)], writes=[("rsc",)])
                        op("v", lambda E: E.tensor_mul(rsc[0:1, 24:32], rsc[0:1, 16:24], io8[0:1, :]),
                           reads=[("rsc",), ("io8",)], writes=[("rsc",)])
                        op("v", lambda E: E.reduce_sum(rsc[0:1, 56:57], rsc[0:1, 24:32], axis=AXX),
                           reads=[("rsc",)], writes=[("rsc",)])
                        op("v", lambda E: E.scalar_tensor_tensor(
                            rsc[0:1, 32:40], rsc[0:1, 16:24], -1e30, rsc[0:1, 0:8], ALU.mult, ALU.add),
                           reads=[("rsc",)], writes=[("rsc",)])
                        op("v", lambda E: E.reduce_max(rsc[0:1, 10:11], rsc[0:1, 32:40], axis=AXX),
                           reads=[("rsc",)], writes=[("rsc",)])
                        op("v", lambda E: E.tensor_single_scalar(rsc[0:1, 40:48], rsc[0:1, 32:40], rsc[0:1, 10:11], ALU.is_ge),
                           reads=[("rsc",)], writes=[("rsc",)])
                        op("v", lambda E: E.tensor_mul(rsc[0:1, 48:56], rsc[0:1, 40:48], io8[0:1, :]),
                           reads=[("rsc",), ("io8",)], writes=[("rsc",)])
                        op("v", lambda E: E.reduce_sum(rsc[0:1, 57:58], rsc[0:1, 48:56], axis=AXX),
                           reads=[("rsc",)], writes=[("rsc",)])
                        op("v", lambda E: E.tensor_sub(rsc[0:1, 12:13], rsc[0:1, 10:11], rsc[0:1, 8:9]),
                           reads=[("rsc",)], writes=[("rsc",)])
                        op("c", lambda E: E.activation(rsc[0:1, 13:14], rsc[0:1, 12:13], AF.Exp),
                           reads=[("rsc",)], writes=[("rsc",)])
                        op("v", lambda E: E.tensor_scalar_add(rsc[0:1, 14:15], rsc[0:1, 13:14], 1.0),
                           reads=[("rsc",)], writes=[("rsc",)])
                        op("v", lambda E: E.reciprocal(rsc[0:1, 58:59], rsc[0:1, 14:15]),
                           reads=[("rsc",)], writes=[("rsc",)])
                        op("v", lambda E: E.tensor_mul(rsc[0:1, 59:60], rsc[0:1, 13:14], rsc[0:1, 58:59]),
                           reads=[("rsc",)], writes=[("rsc",)])
                        op("t", lambda E: E.matmul(psC[:, 7 * 512:7 * 512 + 4], ones_r[:], rsc[0:1, 56:60],
                                                   start=True, stop=True),
                           reads=[("rsc",), ("onesr",)], writes=[("psC", 7)])
                        op("v", lambda E: E.tensor_copy(rt[:, 0:4], psC[:, 7 * 512:7 * 512 + 4]),
                           reads=[("psC", 7)], writes=[("rt",)])
                        for sx in range(2):
                            op("v", lambda E, sx=sx: E.tensor_scalar_mul(rt[:, 4:5], rt[:, sx:sx + 1], 1024.0),
                               reads=[("rt",)], writes=[("rt",)])
                            op("v", lambda E: E.tensor_single_scalar(idxf[:, 0:8], iotas[:, 0:8], rt[:, 4:5], ALU.add),
                               reads=[("rt",), ("iotas",)], writes=[("idxf",)])
                            op("v", lambda E, sx=sx: E.tensor_copy(idxW1[:, sx, :], idxf[:, 0:8]),
                               reads=[("idxf",)], writes=[("idxW1",)])
                            op("v", lambda E, sx=sx: E.tensor_scalar_mul(rt[:, 5:6], rt[:, sx:sx + 1], 4096.0),
                               reads=[("rt",)], writes=[("rt",)])
                            op("v", lambda E: E.tensor_single_scalar(idxf[:, 0:32], iotas[:, 0:32], rt[:, 5:6], ALU.add),
                               reads=[("rt",), ("iotas",)], writes=[("idxf",)])
                            op("v", lambda E, sx=sx: E.tensor_copy(idxW2[:, sx, :], idxf[:, 0:32]),
                               reads=[("idxf",)], writes=[("idxW2",)])
                            op("v", lambda E, sx=sx: E.tensor_scalar_mul(rt[:, 6:7], rt[:, sx:sx + 1], 32.0),
                               reads=[("rt",)], writes=[("rt",)])
                            op("v", lambda E: E.tensor_single_scalar(idxf[0:32, 0:1], iotas[0:32, 0:1], rt[0:32, 6:7], ALU.add),
                               reads=[("rt",), ("iotas",)], writes=[("idxf",)])
                            op("v", lambda E, sx=sx: E.tensor_copy(idxb1[:, sx:sx + 1], idxf[0:32, 0:1]),
                               reads=[("idxf",)], writes=[("idxb1",)])
                            op("v", lambda E: E.tensor_scalar_mul(idxf[0:8, 1:2], iotas[0:8, 0:1], 1.0e6),
                               reads=[("iotas",)], writes=[("idxf",)])
                            op("v", lambda E, sx=sx: E.tensor_single_scalar(
                                idxf[0:8, 1:2], idxf[0:8, 1:2], rt[0:8, sx:sx + 1], ALU.add),
                               reads=[("rt",)], writes=[("idxf",)])
                            op("v", lambda E, sx=sx: E.tensor_copy(idxb2[:, sx:sx + 1], idxf[0:8, 1:2]),
                               reads=[("idxf",)], writes=[("idxb2",)])
                        for sx in range(2):
                            op("g", lambda E, sx=sx: E.indirect_dma_start(
                                b2g[:, :], None, d_b2.ap(), IndirectOffsetOnAxis(idxb2[:, sx:sx + 1], 0),
                                bounds_check=NE - 1, oob_is_err=False),
                               reads=[("idxb2",)], writes=[("b2g",)], inc=16, sem=RSEM)
                            wcol = 58 + sx
                            op("v", lambda E, sx=sx, wcol=wcol: E.tensor_single_scalar(
                                b2r[0:1, sx, :], b2g[0:1, :], rsc[0:1, wcol:wcol + 1], ALU.mult),
                               reads=[("b2g",), ("rsc",)], writes=[("b2r",)])
                        op("v", lambda E: E.tensor_add(b2r[0:1, 0, :], b2r[0:1, 0, :], b2r[0:1, 1, :]),
                           reads=[("b2r",)], writes=[("b2r",)])
                        for dh in range(2):
                            op("t", lambda E, dh=dh: E.matmul(
                                psC[:, (6 + dh) * 512:(6 + dh) * 512 + 512], ones_r[:],
                                b2r[0:1, 0, dh * 512:(dh + 1) * 512], start=True, stop=True),
                               reads=[("b2r",), ("onesr",)], writes=[("psC", 6 + dh)])
                            op("v", lambda E, dh=dh: E.tensor_copy(
                                b2c[:, dh * 512:(dh + 1) * 512], psC[:, (6 + dh) * 512:(6 + dh) * 512 + 512]),
                               reads=[("psC", 6 + dh)], writes=[("b2c",)])

            if LV >= 5:
                for ee in ENGS:
                    bld.barrier_into(ee)
                bld.mark("ffn")
                with ExitStack() as ctxD:
                    eD = ctxD.enter_context
                    W1sb = eD(nc.sbuf_tensor("sb_W1sb", [128, 8, FF], BF16))
                    hT = eD(nc.sbuf_tensor("sb_hT", [128, 32, 512], BF16))
                    W2sb = eD(nc.sbuf_tensor("sb_W2sb", [128, 2, D], BF16))
                    b1g = eD(nc.sbuf_tensor("sb_b1g", [32, 128], F32))
                    b1T = eD(nc.sbuf_tensor("sb_b1T", [128, 32], F32))
                    psF = eD(nc.psum_tensor("pm_psF", [128, 4096], F32))

                    for sx in range(2):
                        gatecol = rt[:, 2 + sx:3 + sx]
                        for dc in range(8):
                            op("g", lambda E, sx=sx, dc=dc: E.indirect_dma_start(
                                W1sb[:, dc, :], None, W1v, IndirectOffsetOnAxis(idxW1[:, sx, dc:dc + 1], 0)),
                               reads=[("idxW1",)], writes=[("W1sb",)], inc=16, sem=FSEM)
                        op("g", lambda E, sx=sx: E.indirect_dma_start(
                            b1g[:, :], None, b1v, IndirectOffsetOnAxis(idxb1[:, sx:sx + 1], 0)),
                           reads=[("idxb1",)], writes=[("b1g",)], inc=16, sem=RSEM)
                        op("t", lambda E: E.transpose(psF[:, 7 * 512:7 * 512 + 32], b1g[:, :], id128[0:32, 0:32]),
                           reads=[("b1g",), ("id",)], writes=[("psF", 7)])
                        op("v", lambda E: E.tensor_copy(b1T[:], psF[:, 7 * 512:7 * 512 + 32]),
                           reads=[("psF", 7)], writes=[("b1T",)])

                        for v4 in range(4):
                            for ft in range(32):
                                fb = ft % 2
                                for dc in range(8):
                                    op("t", lambda E, ft=ft, dc=dc, v4=v4, fb=fb: E.matmul(
                                        psF[:, fb * 512:(fb + 1) * 512], W1sb[:, dc, ft * 128:(ft + 1) * 128],
                                        fT[:, dc, v4 * 512:(v4 + 1) * 512], start=(dc == 0), stop=(dc == 7)),
                                       reads=[("W1sb",)] + [("fT", v4 * 4 + q) for q in range(4)],
                                       writes=[("psF", fb)])
                                op("c", lambda E, ft=ft, fb=fb: E.activation(
                                    hT[:, ft, :], psF[:, fb * 512:(fb + 1) * 512], AF.Gelu,
                                    bias=b1T[:, ft:ft + 1]),
                                   reads=[("psF", fb), ("b1T",)], writes=[("hT", ft)])
                            for fc in range(32):
                                par2 = fc % 2
                                op("g", lambda E, sx=sx, fc=fc, par2=par2: E.indirect_dma_start(
                                    W2sb[:, par2, :], None, W2v, IndirectOffsetOnAxis(idxW2[:, sx, fc:fc + 1], 0)),
                                   reads=[("idxW2",)], writes=[("W2sb", par2)], inc=16, sem=W2SEM[par2])
                                for tt in range(4):
                                    for dh in range(2):
                                        bk = tt * 2 + dh
                                        op("t", lambda E, fc=fc, tt=tt, dh=dh, par2=par2, bk=bk: E.matmul(
                                            psF[:, bk * 512:(bk + 1) * 512],
                                            hT[:, fc, tt * 128:(tt + 1) * 128],
                                            W2sb[:, par2, dh * 512:(dh + 1) * 512],
                                            start=(fc == 0), stop=(fc == 31)),
                                           reads=[("hT", fc), ("W2sb", par2)], writes=[("psF", bk)])
                            for tt in range(4):
                                for dh in range(2):
                                    bk = tt * 2 + dh
                                    jj = v4 * 4 + tt
                                    op("v", lambda E, bk=bk, jj=jj, dh=dh, gatecol=gatecol: E.scalar_tensor_tensor(
                                        r1[:, jj, dh * 512:(dh + 1) * 512], psF[:, bk * 512:(bk + 1) * 512],
                                        gatecol, r1[:, jj, dh * 512:(dh + 1) * 512], ALU.mult, ALU.add),
                                       reads=[("psF", bk), ("rt",), ("r1", jj)], writes=[("r1", jj)])

                    for j in range(NSLOT):
                        op("v", lambda E, j=j: E.tensor_add(r1[:, j, :], r1[:, j, :], b2c[:]),
                           reads=[("b2c",), ("r1", j)], writes=[("r1", j)])
                        op("s", lambda E, j=j: E.dma_start(out=d_out[j * 128:(j + 1) * 128, :], in_=r1[:, j, :]),
                           reads=[("r1", j)], inc=16, sem=OUTSEM)
                    raw("s", OUTSEM, 16 * NSLOT)

            if LV < 5:
                for ee in ENGS:
                    bld.barrier_into(ee)
                ns = dbg_ns
                for j in range(NSLOT):
                    op("s", lambda E, j=j: E.dma_start(out=dbg["r1"][j * 128:(j + 1) * 128, :], in_=r1[:, j, :]),
                       inc=16, sem=OUTSEM)
                    ns += 16
                if LV >= 2:
                    op("g", lambda E: E.dma_start(out=dbg["ft"][:, :, :], in_=fT[:]), inc=16, sem=OUTSEM); ns += 16
                if LV >= 4:
                    op("s", lambda E: E.dma_start(out=dbg["rt"][:, 0:4], in_=rt[:, 0:4]), inc=16, sem=OUTSEM); ns += 16
                    op("s", lambda E: E.dma_start(out=dbg["rt"][:, 4:5], in_=poolT[:, 0:1], allow_slow_non_contiguous=True), inc=16, sem=OUTSEM); ns += 16
                raw("s", OUTSEM, ns)

            def _run_eng(E, key):
                bounds = [(nm, idx[key]) for nm, idx in bld.marks] + [("end", len(bld.progs[key]))]
                for (nm, lo), (_, hi) in zip(bounds, bounds[1:]):
                    if hi <= lo:
                        continue
                    import os as _os
                    if _os.environ.get("KSCOPES", "1") == "1":
                        with nc.named_scope(nm):
                            for f in bld.progs[key][lo:hi]:
                                f(E)
                    else:
                        for f in bld.progs[key][lo:hi]:
                            f(E)

            with nc.Block() as block:
                @block.tensor
                def _(E):
                    _run_eng(E, "t")

                @block.vector
                def _(E):
                    _run_eng(E, "v")

                @block.scalar
                def _(E):
                    _run_eng(E, "c")

                @block.sync
                def _(E):
                    _run_eng(E, "s")

                @block.gpsimd
                def _(E):
                    _run_eng(E, "g")

    return nc


_NC_CACHE = {}


def _get_nc(upto="full"):
    if upto not in _NC_CACHE:
        _NC_CACHE[upto] = _build(upto)
    return _NC_CACHE[upto]


def _host_prep(inputs):
    x = np.ascontiguousarray(np.asarray(inputs["L0_attn_x"], dtype=np.float32))
    wq = np.ascontiguousarray(np.asarray(inputs["L0_attn_w_q"], dtype=np.float32))
    wk = np.ascontiguousarray(np.asarray(inputs["L0_attn_w_k"], dtype=np.float32))
    wv = np.ascontiguousarray(np.asarray(inputs["L0_attn_w_v"], dtype=np.float32))
    wo = np.ascontiguousarray(np.asarray(inputs["L0_attn_w_o"], dtype=np.float32))
    nw = np.asarray(inputs["L0_attn_norm_w"], dtype=np.float32)
    Wr = np.ascontiguousarray(np.asarray(inputs["L0_ffn_Wr"], dtype=np.float32))
    W1 = np.ascontiguousarray(np.asarray(inputs["L0_ffn_W1"], dtype=np.float32))
    b1 = np.ascontiguousarray(np.asarray(inputs["L0_ffn_b1"], dtype=np.float32))
    W2 = np.ascontiguousarray(np.asarray(inputs["L0_ffn_W2"], dtype=np.float32))
    b2 = np.ascontiguousarray(np.asarray(inputs["L0_ffn_b2"], dtype=np.float32))

    i = np.arange(DH)
    theta = np.power(10000.0, -2.0 * (i // 2).astype(np.float64) / DH)
    pos = np.arange(N, dtype=np.float64)
    ang = pos[:, None] * theta[None, :]
    sign = np.where(np.arange(N) % 2 == 0, -1.0, 1.0)
    rf = (np.cos(ang) + np.sin(ang) * sign[:, None]).astype(np.float32)
    rfT_base = np.concatenate([rf.T, rf.T], axis=0)

    nwT = np.ascontiguousarray(nw.reshape(8, 128).T)
    wr_r = np.ascontiguousarray(Wr.reshape(128, 64))

    in_maps = []
    for core in range(8):
        b, par = core // 2, core % 2
        xl = x[b]
        rfT = rfT_base
        if par:
            perm = np.arange(NT).reshape(NT // 2, 2)[:, ::-1].reshape(NT)
            xl = xl.reshape(NT, 128, D)[perm].reshape(N, D)
            rfT = rfT_base.reshape(128, NT, 128)[:, perm].reshape(128, N)
        zf = np.zeros((128, 128), np.float32) if par == 0 else np.ones((128, 128), np.float32)
        in_maps.append({
            "x": np.ascontiguousarray(xl), "wq": wq, "wk": wk, "wv": wv, "wo": wo,
            "nwT": nwT, "rfT": np.ascontiguousarray(rfT), "zf": zf, "wr": wr_r,
            "W1": W1, "b1": b1, "W2": W2, "b2": b2,
        })
    return in_maps


def _assemble(results, key="out"):
    out = np.empty((B, N, D), dtype=np.float32)
    for core in range(8):
        b, par = core // 2, core % 2
        ol = results[core][key]
        for j in range(NSLOT):
            at = 2 * j + par
            out[b, at * 128:(at + 1) * 128, :] = ol[j * 128:(j + 1) * 128, :]
    return out


def kernel(**inputs):
    nc = _get_nc("full")
    in_maps = _host_prep(inputs)
    res = run_bass_kernel_spmd(nc, in_maps, core_ids=list(range(8)))
    return _assemble(res.results)


# version 03
# speedup vs baseline: 1.2943x; 1.1243x over previous
import numpy as np
from contextlib import ExitStack

import concourse.bass as bass
import concourse.mybir as mybir
from concourse.bass import IndirectOffsetOnAxis
from concourse.bass_utils import run_bass_kernel_spmd

F32 = mybir.dt.float32
BF16 = mybir.dt.bfloat16
I32 = mybir.dt.int32
AF = mybir.ActivationFunctionType
ALU = mybir.AluOpType
AXX = mybir.AxisListType.X

B, N, D = 4, 4096, 1024
H, KVH, DH = 16, 4, 64
NT = N // 128
NSLOT = 16
FF, NE = 4096, 8
EPS1, EPS2 = 1e-5, 1.1920929e-07

ENGS = ("t", "v", "c", "s", "g")


class Bld:

    def __init__(self, nc):
        self.nc = nc
        self.progs = {e: [] for e in ENGS}
        self.sem = {e: nc.alloc_semaphore(f"sem_{e}") for e in ENGS}
        self.cnt = {e: 0 for e in ENGS}
        self.ext = {}
        self.lastw = {}
        self.readers = {}
        self.satisfied = {e: {} for e in ENGS}
        self.marks = [("init", {e: 0 for e in ENGS})]

    def _wait_tok(self, eng, tok, sem, val):
        if val <= 0:
            return
        if self.satisfied[eng].get(tok, 0) >= val:
            return
        self.progs[eng].append(lambda E, s=sem, c=val: E.wait_ge(s, c))
        self.satisfied[eng][tok] = val

    def _wait_prod(self, eng, prod):
        kind = prod[0]
        if kind == "eng":
            _, peng, pcnt = prod
            if peng == eng:
                if eng == "t":
                    return
                self._wait_tok(eng, ("e", eng), self.sem[eng], pcnt)
                return
            self._wait_tok(eng, ("e", peng), self.sem[peng], pcnt)
        else:
            _, semid, pcnt = prod
            sem = self.ext[semid][0]
            self._wait_tok(eng, ("x", semid), sem, pcnt)

    def op(self, eng, fn, reads=(), writes=(), inc=1, sem=None):
        for k in reads:
            lw = self.lastw.get(k)
            if lw:
                self._wait_prod(eng, lw)
        for k in writes:
            lw = self.lastw.get(k)
            if lw:
                self._wait_prod(eng, lw)
            for pe, pc in self.readers.get(k, {}).items():
                self._wait_prod(eng, ("eng", pe, pc))
        if sem is None:
            self.cnt[eng] += inc
            ss = self.sem[eng]
            prod = ("eng", eng, self.cnt[eng])
        else:
            rec = self.ext.setdefault(id(sem), [sem, 0])
            rec[1] += inc
            ss = sem
            prod = ("ext", id(sem), rec[1])
        self.progs[eng].append(lambda E, f=fn, s2=ss, i=inc: f(E).then_inc(s2, i))
        mycnt = self.cnt[eng]
        for k in reads:
            self.readers.setdefault(k, {})[eng] = mycnt
        for k in writes:
            self.lastw[k] = prod
            self.readers[k] = {}
        return prod

    def raw(self, eng, semh, val):
        self.progs[eng].append(lambda E, s=semh, c=val: E.wait_ge(s, c))

    def mark(self, name):
        self.marks.append((name, {e: len(self.progs[e]) for e in ENGS}))

    def barrier_into(self, eng):
        for pe in ENGS:
            if pe != eng:
                self._wait_tok(eng, ("e", pe), self.sem[pe], self.cnt[pe])
        for semid, (sem, val) in self.ext.items():
            self._wait_tok(eng, ("x", semid), sem, val)


def _build(upto="full"):
    LV = {"phase1": 1, "attn": 2, "oproj": 3, "route": 4, "full": 5}[upto]
    nc = bass.Bass("TRN2", target_bir_lowering=False)

    d_x = nc.dram_tensor("x", [N, D], F32, kind="ExternalInput")
    d_wq = nc.dram_tensor("wq", [D, D], F32, kind="ExternalInput")
    d_wk = nc.dram_tensor("wk", [D, 256], F32, kind="ExternalInput")
    d_wv = nc.dram_tensor("wv", [D, 256], F32, kind="ExternalInput")
    d_wo = nc.dram_tensor("wo", [D, D], F32, kind="ExternalInput")
    d_nwT = nc.dram_tensor("nwT", [128, 8], F32, kind="ExternalInput")
    d_rfT = nc.dram_tensor("rfT", [128, N], F32, kind="ExternalInput")
    d_zf = nc.dram_tensor("zf", [128, 128], F32, kind="ExternalInput")
    d_wr = nc.dram_tensor("wr", [128, 64], F32, kind="ExternalInput")
    d_W1 = nc.dram_tensor("W1", [NE, D, FF], F32, kind="ExternalInput")
    d_b1 = nc.dram_tensor("b1", [NE, FF], F32, kind="ExternalInput")
    d_W2 = nc.dram_tensor("W2", [NE, FF, D], F32, kind="ExternalInput")
    d_b2 = nc.dram_tensor("b2", [NE, D], F32, kind="ExternalInput")
    d_out = nc.dram_tensor("out", [NSLOT * 128, D], F32, kind="ExternalOutput")

    tri_np = np.tril(np.ones((128, 128), np.float32)).T
    d_tri = nc.inline_tensor(tri_np, "triconst")
    iotas_np = (np.arange(32)[None, :] * 128 + np.arange(128)[:, None]).astype(np.float32)
    d_iotas = nc.inline_tensor(iotas_np, "iotasconst")
    d_io8 = nc.inline_tensor(np.arange(8, dtype=np.float32).reshape(1, 8), "io8const")
    d_id = nc.inline_tensor(np.eye(128, dtype=np.float32), "idconst")

    cc_in = nc.dram_tensor("cc_in", [1, D], F32)
    cc_out = nc.dram_tensor("cc_out", [1, D], F32)

    dbg = {}
    if LV < 5:
        dbg["r1"] = nc.dram_tensor("dbg_r1", [NSLOT * 128, D], F32, kind="ExternalOutput")
        dbg["qt"] = nc.dram_tensor("dbg_qt", [128, 8, 2048], F32, kind="ExternalOutput")
        dbg["kt"] = nc.dram_tensor("dbg_kt", [128, 2, N], F32, kind="ExternalOutput")
        dbg["va"] = nc.dram_tensor("dbg_va", [128, 32, 260], F32, kind="ExternalOutput")
        dbg["ft"] = nc.dram_tensor("dbg_ft", [128, 8, 2048], F32, kind="ExternalOutput")
        dbg["rt"] = nc.dram_tensor("dbg_rt", [128, 8], F32, kind="ExternalOutput")

    W1v = d_W1.ap().rearrange("e d f -> (e d) f")
    W2v = d_W2.ap().rearrange("e f d -> (e f) d")
    b1v = d_b1.ap().rearrange("e (r p) -> (e r) p", p=128)

    with ExitStack() as ctx:
        e = ctx.enter_context
        r1 = e(nc.sbuf_tensor("sb_r1", [128, NSLOT, D], F32))
        id128 = e(nc.sbuf_tensor("sb_id128", [128, 128], F32))
        tri_b = e(nc.sbuf_tensor("sb_tri_b", [128, 128], BF16))
        zf_b = e(nc.sbuf_tensor("sb_zf_b", [128, 128], BF16))
        iotas = e(nc.sbuf_tensor("sb_iotas", [128, 32], F32))
        io8 = e(nc.sbuf_tensor("sb_io8", [1, 8], F32))
        nwT = e(nc.sbuf_tensor("sb_nwT", [128, 8], F32))
        onesf = e(nc.sbuf_tensor("sb_onesf", [128, 1], F32))
        onesb = e(nc.sbuf_tensor("sb_onesb", [128, 1], BF16))
        ones_r = e(nc.sbuf_tensor("sb_ones_r", [1, 128], F32))
        negM0 = e(nc.sbuf_tensor("sb_negM0", [128, 1], F32))
        rt = e(nc.sbuf_tensor("sb_rt", [128, 8], F32))
        wr_s = e(nc.sbuf_tensor("sb_wr_s", [128, 64], F32))
        b2c = e(nc.sbuf_tensor("sb_b2c", [128, D], F32))
        poolT = e(nc.sbuf_tensor("sb_poolT", [128, 8], F32))
        idxW1 = e(nc.sbuf_tensor("sb_idxW1", [128, 2, 8], I32))
        idxW2 = e(nc.sbuf_tensor("sb_idxW2", [128, 2, 32], I32))
        idxb1 = e(nc.sbuf_tensor("sb_idxb1", [32, 2], I32))
        idxb2 = e(nc.sbuf_tensor("sb_idxb2", [8, 2], I32))
        idxf = e(nc.sbuf_tensor("sb_idxf", [128, 32], F32))
        fT = e(nc.sbuf_tensor("sb_fT", [128, 8, 2048], BF16))

        bld = Bld(nc)
        op, raw = bld.op, bld.raw
        OUTSEM = nc.alloc_semaphore("outsem")
        CCSEM = nc.alloc_semaphore("ccsem")
        XSEM = [nc.alloc_semaphore("xsa"), nc.alloc_semaphore("xsb")]
        RSEM = nc.alloc_semaphore("rsem")
        FSEM = nc.alloc_semaphore("fsem")
        W2SEM = [nc.alloc_semaphore("w2a"), nc.alloc_semaphore("w2b")]
        RFSEM = [nc.alloc_semaphore("rfa"), nc.alloc_semaphore("rfb")]

        with ExitStack() as ctxB:
            eB = ctxB.enter_context
            QT = eB(nc.sbuf_tensor("sb_QT", [128, 8, 2048], BF16))
            KT = eB(nc.sbuf_tensor("sb_KT", [128, 2, N], BF16))
            VA = eB(nc.sbuf_tensor("sb_VA", [128, 32, 260], BF16))

            op("s", lambda E: E.dma_start(out=id128[:], in_=d_id[:, :]), writes=[("id",)], inc=16)
            op("s", lambda E: E.dma_start(out=iotas[:], in_=d_iotas[:, :]), writes=[("iotas",)], inc=16)
            op("s", lambda E: E.dma_start(out=io8[:], in_=d_io8[:, :]), writes=[("io8",)], inc=16)
            op("s", lambda E: E.dma_start(out=nwT[:], in_=d_nwT[:, :]), writes=[("nwT",)], inc=16)
            op("s", lambda E: E.dma_start(out=wr_s[:], in_=d_wr[:, :]), writes=[("wr",)], inc=16)
            op("g", lambda E: E.dma_start(out=tri_b[:], in_=d_tri[:, :]), writes=[("tri",)], inc=16)
            op("g", lambda E: E.dma_start(out=zf_b[:], in_=d_zf[:, :]), writes=[("zfb",)], inc=16)
            op("v", lambda E: E.memset(onesf[:], 1.0), writes=[("onesf",)])
            op("v", lambda E: E.memset(onesb[:], 1.0), writes=[("onesb",)])
            op("v", lambda E: E.memset(ones_r[:], 1.0), writes=[("onesr",)])
            op("v", lambda E: E.memset(VA[:, :, 64:260:65], 1.0), writes=[("vaones",)])

            with ExitStack() as ctxB1:
                eB1 = ctxB1.enter_context
                xw = eB1(nc.sbuf_tensor("sb_xw", [128, 2, D], F32))
                rfs = eB1(nc.sbuf_tensor("sb_rfs", [128, 2, 256], F32))
                xnT = eB1(nc.sbuf_tensor("sb_xnT", [128, 8, 256], BF16))
                wqb = eB1(nc.sbuf_tensor("sb_wqb", [128, 8, D], BF16))
                wkb = eB1(nc.sbuf_tensor("sb_wkb", [128, 8, 256], BF16))
                wvb = eB1(nc.sbuf_tensor("sb_wvb", [128, 8, 256], BF16))
                sc4k = eB1(nc.sbuf_tensor("sb_sc4k", [128, 512], F32))
                stats = eB1(nc.sbuf_tensor("sb_stats", [128, 2, 12], F32))
                m0sc = eB1(nc.sbuf_tensor("sb_m0sc", [1, 8], F32))
                ps1 = eB1(nc.psum_tensor("pm_ps1", [128, 4096], F32))

                for dc in range(8):
                    op("g", lambda E, dc=dc: E.dma_start(out=wqb[:, dc, :], in_=d_wq[dc * 128:(dc + 1) * 128, :]),
                       writes=[("wqb",)], inc=16)
                    op("g", lambda E, dc=dc: E.dma_start(out=wkb[:, dc, :], in_=d_wk[dc * 128:(dc + 1) * 128, :]),
                       writes=[("wkb",)], inc=16)
                    op("g", lambda E, dc=dc: E.dma_start(out=wvb[:, dc, :], in_=d_wv[dc * 128:(dc + 1) * 128, :]),
                       writes=[("wvb",)], inc=16)

                bld.mark("phase1")
                for hw in range(NT // 2):
                    t0 = 2 * hw
                    hb = hw % 2
                    op("s", lambda E, hb=hb, t0=t0: E.dma_start(
                        out=rfs[:, hb, :], in_=d_rfT[:, t0 * 128:t0 * 128 + 256]),
                       writes=[("rfs", hb)], inc=16, sem=RFSEM[hb])
                    for i in range(2):
                        tl = t0 + i
                        xin = d_x[tl * 128:(tl + 1) * 128, :]
                        op("s", lambda E, i=i, xin=xin: E.dma_start(out=xw[:, i, :], in_=xin),
                           writes=[("xw", i)], inc=16, sem=XSEM[i])
                    for i in range(2):
                        S = stats[:, i, :]
                        op("c", lambda E, i=i: E.activation(sc4k[:], xw[:, i, 0:512], AF.Square),
                           reads=[("xw", i)], writes=[("sc4k",)])
                        op("v", lambda E, S=S: E.reduce_sum(S[:, 1:2], sc4k[:], axis=AXX),
                           reads=[("sc4k",)], writes=[("st", i)])
                        op("c", lambda E, i=i: E.activation(sc4k[:], xw[:, i, 512:1024], AF.Square),
                           reads=[("xw", i)], writes=[("sc4k",)])
                        op("v", lambda E, S=S: E.reduce_sum(S[:, 4:5], sc4k[:], axis=AXX),
                           reads=[("sc4k",)], writes=[("st", i)])
                        op("v", lambda E, S=S: E.tensor_add(S[:, 1:2], S[:, 1:2], S[:, 4:5]),
                           reads=[("st", i)], writes=[("st", i)])
                        op("v", lambda E, S=S, i=i: E.reduce_sum(S[:, 0:1], xw[:, i, :], axis=AXX),
                           reads=[("xw", i)], writes=[("st", i)])
                        op("v", lambda E, S=S: E.tensor_scalar_mul(S[:, 2:3], S[:, 0:1], 1.0 / D),
                           reads=[("st", i)], writes=[("st", i)])
                        op("v", lambda E, S=S: E.tensor_scalar_mul(S[:, 3:4], S[:, 1:2], 1.0 / D),
                           reads=[("st", i)], writes=[("st", i)])
                        op("v", lambda E, S=S: E.tensor_mul(S[:, 4:5], S[:, 2:3], S[:, 2:3]),
                           reads=[("st", i)], writes=[("st", i)])
                        op("v", lambda E, S=S: E.tensor_sub(S[:, 3:4], S[:, 3:4], S[:, 4:5]),
                           reads=[("st", i)], writes=[("st", i)])
                        op("v", lambda E, S=S: E.tensor_scalar_add(S[:, 5:6], S[:, 3:4], EPS1),
                           reads=[("st", i)], writes=[("st", i)])
                        op("c", lambda E, S=S: E.activation(S[:, 6:7], S[:, 5:6], AF.Ln),
                           reads=[("st", i)], writes=[("st", i)])
                        op("c", lambda E, S=S: E.activation(S[:, 6:7], S[:, 6:7], AF.Exp, scale=-0.5),
                           reads=[("st", i)], writes=[("st", i)])
                        op("v", lambda E, S=S: E.reciprocal(S[:, 7:8], S[:, 5:6]),
                           reads=[("st", i)], writes=[("st", i)])
                        op("v", lambda E, S=S: E.tensor_mul(S[:, 7:8], S[:, 7:8], S[:, 3:4]),
                           reads=[("st", i)], writes=[("st", i)])
                        op("v", lambda E, S=S: E.tensor_scalar_add(S[:, 8:9], S[:, 7:8], EPS2),
                           reads=[("st", i)], writes=[("st", i)])
                        op("c", lambda E, S=S: E.activation(S[:, 8:9], S[:, 8:9], AF.Ln),
                           reads=[("st", i)], writes=[("st", i)])
                        op("c", lambda E, S=S: E.activation(S[:, 8:9], S[:, 8:9], AF.Exp, scale=-0.5),
                           reads=[("st", i)], writes=[("st", i)])
                        op("v", lambda E, S=S: E.tensor_mul(S[:, 9:10], S[:, 6:7], S[:, 8:9]),
                           reads=[("st", i)], writes=[("st", i)])
                        op("v", lambda E, S=S: E.tensor_scalar_add(S[:, 10:11], S[:, 6:7], 1.0),
                           reads=[("st", i)], writes=[("st", i)])
                        op("v", lambda E, S=S: E.tensor_mul(S[:, 11:12], S[:, 2:3], S[:, 6:7]),
                           reads=[("st", i)], writes=[("st", i)])
                        if i == 0:
                            op("v", lambda E, S=S, i=i, hw=hw: E.tensor_scalar(
                                r1[:, hw, :], xw[:, i, :], S[:, 10:11], S[:, 11:12],
                                ALU.mult, ALU.subtract),
                               reads=[("xw", i), ("st", i)], writes=[("r1", hw)])
                        op("v", lambda E, S=S, i=i: E.tensor_scalar(
                            xw[:, i, :], xw[:, i, :], S[:, 2:3], S[:, 9:10],
                            ALU.subtract, ALU.mult),
                           reads=[("st", i)], writes=[("xw", i)])
                        for c8 in range(8):
                            tb = c8 % 2
                            op("t", lambda E, c8=c8, tb=tb, i=i: E.transpose(
                                ps1[:, tb * 512:tb * 512 + 128], xw[:, i, c8 * 128:(c8 + 1) * 128], id128[:]),
                               reads=[("xw", i), ("id",)], writes=[("ps1", tb)])
                            op("c", lambda E, c8=c8, tb=tb, i=i: E.mul(
                                xnT[:, c8, i * 128:(i + 1) * 128], ps1[:, tb * 512:tb * 512 + 128],
                                nwT[:, c8:c8 + 1]),
                               reads=[("ps1", tb), ("nwT",)], writes=[("xnT", c8)])
                    for qc in range(8):
                        qb = 2 + qc % 2
                        for dc in range(8):
                            op("t", lambda E, qc=qc, dc=dc, qb=qb: E.matmul(
                                ps1[:, qb * 512:qb * 512 + 128], wqb[:, dc, qc * 128:(qc + 1) * 128],
                                xnT[:, dc, 0:128], start=(dc == 0), stop=(dc == 7)),
                               reads=[("wqb",), ("xnT", dc)], writes=[("ps1", qb)])
                        op("v", lambda E, qc=qc, qb=qb, hw=hw, hb=hb: E.tensor_mul(
                            QT[:, qc, hw * 128:(hw + 1) * 128], ps1[:, qb * 512:qb * 512 + 128],
                            rfs[:, hb, 0:128]),
                           reads=[("ps1", qb), ("rfs", hb)], writes=[("QT",)])
                    for kc2 in range(2):
                        kb = 4 + kc2
                        for dc in range(8):
                            op("t", lambda E, kc2=kc2, dc=dc, kb=kb: E.matmul(
                                ps1[:, kb * 512:kb * 512 + 256], wkb[:, dc, kc2 * 128:(kc2 + 1) * 128],
                                xnT[:, dc, 0:256], start=(dc == 0), stop=(dc == 7)),
                               reads=[("wkb",), ("xnT", dc)], writes=[("ps1", kb)])
                        op("v", lambda E, kc2=kc2, kb=kb, t0=t0, hb=hb: E.tensor_mul(
                            KT[:, kc2, t0 * 128:t0 * 128 + 256], ps1[:, kb * 512:kb * 512 + 256],
                            rfs[:, hb, 0:256]),
                           reads=[("ps1", kb), ("rfs", hb)], writes=[("KT",)])
                    for i in range(2):
                        vb = 6 + i
                        tl = t0 + i
                        for dc in range(8):
                            op("t", lambda E, i=i, dc=dc, vb=vb: E.matmul(
                                ps1[:, vb * 512:vb * 512 + 256], xnT[:, dc, i * 128:(i + 1) * 128],
                                wvb[:, dc, :], start=(dc == 0), stop=(dc == 7)),
                               reads=[("wvb",), ("xnT", dc)], writes=[("ps1", vb)])
                        op("v", lambda E, vb=vb, tl=tl: E.tensor_copy(
                            VA[:, tl, :].rearrange("p (k s) -> p k s", s=65)[:, :, 0:64],
                            ps1[:, vb * 512:vb * 512 + 256].rearrange("p (k s) -> p k s", s=64)),
                           reads=[("ps1", vb), ("vaones",)], writes=[("VA", tl)])

                bld.mark("m0")
                msc = wqb
                op("v", lambda E: E.memset(m0sc[0:1, 0:1], 0.0), writes=[("m0",)])
                op("v", lambda E: E.memset(m0sc[0:1, 2:3], 0.0), writes=[("m0",)])
                for qc in range(8):
                    for half in range(2):
                        op("v", lambda E, qc=qc, half=half: E.tensor_mul(
                            msc[:, half, :], QT[:, qc, half * 1024:(half + 1) * 1024],
                            QT[:, qc, half * 1024:(half + 1) * 1024]),
                           reads=[("QT",)], writes=[("msc",), ("wqb",)])
                    for pc in range(4):
                        op("t", lambda E, pc=pc: E.matmul(
                            ps1[0:1, 0:512], onesb[:], msc[:, pc // 2, (pc % 2) * 512:(pc % 2) * 512 + 512],
                            start=True, stop=True),
                           reads=[("msc",), ("onesb",)], writes=[("ps1", 0)])
                        op("v", lambda E: E.reduce_max(m0sc[0:1, 1:2], ps1[0:1, 0:512], axis=AXX),
                           reads=[("ps1", 0)], writes=[("m0",)])
                        op("v", lambda E: E.tensor_max(m0sc[0:1, 0:1], m0sc[0:1, 0:1], m0sc[0:1, 1:2]),
                           reads=[("m0",)], writes=[("m0",)])
                for kc2 in range(2):
                    for half in range(4):
                        op("v", lambda E, kc2=kc2, half=half: E.tensor_mul(
                            msc[:, half % 2, :], KT[:, kc2, half * 1024:(half + 1) * 1024],
                            KT[:, kc2, half * 1024:(half + 1) * 1024]),
                           reads=[("KT",)], writes=[("msc",), ("wqb",)])
                        for pc in range(2):
                            op("t", lambda E, half=half, pc=pc: E.matmul(
                                ps1[0:1, 0:512], onesb[:], msc[:, half % 2, pc * 512:(pc + 1) * 512],
                                start=True, stop=True),
                               reads=[("msc",), ("onesb",)], writes=[("ps1", 0)])
                            op("v", lambda E: E.reduce_max(m0sc[0:1, 3:4], ps1[0:1, 0:512], axis=AXX),
                               reads=[("ps1", 0)], writes=[("m0",)])
                            op("v", lambda E: E.tensor_max(m0sc[0:1, 2:3], m0sc[0:1, 2:3], m0sc[0:1, 3:4]),
                               reads=[("m0",)], writes=[("m0",)])
                op("v", lambda E: E.tensor_add(m0sc[0:1, 4:5], m0sc[0:1, 0:1], m0sc[0:1, 2:3]),
                   reads=[("m0",)], writes=[("m0",)])
                op("v", lambda E: E.tensor_scalar_mul(m0sc[0:1, 4:5], m0sc[0:1, 4:5], -0.0625),
                   reads=[("m0",)], writes=[("m0",)])
                op("t", lambda E: E.matmul(ps1[:, 512:513], ones_r[:], m0sc[0:1, 4:5], start=True, stop=True),
                   reads=[("m0",), ("onesr",)], writes=[("ps1", 1)])
                op("v", lambda E: E.tensor_copy(negM0[:], ps1[:, 512:513]),
                   reads=[("ps1", 1)], writes=[("negM0",)])

                for ee in ENGS:
                    bld.barrier_into(ee)

            if LV >= 2:
                bld.mark("attn")
                with ExitStack() as ctxB2:
                    eB2 = ctxB2.enter_context
                    Psl = eB2(nc.sbuf_tensor("sb_Psl", [128, 2, 4, 512], BF16))
                    linv = eB2(nc.sbuf_tensor("sb_linv", [1, 1024], F32))
                    lnl = eB2(nc.sbuf_tensor("sb_lnl", [1, 1024], F32))
                    lrep = eB2(nc.sbuf_tensor("sb_lrep", [128, 2, 512], F32))
                    stage = eB2(nc.sbuf_tensor("sb_stage", [65, 2, 1024], F32))
                    psA = eB2(nc.psum_tensor("pm_psA", [128, 3072], F32))
                    psO = eB2(nc.psum_tensor("pm_psO", [128, 1024], F32))

                    for g in range(4):
                        ext = 8 * g + 8
                        for c in range(8):
                            kch = c % 2
                            kv_e = (2 * c) % 4
                            sg = (8 * g + c) % 2
                            batches, k0 = [], 0
                            while k0 < 8 * g:
                                sz = 2 if (len(batches) % 2 == 0) else 1
                                sz = min(sz, 8 * g - k0)
                                batches.append(list(range(k0, k0 + sz)))
                                k0 += sz
                            for n in range(8):
                                batches.append([8 * g + n])
                            for bi2, kcs in enumerate(batches):
                                slot = 0 if len(batches[bi2]) == 2 else (bi2 % 2)
                                if len(kcs) == 2:
                                    slot = 0 if (bi2 % 2 == 0) else 0
                                if len(kcs) == 2:
                                    slot, bank0 = 0, 0
                                else:
                                    slot = bi2 % 2
                                    bank0 = 0 if slot == 0 else 4
                                n0 = kcs[0] - 8 * g
                                mstart = max(0, n0 // 2) if kcs[0] >= 8 * g else 0
                                w = 512 - mstart * 128
                                for ki, kc in enumerate(kcs):
                                    for h2 in range(2):
                                        bk = bank0 + ki * 2 + h2
                                        op("t", lambda E, kch=kch, kc=kc, h2=h2, c=c, g=g, bk=bk, mstart=mstart, w=w: E.matmul(
                                            psA[:, bk * 512 + mstart * 128:bk * 512 + mstart * 128 + w],
                                            KT[h2 * 64:(h2 + 1) * 64, kch, kc * 128:(kc + 1) * 128],
                                            QT[h2 * 64:(h2 + 1) * 64, c, g * 512 + mstart * 128:(g + 1) * 512],
                                            start=True, stop=True, tile_position=(h2 * 64, 0)),
                                           reads=[("KT",), ("QT",)], writes=[("psA", bk)])
                                nb = len(kcs) * 2
                                if mstart > 0:
                                    for ti in range(nb):
                                        op("v", lambda E, slot=slot, ti=ti, mstart=mstart: E.memset(
                                            Psl[:, slot, ti, 0:mstart * 128], 0.0),
                                           writes=[("Pz", slot)])
                                op("c", lambda E, bank0=bank0, nb=nb, slot=slot, mstart=mstart, w=w: E.activation(
                                    Psl[:, slot, 0:nb, mstart * 128:512],
                                    psA[:, bank0 * 512:(bank0 + nb) * 512].rearrange(
                                        "p (b f) -> p b f", f=512)[:, :, mstart * 128:512] if mstart else
                                    psA[:, bank0 * 512:(bank0 + nb) * 512],
                                    AF.Exp, bias=negM0[:, 0:1], scale=0.125),
                                   reads=[("negM0",), ("Pz", slot)] + [("psA", bank0 + j) for j in range(nb)],
                                   writes=[("P", slot)])
                                for ki, kc in enumerate(kcs):
                                    if kc < 8 * g:
                                        continue
                                    n = kc - 8 * g
                                    m = n // 2
                                    for h2 in range(2):
                                        ti = ki * 2 + h2
                                        if n % 2 == 0:
                                            msk, mk = tri_b, ("tri",)
                                        else:
                                            msk, mk = zf_b, ("zfb",)
                                        op("v", lambda E, slot=slot, ti=ti, m=m, msk=msk: E.tensor_tensor(
                                            Psl[:, slot, ti, m * 128:(m + 1) * 128],
                                            Psl[:, slot, ti, m * 128:(m + 1) * 128],
                                            msk[:, :], ALU.mult),
                                           reads=[mk], writes=[("P", slot)])
                                for ki, kc in enumerate(kcs):
                                    for h2 in range(2):
                                        kv = kv_e + h2
                                        op("t", lambda E, kc=kc, h2=h2, kv=kv, slot=slot, ki=ki, ext=ext, mstart=mstart, g=g: E.matmul(
                                            psO[0:65, h2 * 512 + mstart * 128:(h2 + 1) * 512],
                                            VA[:, kc, kv * 65:kv * 65 + 65],
                                            Psl[:, slot, ki * 2 + h2, mstart * 128:512],
                                            start=(kc == 0), stop=(kc == ext - 1)),
                                           reads=[("P", slot), ("Pz", slot), ("VA", kc)], writes=[("psO", h2)])
                            op("v", lambda E, sg=sg: E.tensor_copy(stage[:, sg, :], psO[0:65, :]),
                               reads=[("psO", 0), ("psO", 1)], writes=[("stg", sg)])
                            op("c", lambda E, sg=sg: E.activation(lnl[0:1, :], stage[64:65, sg, :], AF.Ln),
                               reads=[("stg", sg)], writes=[("lnl",)])
                            op("c", lambda E: E.activation(linv[0:1, :], lnl[0:1, :], AF.Exp, scale=-1.0),
                               reads=[("lnl",)], writes=[("linv",)])
                            for h2 in range(2):
                                op("t", lambda E, h2=h2: E.matmul(
                                    psO[0:128, h2 * 512:(h2 + 1) * 512], ones_r[:],
                                    linv[0:1, h2 * 512:(h2 + 1) * 512],
                                    start=True, stop=True),
                                   reads=[("linv",), ("onesr",)], writes=[("psO", h2)])
                                op("v", lambda E, h2=h2: E.tensor_copy(lrep[:, h2, :], psO[:, h2 * 512:(h2 + 1) * 512]),
                                   reads=[("psO", h2)], writes=[("lrep", h2)])
                                op("v", lambda E, h2=h2, c=c, g=g, sg=sg: E.tensor_mul(
                                    fT[h2 * 64:h2 * 64 + 64, c, g * 512:(g + 1) * 512],
                                    stage[0:64, sg, h2 * 512:(h2 + 1) * 512], lrep[0:64, h2, :]),
                                   reads=[("stg", sg), ("lrep", h2)],
                                   writes=[("fT", 4 * g + mm) for mm in range(4)])

            if LV < 5:
                for ee in ENGS:
                    bld.barrier_into(ee)
                ns = 0
                op("g", lambda E: E.dma_start(out=dbg["qt"][:, :, :], in_=QT[:]), inc=16, sem=OUTSEM); ns += 16
                op("g", lambda E: E.dma_start(out=dbg["kt"][:, :, :], in_=KT[:]), inc=16, sem=OUTSEM); ns += 16
                op("g", lambda E: E.dma_start(out=dbg["va"][:, :, :], in_=VA[:]), inc=16, sem=OUTSEM); ns += 16
                dbg_ns = ns

            for ee in ENGS:
                bld.barrier_into(ee)
            ctxB.close()

            if LV >= 3:
                bld.mark("oproj")
                with ExitStack() as ctxC:
                    eC = ctxC.enter_context
                    wob = eC(nc.sbuf_tensor("sb_wob", [128, 8, D], BF16))
                    ln2sc = eC(nc.sbuf_tensor("sb_ln2sc", [128, D], F32))
                    st2 = eC(nc.sbuf_tensor("sb_st2", [128, 8], F32))
                    rsc = eC(nc.sbuf_tensor("sb_rsc", [1, 64], F32))
                    poolrow = eC(nc.sbuf_tensor("sb_poolrow", [1, D], F32))
                    b2r = eC(nc.sbuf_tensor("sb_b2r", [1, 2, D], F32))
                    b2g = eC(nc.sbuf_tensor("sb_b2g", [8, D], F32))
                    psC = eC(nc.psum_tensor("pm_psC", [128, 4096], F32))

                    for dc in range(8):
                        op("g", lambda E, dc=dc: E.dma_start(out=wob[:, dc, :], in_=d_wo[dc * 128:(dc + 1) * 128, :]),
                           writes=[("wob",)], inc=16)

                    for j in range(NSLOT):
                        ob = (j % 2) * 2
                        for dh in range(2):
                            for cc in range(8):
                                op("t", lambda E, j=j, dh=dh, cc=cc, ob=ob: E.matmul(
                                    psC[:, (ob + dh) * 512:(ob + dh + 1) * 512],
                                    fT[:, cc, j * 128:(j + 1) * 128], wob[:, cc, dh * 512:(dh + 1) * 512],
                                    start=(cc == 0), stop=(cc == 7)),
                                   reads=[("fT", j), ("wob",)], writes=[("psC", ob + dh)])
                        for dh in range(2):
                            op("v", lambda E, j=j, dh=dh, ob=ob: E.tensor_add(
                                r1[:, j, dh * 512:(dh + 1) * 512], r1[:, j, dh * 512:(dh + 1) * 512],
                                psC[:, (ob + dh) * 512:(ob + dh + 1) * 512]),
                               reads=[("psC", ob + dh), ("r1", j)], writes=[("r1", j)])
                        op("c", lambda E, j=j: E.activation(ln2sc[:], r1[:, j, :], AF.Square),
                           reads=[("r1", j)], writes=[("ln2sc",)])
                        op("v", lambda E: E.reduce_sum(st2[:, 1:2], ln2sc[:], axis=AXX),
                           reads=[("ln2sc",)], writes=[("st2",)])
                        op("v", lambda E, j=j: E.reduce_sum(st2[:, 0:1], r1[:, j, :], axis=AXX),
                           reads=[("r1", j)], writes=[("st2",)])
                        op("v", lambda E: E.tensor_scalar_mul(st2[:, 2:3], st2[:, 0:1], 1.0 / D),
                           reads=[("st2",)], writes=[("st2",)])
                        op("v", lambda E: E.tensor_scalar_mul(st2[:, 3:4], st2[:, 1:2], 1.0 / D),
                           reads=[("st2",)], writes=[("st2",)])
                        op("v", lambda E: E.tensor_mul(st2[:, 4:5], st2[:, 2:3], st2[:, 2:3]),
                           reads=[("st2",)], writes=[("st2",)])
                        op("v", lambda E: E.tensor_sub(st2[:, 3:4], st2[:, 3:4], st2[:, 4:5]),
                           reads=[("st2",)], writes=[("st2",)])
                        op("v", lambda E: E.tensor_scalar_add(st2[:, 5:6], st2[:, 3:4], EPS1),
                           reads=[("st2",)], writes=[("st2",)])
                        op("c", lambda E: E.activation(st2[:, 6:7], st2[:, 5:6], AF.Ln),
                           reads=[("st2",)], writes=[("st2",)])
                        op("c", lambda E: E.activation(st2[:, 6:7], st2[:, 6:7], AF.Exp, scale=-0.5),
                           reads=[("st2",)], writes=[("st2",)])
                        op("v", lambda E, j=j: E.tensor_scalar(
                            ln2sc[:], r1[:, j, :], st2[:, 2:3], st2[:, 6:7], ALU.subtract, ALU.mult),
                           reads=[("r1", j), ("st2",)], writes=[("ln2sc",)])
                        for c8 in range(8):
                            tb = 4 + c8 % 2
                            op("t", lambda E, c8=c8, tb=tb: E.transpose(
                                psC[:, tb * 512:tb * 512 + 128], ln2sc[:, c8 * 128:(c8 + 1) * 128], id128[:]),
                               reads=[("ln2sc",), ("id",)], writes=[("psC", tb)])
                            op("c", lambda E, c8=c8, tb=tb, j=j: E.copy(
                                fT[:, c8, j * 128:(j + 1) * 128], psC[:, tb * 512:tb * 512 + 128]),
                               reads=[("psC", tb)], writes=[("fT", j)])
                        for dh in range(2):
                            op("t", lambda E, j=j, dh=dh: E.matmul(
                                psC[0:1, (6 + dh) * 512:(6 + dh) * 512 + 512], onesf[:],
                                ln2sc[:, dh * 512:(dh + 1) * 512], start=(j == 0), stop=(j == NSLOT - 1)),
                               reads=[("ln2sc",), ("onesf",)], writes=[("psC", 6 + dh)] if j == NSLOT - 1 else [("psCpool", 6 + dh)])

                    if LV >= 4:
                        bld.mark("route")
                        op("v", lambda E: E.tensor_copy(poolrow[0:1, 0:512], psC[0:1, 6 * 512:6 * 512 + 512]),
                           reads=[("psC", 6)], writes=[("poolrow",)])
                        op("v", lambda E: E.tensor_copy(poolrow[0:1, 512:1024], psC[0:1, 7 * 512:7 * 512 + 512]),
                           reads=[("psC", 7)], writes=[("poolrow",)])
                        op("s", lambda E: E.dma_start(out=cc_in[:, :], in_=poolrow[:]),
                           reads=[("poolrow",)], writes=[("ccin",)], inc=16)
                        op("g", lambda E: E.collective_compute(
                            "AllReduce", ALU.add,
                            replica_groups=[[0, 1], [2, 3], [4, 5], [6, 7]],
                            ins=[cc_in.ap().opt()], outs=[cc_out.ap().opt()]),
                           reads=[("ccin",)], writes=[("ccout",)], inc=1, sem=CCSEM)
                        op("g", lambda E: E.dma_start(
                            out=poolT[:], in_=cc_out.ap().rearrange("one (p c) -> (one p) c", c=8)),
                           reads=[("ccout",)], writes=[("poolT",)], inc=16, sem=RSEM)
                        op("v", lambda E: E.tensor_scalar_mul(poolT[:], poolT[:], 1.0 / N),
                           reads=[("poolT",)], writes=[("poolT",)])
                        for cb in range(8):
                            op("t", lambda E, cb=cb: E.matmul(
                                psC[0:1, 6 * 512:6 * 512 + 8], poolT[:, cb:cb + 1], wr_s[:, cb * 8:(cb + 1) * 8],
                                start=(cb == 0), stop=(cb == 7)),
                               reads=[("poolT",), ("wr",)], writes=[("psC", 6)] if cb == 7 else [("psClg",)])
                        op("v", lambda E: E.tensor_copy(rsc[0:1, 0:8], psC[0:1, 6 * 512:6 * 512 + 8]),
                           reads=[("psC", 6)], writes=[("rsc",)])
                        op("v", lambda E: E.reduce_max(rsc[0:1, 8:9], rsc[0:1, 0:8], axis=AXX),
                           reads=[("rsc",)], writes=[("rsc",)])
                        op("v", lambda E: E.tensor_single_scalar(rsc[0:1, 16:24], rsc[0:1, 0:8], rsc[0:1, 8:9], ALU.is_ge),
                           reads=[("rsc",)], writes=[("rsc",)])
                        op("v", lambda E: E.tensor_mul(rsc[0:1, 24:32], rsc[0:1, 16:24], io8[0:1, :]),
                           reads=[("rsc",), ("io8",)], writes=[("rsc",)])
                        op("v", lambda E: E.reduce_sum(rsc[0:1, 56:57], rsc[0:1, 24:32], axis=AXX),
                           reads=[("rsc",)], writes=[("rsc",)])
                        op("v", lambda E: E.scalar_tensor_tensor(
                            rsc[0:1, 32:40], rsc[0:1, 16:24], -1e30, rsc[0:1, 0:8], ALU.mult, ALU.add),
                           reads=[("rsc",)], writes=[("rsc",)])
                        op("v", lambda E: E.reduce_max(rsc[0:1, 10:11], rsc[0:1, 32:40], axis=AXX),
                           reads=[("rsc",)], writes=[("rsc",)])
                        op("v", lambda E: E.tensor_single_scalar(rsc[0:1, 40:48], rsc[0:1, 32:40], rsc[0:1, 10:11], ALU.is_ge),
                           reads=[("rsc",)], writes=[("rsc",)])
                        op("v", lambda E: E.tensor_mul(rsc[0:1, 48:56], rsc[0:1, 40:48], io8[0:1, :]),
                           reads=[("rsc",), ("io8",)], writes=[("rsc",)])
                        op("v", lambda E: E.reduce_sum(rsc[0:1, 57:58], rsc[0:1, 48:56], axis=AXX),
                           reads=[("rsc",)], writes=[("rsc",)])
                        op("v", lambda E: E.tensor_sub(rsc[0:1, 12:13], rsc[0:1, 10:11], rsc[0:1, 8:9]),
                           reads=[("rsc",)], writes=[("rsc",)])
                        op("c", lambda E: E.activation(rsc[0:1, 13:14], rsc[0:1, 12:13], AF.Exp),
                           reads=[("rsc",)], writes=[("rsc",)])
                        op("v", lambda E: E.tensor_scalar_add(rsc[0:1, 14:15], rsc[0:1, 13:14], 1.0),
                           reads=[("rsc",)], writes=[("rsc",)])
                        op("v", lambda E: E.reciprocal(rsc[0:1, 58:59], rsc[0:1, 14:15]),
                           reads=[("rsc",)], writes=[("rsc",)])
                        op("v", lambda E: E.tensor_mul(rsc[0:1, 59:60], rsc[0:1, 13:14], rsc[0:1, 58:59]),
                           reads=[("rsc",)], writes=[("rsc",)])
                        op("t", lambda E: E.matmul(psC[:, 7 * 512:7 * 512 + 4], ones_r[:], rsc[0:1, 56:60],
                                                   start=True, stop=True),
                           reads=[("rsc",), ("onesr",)], writes=[("psC", 7)])
                        op("v", lambda E: E.tensor_copy(rt[:, 0:4], psC[:, 7 * 512:7 * 512 + 4]),
                           reads=[("psC", 7)], writes=[("rt",)])
                        for sx in range(2):
                            op("v", lambda E, sx=sx: E.tensor_scalar_mul(rt[:, 4:5], rt[:, sx:sx + 1], 1024.0),
                               reads=[("rt",)], writes=[("rt",)])
                            op("v", lambda E: E.tensor_single_scalar(idxf[:, 0:8], iotas[:, 0:8], rt[:, 4:5], ALU.add),
                               reads=[("rt",), ("iotas",)], writes=[("idxf",)])
                            op("v", lambda E, sx=sx: E.tensor_copy(idxW1[:, sx, :], idxf[:, 0:8]),
                               reads=[("idxf",)], writes=[("idxW1",)])
                            op("v", lambda E, sx=sx: E.tensor_scalar_mul(rt[:, 5:6], rt[:, sx:sx + 1], 4096.0),
                               reads=[("rt",)], writes=[("rt",)])
                            op("v", lambda E: E.tensor_single_scalar(idxf[:, 0:32], iotas[:, 0:32], rt[:, 5:6], ALU.add),
                               reads=[("rt",), ("iotas",)], writes=[("idxf",)])
                            op("v", lambda E, sx=sx: E.tensor_copy(idxW2[:, sx, :], idxf[:, 0:32]),
                               reads=[("idxf",)], writes=[("idxW2",)])
                            op("v", lambda E, sx=sx: E.tensor_scalar_mul(rt[:, 6:7], rt[:, sx:sx + 1], 32.0),
                               reads=[("rt",)], writes=[("rt",)])
                            op("v", lambda E: E.tensor_single_scalar(idxf[0:32, 0:1], iotas[0:32, 0:1], rt[0:32, 6:7], ALU.add),
                               reads=[("rt",), ("iotas",)], writes=[("idxf",)])
                            op("v", lambda E, sx=sx: E.tensor_copy(idxb1[:, sx:sx + 1], idxf[0:32, 0:1]),
                               reads=[("idxf",)], writes=[("idxb1",)])
                            op("v", lambda E: E.tensor_scalar_mul(idxf[0:8, 1:2], iotas[0:8, 0:1], 1.0e6),
                               reads=[("iotas",)], writes=[("idxf",)])
                            op("v", lambda E, sx=sx: E.tensor_single_scalar(
                                idxf[0:8, 1:2], idxf[0:8, 1:2], rt[0:8, sx:sx + 1], ALU.add),
                               reads=[("rt",)], writes=[("idxf",)])
                            op("v", lambda E, sx=sx: E.tensor_copy(idxb2[:, sx:sx + 1], idxf[0:8, 1:2]),
                               reads=[("idxf",)], writes=[("idxb2",)])
                        for sx in range(2):
                            op("g", lambda E, sx=sx: E.indirect_dma_start(
                                b2g[:, :], None, d_b2.ap(), IndirectOffsetOnAxis(idxb2[:, sx:sx + 1], 0),
                                bounds_check=NE - 1, oob_is_err=False),
                               reads=[("idxb2",)], writes=[("b2g",)], inc=16, sem=RSEM)
                            wcol = 58 + sx
                            op("v", lambda E, sx=sx, wcol=wcol: E.tensor_single_scalar(
                                b2r[0:1, sx, :], b2g[0:1, :], rsc[0:1, wcol:wcol + 1], ALU.mult),
                               reads=[("b2g",), ("rsc",)], writes=[("b2r",)])
                        op("v", lambda E: E.tensor_add(b2r[0:1, 0, :], b2r[0:1, 0, :], b2r[0:1, 1, :]),
                           reads=[("b2r",)], writes=[("b2r",)])
                        for dh in range(2):
                            op("t", lambda E, dh=dh: E.matmul(
                                psC[:, (6 + dh) * 512:(6 + dh) * 512 + 512], ones_r[:],
                                b2r[0:1, 0, dh * 512:(dh + 1) * 512], start=True, stop=True),
                               reads=[("b2r",), ("onesr",)], writes=[("psC", 6 + dh)])
                            op("v", lambda E, dh=dh: E.tensor_copy(
                                b2c[:, dh * 512:(dh + 1) * 512], psC[:, (6 + dh) * 512:(6 + dh) * 512 + 512]),
                               reads=[("psC", 6 + dh)], writes=[("b2c",)])

            if LV >= 5:
                for ee in ENGS:
                    bld.barrier_into(ee)
                bld.mark("ffn")
                with ExitStack() as ctxD:
                    eD = ctxD.enter_context
                    W1sb = eD(nc.sbuf_tensor("sb_W1sb", [128, 8, FF], BF16))
                    hT = eD(nc.sbuf_tensor("sb_hT", [128, 32, 512], BF16))
                    W2sb = eD(nc.sbuf_tensor("sb_W2sb", [128, 2, D], BF16))
                    b1g = eD(nc.sbuf_tensor("sb_b1g", [32, 128], F32))
                    b1T = eD(nc.sbuf_tensor("sb_b1T", [128, 32], F32))
                    psF = eD(nc.psum_tensor("pm_psF", [128, 4096], F32))

                    for sx in range(2):
                        gatecol = rt[:, 2 + sx:3 + sx]
                        for dc in range(8):
                            op("g", lambda E, sx=sx, dc=dc: E.indirect_dma_start(
                                W1sb[:, dc, :], None, W1v, IndirectOffsetOnAxis(idxW1[:, sx, dc:dc + 1], 0)),
                               reads=[("idxW1",)], writes=[("W1sb",)], inc=16, sem=FSEM)
                        op("g", lambda E, sx=sx: E.indirect_dma_start(
                            b1g[:, :], None, b1v, IndirectOffsetOnAxis(idxb1[:, sx:sx + 1], 0)),
                           reads=[("idxb1",)], writes=[("b1g",)], inc=16, sem=RSEM)
                        op("t", lambda E: E.transpose(psF[:, 7 * 512:7 * 512 + 32], b1g[:, :], id128[0:32, 0:32]),
                           reads=[("b1g",), ("id",)], writes=[("psF", 7)])
                        op("v", lambda E: E.tensor_copy(b1T[:], psF[:, 7 * 512:7 * 512 + 32]),
                           reads=[("psF", 7)], writes=[("b1T",)])

                        for v4 in range(4):
                            for ft in range(32):
                                fb = ft % 2
                                for dc in range(8):
                                    op("t", lambda E, ft=ft, dc=dc, v4=v4, fb=fb: E.matmul(
                                        psF[:, fb * 512:(fb + 1) * 512], W1sb[:, dc, ft * 128:(ft + 1) * 128],
                                        fT[:, dc, v4 * 512:(v4 + 1) * 512], start=(dc == 0), stop=(dc == 7)),
                                       reads=[("W1sb",)] + [("fT", v4 * 4 + q) for q in range(4)],
                                       writes=[("psF", fb)])
                                op("c", lambda E, ft=ft, fb=fb: E.activation(
                                    hT[:, ft, :], psF[:, fb * 512:(fb + 1) * 512], AF.Gelu,
                                    bias=b1T[:, ft:ft + 1]),
                                   reads=[("psF", fb), ("b1T",)], writes=[("hT", ft)])
                            for fc in range(32):
                                par2 = fc % 2
                                op("g", lambda E, sx=sx, fc=fc, par2=par2: E.indirect_dma_start(
                                    W2sb[:, par2, :], None, W2v, IndirectOffsetOnAxis(idxW2[:, sx, fc:fc + 1], 0)),
                                   reads=[("idxW2",)], writes=[("W2sb", par2)], inc=16, sem=W2SEM[par2])
                                for tt in range(4):
                                    for dh in range(2):
                                        bk = tt * 2 + dh
                                        op("t", lambda E, fc=fc, tt=tt, dh=dh, par2=par2, bk=bk: E.matmul(
                                            psF[:, bk * 512:(bk + 1) * 512],
                                            hT[:, fc, tt * 128:(tt + 1) * 128],
                                            W2sb[:, par2, dh * 512:(dh + 1) * 512],
                                            start=(fc == 0), stop=(fc == 31)),
                                           reads=[("hT", fc), ("W2sb", par2)], writes=[("psF", bk)])
                            for tt in range(4):
                                for dh in range(2):
                                    bk = tt * 2 + dh
                                    jj = v4 * 4 + tt
                                    op("v", lambda E, bk=bk, jj=jj, dh=dh, gatecol=gatecol: E.scalar_tensor_tensor(
                                        r1[:, jj, dh * 512:(dh + 1) * 512], psF[:, bk * 512:(bk + 1) * 512],
                                        gatecol, r1[:, jj, dh * 512:(dh + 1) * 512], ALU.mult, ALU.add),
                                       reads=[("psF", bk), ("rt",), ("r1", jj)], writes=[("r1", jj)])

                    for j in range(NSLOT):
                        op("v", lambda E, j=j: E.tensor_add(r1[:, j, :], r1[:, j, :], b2c[:]),
                           reads=[("b2c",), ("r1", j)], writes=[("r1", j)])
                        op("s", lambda E, j=j: E.dma_start(out=d_out[j * 128:(j + 1) * 128, :], in_=r1[:, j, :]),
                           reads=[("r1", j)], inc=16, sem=OUTSEM)
                    raw("s", OUTSEM, 16 * NSLOT)

            if LV < 5:
                for ee in ENGS:
                    bld.barrier_into(ee)
                ns = dbg_ns
                for j in range(NSLOT):
                    op("s", lambda E, j=j: E.dma_start(out=dbg["r1"][j * 128:(j + 1) * 128, :], in_=r1[:, j, :]),
                       inc=16, sem=OUTSEM)
                    ns += 16
                if LV >= 2:
                    op("g", lambda E: E.dma_start(out=dbg["ft"][:, :, :], in_=fT[:]), inc=16, sem=OUTSEM); ns += 16
                if LV >= 4:
                    op("s", lambda E: E.dma_start(out=dbg["rt"][:, 0:4], in_=rt[:, 0:4]), inc=16, sem=OUTSEM); ns += 16
                    op("s", lambda E: E.dma_start(out=dbg["rt"][:, 4:5], in_=poolT[:, 0:1], allow_slow_non_contiguous=True), inc=16, sem=OUTSEM); ns += 16
                raw("s", OUTSEM, ns)

            def _run_eng(E, key):
                bounds = [(nm, idx[key]) for nm, idx in bld.marks] + [("end", len(bld.progs[key]))]
                for (nm, lo), (_, hi) in zip(bounds, bounds[1:]):
                    if hi <= lo:
                        continue
                    import os as _os
                    if _os.environ.get("KSCOPES", "1") == "1":
                        with nc.named_scope(nm):
                            for f in bld.progs[key][lo:hi]:
                                f(E)
                    else:
                        for f in bld.progs[key][lo:hi]:
                            f(E)

            with nc.Block() as block:
                @block.tensor
                def _(E):
                    _run_eng(E, "t")

                @block.vector
                def _(E):
                    _run_eng(E, "v")

                @block.scalar
                def _(E):
                    _run_eng(E, "c")

                @block.sync
                def _(E):
                    _run_eng(E, "s")

                @block.gpsimd
                def _(E):
                    _run_eng(E, "g")

    return nc


_NC_CACHE = {}


def _get_nc(upto="full"):
    if upto not in _NC_CACHE:
        _NC_CACHE[upto] = _build(upto)
    return _NC_CACHE[upto]


def _host_prep(inputs):
    x = np.ascontiguousarray(np.asarray(inputs["L0_attn_x"], dtype=np.float32))
    wq = np.ascontiguousarray(np.asarray(inputs["L0_attn_w_q"], dtype=np.float32))
    wk = np.ascontiguousarray(np.asarray(inputs["L0_attn_w_k"], dtype=np.float32))
    wv = np.ascontiguousarray(np.asarray(inputs["L0_attn_w_v"], dtype=np.float32))
    wo = np.ascontiguousarray(np.asarray(inputs["L0_attn_w_o"], dtype=np.float32))
    nw = np.asarray(inputs["L0_attn_norm_w"], dtype=np.float32)
    Wr = np.ascontiguousarray(np.asarray(inputs["L0_ffn_Wr"], dtype=np.float32))
    W1 = np.ascontiguousarray(np.asarray(inputs["L0_ffn_W1"], dtype=np.float32))
    b1 = np.ascontiguousarray(np.asarray(inputs["L0_ffn_b1"], dtype=np.float32))
    W2 = np.ascontiguousarray(np.asarray(inputs["L0_ffn_W2"], dtype=np.float32))
    b2 = np.ascontiguousarray(np.asarray(inputs["L0_ffn_b2"], dtype=np.float32))

    i = np.arange(DH)
    theta = np.power(10000.0, -2.0 * (i // 2).astype(np.float64) / DH)
    pos = np.arange(N, dtype=np.float64)
    ang = pos[:, None] * theta[None, :]
    sign = np.where(np.arange(N) % 2 == 0, -1.0, 1.0)
    rf = (np.cos(ang) + np.sin(ang) * sign[:, None]).astype(np.float32)
    rfT_base = np.concatenate([rf.T, rf.T], axis=0)

    nwT = np.ascontiguousarray(nw.reshape(8, 128).T)
    wr_r = np.ascontiguousarray(Wr.reshape(128, 64))

    in_maps = []
    for core in range(8):
        b, par = core // 2, core % 2
        xl = x[b]
        rfT = rfT_base
        if par:
            perm = np.arange(NT).reshape(NT // 2, 2)[:, ::-1].reshape(NT)
            xl = xl.reshape(NT, 128, D)[perm].reshape(N, D)
            rfT = rfT_base.reshape(128, NT, 128)[:, perm].reshape(128, N)
        zf = np.zeros((128, 128), np.float32) if par == 0 else np.ones((128, 128), np.float32)
        in_maps.append({
            "x": np.ascontiguousarray(xl), "wq": wq, "wk": wk, "wv": wv, "wo": wo,
            "nwT": nwT, "rfT": np.ascontiguousarray(rfT), "zf": zf, "wr": wr_r,
            "W1": W1, "b1": b1, "W2": W2, "b2": b2,
        })
    return in_maps


def _assemble(results, key="out"):
    out = np.empty((B, N, D), dtype=np.float32)
    for core in range(8):
        b, par = core // 2, core % 2
        ol = results[core][key]
        for j in range(NSLOT):
            at = 2 * j + par
            out[b, at * 128:(at + 1) * 128, :] = ol[j * 128:(j + 1) * 128, :]
    return out


def kernel(**inputs):
    nc = _get_nc("full")
    in_maps = _host_prep(inputs)
    res = run_bass_kernel_spmd(nc, in_maps, core_ids=list(range(8)))
    return _assemble(res.results)


# version 05 (final)
# speedup vs baseline: 1.3390x; 1.0345x over previous
import numpy as np
from contextlib import ExitStack

import concourse.bass as bass
import concourse.mybir as mybir
from concourse.bass import IndirectOffsetOnAxis
from concourse.bass_utils import run_bass_kernel_spmd

F32 = mybir.dt.float32
BF16 = mybir.dt.bfloat16
I32 = mybir.dt.int32
AF = mybir.ActivationFunctionType
ALU = mybir.AluOpType
AXX = mybir.AxisListType.X

B, N, D = 4, 4096, 1024
H, KVH, DH = 16, 4, 64
NT = N // 128
NSLOT = 16
FF, NE = 4096, 8
EPS1, EPS2 = 1e-5, 1.1920929e-07

ENGS = ("t", "v", "c", "s", "g")


class Bld:

    def __init__(self, nc):
        self.nc = nc
        self.progs = {e: [] for e in ENGS}
        self.sem = {e: nc.alloc_semaphore(f"sem_{e}") for e in ENGS}
        self.cnt = {e: 0 for e in ENGS}
        self.ext = {}
        self.lastw = {}
        self.readers = {}
        self.satisfied = {e: {} for e in ENGS}
        self.marks = [("init", {e: 0 for e in ENGS})]

    def _wait_tok(self, eng, tok, sem, val):
        if val <= 0:
            return
        if self.satisfied[eng].get(tok, 0) >= val:
            return
        self.progs[eng].append(lambda E, s=sem, c=val: E.wait_ge(s, c))
        self.satisfied[eng][tok] = val

    def _wait_prod(self, eng, prod):
        kind = prod[0]
        if kind == "eng":
            _, peng, pcnt = prod
            if peng == eng:
                if eng == "t":
                    return
                self._wait_tok(eng, ("e", eng), self.sem[eng], pcnt)
                return
            self._wait_tok(eng, ("e", peng), self.sem[peng], pcnt)
        else:
            _, semid, pcnt = prod
            sem = self.ext[semid][0]
            self._wait_tok(eng, ("x", semid), sem, pcnt)

    def op(self, eng, fn, reads=(), writes=(), inc=1, sem=None):
        for k in reads:
            lw = self.lastw.get(k)
            if lw:
                self._wait_prod(eng, lw)
        for k in writes:
            lw = self.lastw.get(k)
            if lw:
                self._wait_prod(eng, lw)
            for pe, pc in self.readers.get(k, {}).items():
                self._wait_prod(eng, ("eng", pe, pc))
        if sem is None:
            self.cnt[eng] += inc
            ss = self.sem[eng]
            prod = ("eng", eng, self.cnt[eng])
        else:
            rec = self.ext.setdefault(id(sem), [sem, 0])
            rec[1] += inc
            ss = sem
            prod = ("ext", id(sem), rec[1])
        self.progs[eng].append(lambda E, f=fn, s2=ss, i=inc: f(E).then_inc(s2, i))
        mycnt = self.cnt[eng]
        for k in reads:
            self.readers.setdefault(k, {})[eng] = mycnt
        for k in writes:
            self.lastw[k] = prod
            self.readers[k] = {}
        return prod

    def raw(self, eng, semh, val):
        self.progs[eng].append(lambda E, s=semh, c=val: E.wait_ge(s, c))

    def mark(self, name):
        self.marks.append((name, {e: len(self.progs[e]) for e in ENGS}))

    def barrier_into(self, eng):
        for pe in ENGS:
            if pe != eng:
                self._wait_tok(eng, ("e", pe), self.sem[pe], self.cnt[pe])
        for semid, (sem, val) in self.ext.items():
            self._wait_tok(eng, ("x", semid), sem, val)


def _build(upto="full"):
    LV = {"phase1": 1, "attn": 2, "oproj": 3, "route": 4, "full": 5}[upto]
    nc = bass.Bass("TRN2", target_bir_lowering=False)

    d_x = nc.dram_tensor("x", [N, D], F32, kind="ExternalInput")
    d_wq = nc.dram_tensor("wq", [D, D], F32, kind="ExternalInput")
    d_wk = nc.dram_tensor("wk", [D, 256], F32, kind="ExternalInput")
    d_wv = nc.dram_tensor("wv", [D, 256], F32, kind="ExternalInput")
    d_wo = nc.dram_tensor("wo", [D, D], F32, kind="ExternalInput")
    d_nwT = nc.dram_tensor("nwT", [128, 8], F32, kind="ExternalInput")
    d_rfT = nc.dram_tensor("rfT", [128, N], F32, kind="ExternalInput")
    d_zf = nc.dram_tensor("zf", [128, 128], F32, kind="ExternalInput")
    d_wr = nc.dram_tensor("wr", [128, 64], F32, kind="ExternalInput")
    d_W1 = nc.dram_tensor("W1", [NE, D, FF], F32, kind="ExternalInput")
    d_b1 = nc.dram_tensor("b1", [NE, FF], F32, kind="ExternalInput")
    d_W2 = nc.dram_tensor("W2", [NE, FF, D], F32, kind="ExternalInput")
    d_b2 = nc.dram_tensor("b2", [NE, D], F32, kind="ExternalInput")
    d_out = nc.dram_tensor("out", [NSLOT * 128, D], F32, kind="ExternalOutput")

    tri_np = np.tril(np.ones((128, 128), np.float32)).T
    d_tri = nc.inline_tensor(tri_np, "triconst")
    iotas_np = (np.arange(32)[None, :] * 128 + np.arange(128)[:, None]).astype(np.float32)
    d_iotas = nc.inline_tensor(iotas_np, "iotasconst")
    d_io8 = nc.inline_tensor(np.arange(8, dtype=np.float32).reshape(1, 8), "io8const")
    d_id = nc.inline_tensor(np.eye(128, dtype=np.float32), "idconst")

    cc_in = nc.dram_tensor("cc_in", [1, D], F32)
    cc_out = nc.dram_tensor("cc_out", [1, D], F32)

    dbg = {}
    if LV < 5:
        dbg["r1"] = nc.dram_tensor("dbg_r1", [NSLOT * 128, D], F32, kind="ExternalOutput")
        dbg["qt"] = nc.dram_tensor("dbg_qt", [128, 8, 2048], F32, kind="ExternalOutput")
        dbg["kt"] = nc.dram_tensor("dbg_kt", [128, 2, N], F32, kind="ExternalOutput")
        dbg["va"] = nc.dram_tensor("dbg_va", [128, 32, 260], F32, kind="ExternalOutput")
        dbg["ft"] = nc.dram_tensor("dbg_ft", [128, 8, 2048], F32, kind="ExternalOutput")
        dbg["rt"] = nc.dram_tensor("dbg_rt", [128, 8], F32, kind="ExternalOutput")

    W1v = d_W1.ap().rearrange("e d f -> (e d) f")
    W2v = d_W2.ap().rearrange("e f d -> (e f) d")
    b1v = d_b1.ap().rearrange("e (r p) -> (e r) p", p=128)

    with ExitStack() as ctx:
        e = ctx.enter_context
        r1 = e(nc.sbuf_tensor("sb_r1", [128, NSLOT, D], F32))
        id128 = e(nc.sbuf_tensor("sb_id128", [128, 128], F32))
        tri_b = e(nc.sbuf_tensor("sb_tri_b", [128, 128], BF16))
        zf_b = e(nc.sbuf_tensor("sb_zf_b", [128, 128], BF16))
        iotas = e(nc.sbuf_tensor("sb_iotas", [128, 32], F32))
        io8 = e(nc.sbuf_tensor("sb_io8", [1, 8], F32))
        nwT = e(nc.sbuf_tensor("sb_nwT", [128, 8], F32))
        onesf = e(nc.sbuf_tensor("sb_onesf", [128, 1], F32))
        onesb = e(nc.sbuf_tensor("sb_onesb", [128, 1], BF16))
        ones_r = e(nc.sbuf_tensor("sb_ones_r", [1, 128], F32))
        negM0 = e(nc.sbuf_tensor("sb_negM0", [128, 1], F32))
        rt = e(nc.sbuf_tensor("sb_rt", [128, 8], F32))
        wr_s = e(nc.sbuf_tensor("sb_wr_s", [128, 64], F32))
        b2c = e(nc.sbuf_tensor("sb_b2c", [128, D], F32))
        poolT = e(nc.sbuf_tensor("sb_poolT", [128, 8], F32))
        idxW1 = e(nc.sbuf_tensor("sb_idxW1", [128, 2, 8], I32))
        idxW2 = e(nc.sbuf_tensor("sb_idxW2", [128, 2, 32], I32))
        idxb1 = e(nc.sbuf_tensor("sb_idxb1", [32, 2], I32))
        idxb2 = e(nc.sbuf_tensor("sb_idxb2", [8, 2], I32))
        idxf = e(nc.sbuf_tensor("sb_idxf", [128, 32], F32))
        fT = e(nc.sbuf_tensor("sb_fT", [128, 8, 2048], BF16))

        bld = Bld(nc)
        op, raw = bld.op, bld.raw
        OUTSEM = nc.alloc_semaphore("outsem")
        CCSEM = nc.alloc_semaphore("ccsem")
        XSEM = [nc.alloc_semaphore("xsa"), nc.alloc_semaphore("xsb")]
        RSEM = nc.alloc_semaphore("rsem")
        FSEM = nc.alloc_semaphore("fsem")
        W2SEM = [nc.alloc_semaphore("w2a"), nc.alloc_semaphore("w2b")]
        RFSEM = [nc.alloc_semaphore("rfa"), nc.alloc_semaphore("rfb")]

        with ExitStack() as ctxB:
            eB = ctxB.enter_context
            QT = eB(nc.sbuf_tensor("sb_QT", [128, 8, 2048], BF16))
            KT = eB(nc.sbuf_tensor("sb_KT", [128, 2, N], BF16))
            VA = eB(nc.sbuf_tensor("sb_VA", [128, 32, 260], BF16))

            op("s", lambda E: E.dma_start(out=id128[:], in_=d_id[:, :]), writes=[("id",)], inc=16)
            op("s", lambda E: E.dma_start(out=iotas[:], in_=d_iotas[:, :]), writes=[("iotas",)], inc=16)
            op("s", lambda E: E.dma_start(out=io8[:], in_=d_io8[:, :]), writes=[("io8",)], inc=16)
            op("s", lambda E: E.dma_start(out=nwT[:], in_=d_nwT[:, :]), writes=[("nwT",)], inc=16)
            op("s", lambda E: E.dma_start(out=wr_s[:], in_=d_wr[:, :]), writes=[("wr",)], inc=16)
            op("g", lambda E: E.dma_start(out=tri_b[:], in_=d_tri[:, :]), writes=[("tri",)], inc=16)
            op("g", lambda E: E.dma_start(out=zf_b[:], in_=d_zf[:, :]), writes=[("zfb",)], inc=16)
            op("v", lambda E: E.memset(onesf[:], 1.0), writes=[("onesf",)])
            op("v", lambda E: E.memset(onesb[:], 1.0), writes=[("onesb",)])
            op("v", lambda E: E.memset(ones_r[:], 1.0), writes=[("onesr",)])
            op("v", lambda E: E.memset(VA[:, :, 64:260:65], 1.0), writes=[("vaones",)])

            with ExitStack() as ctxB1:
                eB1 = ctxB1.enter_context
                xw = eB1(nc.sbuf_tensor("sb_xw", [128, 2, D], F32))
                rfs = eB1(nc.sbuf_tensor("sb_rfs", [128, 2, 256], F32))
                xnT = eB1(nc.sbuf_tensor("sb_xnT", [128, 8, 256], BF16))
                wqb = eB1(nc.sbuf_tensor("sb_wqb", [128, 8, D], BF16))
                wkb = eB1(nc.sbuf_tensor("sb_wkb", [128, 8, 256], BF16))
                wvb = eB1(nc.sbuf_tensor("sb_wvb", [128, 8, 256], BF16))
                sc4k = eB1(nc.sbuf_tensor("sb_sc4k", [128, 512], F32))
                stats = eB1(nc.sbuf_tensor("sb_stats", [128, 2, 12], F32))
                m0sc = eB1(nc.sbuf_tensor("sb_m0sc", [1, 8], F32))
                ps1 = eB1(nc.psum_tensor("pm_ps1", [128, 4096], F32))

                for dc in range(8):
                    op("g", lambda E, dc=dc: E.dma_start(out=wqb[:, dc, :], in_=d_wq[dc * 128:(dc + 1) * 128, :]),
                       writes=[("wqb",)], inc=16)
                    op("g", lambda E, dc=dc: E.dma_start(out=wkb[:, dc, :], in_=d_wk[dc * 128:(dc + 1) * 128, :]),
                       writes=[("wkb",)], inc=16)
                    op("g", lambda E, dc=dc: E.dma_start(out=wvb[:, dc, :], in_=d_wv[dc * 128:(dc + 1) * 128, :]),
                       writes=[("wvb",)], inc=16)

                bld.mark("phase1")
                for hw in range(NT // 2):
                    t0 = 2 * hw
                    hb = hw % 2
                    op("s", lambda E, hb=hb, t0=t0: E.dma_start(
                        out=rfs[:, hb, :], in_=d_rfT[:, t0 * 128:t0 * 128 + 256]),
                       writes=[("rfs", hb)], inc=16, sem=RFSEM[hb])
                    for i in range(2):
                        tl = t0 + i
                        xin = d_x[tl * 128:(tl + 1) * 128, :]
                        op("s", lambda E, i=i, xin=xin: E.dma_start(out=xw[:, i, :], in_=xin),
                           writes=[("xw", i)], inc=16, sem=XSEM[i])
                    for i in range(2):
                        op("c", lambda E, i=i: E.activation(sc4k[:], xw[:, i, 0:512], AF.Square),
                           reads=[("xw", i)], writes=[("sq",)])
                        op("v", lambda E, i=i: E.reduce_sum(stats[:, i, 1:2], sc4k[:], axis=AXX),
                           reads=[("sq",)], writes=[("st", i)])
                        op("c", lambda E, i=i: E.activation(sc4k[:], xw[:, i, 512:1024], AF.Square),
                           reads=[("xw", i)], writes=[("sq",)])
                        op("v", lambda E, i=i: E.reduce_sum(stats[:, i, 4:5], sc4k[:], axis=AXX),
                           reads=[("sq",)], writes=[("st", i)])
                        op("v", lambda E, i=i: E.reduce_sum(stats[:, i, 0:1], xw[:, i, :], axis=AXX),
                           reads=[("xw", i)], writes=[("st", i)])
                    SB = stats[:, 0:2, :]
                    rdsts = [("st", 0), ("st", 1)]
                    op("v", lambda E, SB=SB: E.tensor_add(SB[:, :, 1:2], SB[:, :, 1:2], SB[:, :, 4:5]),
                       reads=rdsts, writes=rdsts)
                    op("v", lambda E, SB=SB: E.tensor_scalar_mul(SB[:, :, 2:3], SB[:, :, 0:1], 1.0 / D),
                       reads=rdsts, writes=rdsts)
                    op("v", lambda E, SB=SB: E.tensor_scalar_mul(SB[:, :, 3:4], SB[:, :, 1:2], 1.0 / D),
                       reads=rdsts, writes=rdsts)
                    op("v", lambda E, SB=SB: E.tensor_mul(SB[:, :, 4:5], SB[:, :, 2:3], SB[:, :, 2:3]),
                       reads=rdsts, writes=rdsts)
                    op("v", lambda E, SB=SB: E.tensor_sub(SB[:, :, 3:4], SB[:, :, 3:4], SB[:, :, 4:5]),
                       reads=rdsts, writes=rdsts)
                    op("v", lambda E, SB=SB: E.tensor_scalar_add(SB[:, :, 5:6], SB[:, :, 3:4], EPS1),
                       reads=rdsts, writes=rdsts)
                    op("c", lambda E, SB=SB: E.activation(SB[:, :, 6:7], SB[:, :, 5:6], AF.Ln),
                       reads=rdsts, writes=rdsts)
                    op("c", lambda E, SB=SB: E.activation(SB[:, :, 6:7], SB[:, :, 6:7], AF.Exp, scale=-0.5),
                       reads=rdsts, writes=rdsts)
                    op("v", lambda E, SB=SB: E.reciprocal(SB[:, :, 7:8], SB[:, :, 5:6]),
                       reads=rdsts, writes=rdsts)
                    op("v", lambda E, SB=SB: E.tensor_mul(SB[:, :, 7:8], SB[:, :, 7:8], SB[:, :, 3:4]),
                       reads=rdsts, writes=rdsts)
                    op("v", lambda E, SB=SB: E.tensor_scalar_add(SB[:, :, 8:9], SB[:, :, 7:8], EPS2),
                       reads=rdsts, writes=rdsts)
                    op("c", lambda E, SB=SB: E.activation(SB[:, :, 8:9], SB[:, :, 8:9], AF.Ln),
                       reads=rdsts, writes=rdsts)
                    op("c", lambda E, SB=SB: E.activation(SB[:, :, 8:9], SB[:, :, 8:9], AF.Exp, scale=-0.5),
                       reads=rdsts, writes=rdsts)
                    op("v", lambda E, SB=SB: E.tensor_mul(SB[:, :, 9:10], SB[:, :, 6:7], SB[:, :, 8:9]),
                       reads=rdsts, writes=rdsts)
                    op("v", lambda E, SB=SB: E.tensor_scalar_add(SB[:, :, 10:11], SB[:, :, 6:7], 1.0),
                       reads=rdsts, writes=rdsts)
                    op("v", lambda E, SB=SB: E.tensor_mul(SB[:, :, 11:12], SB[:, :, 2:3], SB[:, :, 6:7]),
                       reads=rdsts, writes=rdsts)
                    for i in range(2):
                        S = stats[:, i, :]
                        if i == 0:
                            op("v", lambda E, S=S, i=i, hw=hw: E.tensor_scalar(
                                r1[:, hw, :], xw[:, i, :], S[:, 10:11], S[:, 11:12],
                                ALU.mult, ALU.subtract),
                               reads=[("xw", i), ("st", i)], writes=[("r1", hw)])
                        op("v", lambda E, S=S, i=i: E.tensor_scalar(
                            xw[:, i, :], xw[:, i, :], S[:, 2:3], S[:, 9:10],
                            ALU.subtract, ALU.mult),
                           reads=[("st", i)], writes=[("xw", i)])
                        for c8 in range(8):
                            tb = c8 % 2
                            op("t", lambda E, c8=c8, tb=tb, i=i: E.transpose(
                                ps1[:, tb * 512:tb * 512 + 128], xw[:, i, c8 * 128:(c8 + 1) * 128], id128[:]),
                               reads=[("xw", i), ("id",)], writes=[("ps1", tb)])
                            op("c", lambda E, c8=c8, tb=tb, i=i: E.mul(
                                xnT[:, c8, i * 128:(i + 1) * 128], ps1[:, tb * 512:tb * 512 + 128],
                                nwT[:, c8:c8 + 1]),
                               reads=[("ps1", tb), ("nwT",)], writes=[("xnT", c8)])
                    for qc in range(8):
                        qb = 2 + qc % 2
                        for dc in range(8):
                            op("t", lambda E, qc=qc, dc=dc, qb=qb: E.matmul(
                                ps1[:, qb * 512:qb * 512 + 128], wqb[:, dc, qc * 128:(qc + 1) * 128],
                                xnT[:, dc, 0:128], start=(dc == 0), stop=(dc == 7)),
                               reads=[("wqb",), ("xnT", dc)], writes=[("ps1", qb)])
                        op("v", lambda E, qc=qc, qb=qb, hw=hw, hb=hb: E.tensor_mul(
                            QT[:, qc, hw * 128:(hw + 1) * 128], ps1[:, qb * 512:qb * 512 + 128],
                            rfs[:, hb, 0:128]),
                           reads=[("ps1", qb), ("rfs", hb)], writes=[("QT",)])
                    for kc2 in range(2):
                        kb = 4 + kc2
                        for dc in range(8):
                            op("t", lambda E, kc2=kc2, dc=dc, kb=kb: E.matmul(
                                ps1[:, kb * 512:kb * 512 + 256], wkb[:, dc, kc2 * 128:(kc2 + 1) * 128],
                                xnT[:, dc, 0:256], start=(dc == 0), stop=(dc == 7)),
                               reads=[("wkb",), ("xnT", dc)], writes=[("ps1", kb)])
                        op("v", lambda E, kc2=kc2, kb=kb, t0=t0, hb=hb: E.tensor_mul(
                            KT[:, kc2, t0 * 128:t0 * 128 + 256], ps1[:, kb * 512:kb * 512 + 256],
                            rfs[:, hb, 0:256]),
                           reads=[("ps1", kb), ("rfs", hb)], writes=[("KT",)])
                    for i in range(2):
                        vb = 6 + i
                        tl = t0 + i
                        for dc in range(8):
                            op("t", lambda E, i=i, dc=dc, vb=vb: E.matmul(
                                ps1[:, vb * 512:vb * 512 + 256], xnT[:, dc, i * 128:(i + 1) * 128],
                                wvb[:, dc, :], start=(dc == 0), stop=(dc == 7)),
                               reads=[("wvb",), ("xnT", dc)], writes=[("ps1", vb)])
                        op("v", lambda E, vb=vb, tl=tl: E.tensor_copy(
                            VA[:, tl, :].rearrange("p (k s) -> p k s", s=65)[:, :, 0:64],
                            ps1[:, vb * 512:vb * 512 + 256].rearrange("p (k s) -> p k s", s=64)),
                           reads=[("ps1", vb), ("vaones",)], writes=[("VA", tl)])

                bld.mark("m0")
                msc = wqb
                op("v", lambda E: E.memset(m0sc[0:1, 0:1], 0.0), writes=[("m0",)])
                op("v", lambda E: E.memset(m0sc[0:1, 2:3], 0.0), writes=[("m0",)])
                for qc in range(8):
                    for half in range(2):
                        op("v", lambda E, qc=qc, half=half: E.tensor_mul(
                            msc[:, half, :], QT[:, qc, half * 1024:(half + 1) * 1024],
                            QT[:, qc, half * 1024:(half + 1) * 1024]),
                           reads=[("QT",)], writes=[("msc",), ("wqb",)])
                    for pc in range(4):
                        op("t", lambda E, pc=pc: E.matmul(
                            ps1[0:1, 0:512], onesb[:], msc[:, pc // 2, (pc % 2) * 512:(pc % 2) * 512 + 512],
                            start=True, stop=True),
                           reads=[("msc",), ("onesb",)], writes=[("ps1", 0)])
                        op("v", lambda E: E.reduce_max(m0sc[0:1, 1:2], ps1[0:1, 0:512], axis=AXX),
                           reads=[("ps1", 0)], writes=[("m0",)])
                        op("v", lambda E: E.tensor_max(m0sc[0:1, 0:1], m0sc[0:1, 0:1], m0sc[0:1, 1:2]),
                           reads=[("m0",)], writes=[("m0",)])
                for kc2 in range(2):
                    for half in range(4):
                        op("v", lambda E, kc2=kc2, half=half: E.tensor_mul(
                            msc[:, half % 2, :], KT[:, kc2, half * 1024:(half + 1) * 1024],
                            KT[:, kc2, half * 1024:(half + 1) * 1024]),
                           reads=[("KT",)], writes=[("msc",), ("wqb",)])
                        for pc in range(2):
                            op("t", lambda E, half=half, pc=pc: E.matmul(
                                ps1[0:1, 0:512], onesb[:], msc[:, half % 2, pc * 512:(pc + 1) * 512],
                                start=True, stop=True),
                               reads=[("msc",), ("onesb",)], writes=[("ps1", 0)])
                            op("v", lambda E: E.reduce_max(m0sc[0:1, 3:4], ps1[0:1, 0:512], axis=AXX),
                               reads=[("ps1", 0)], writes=[("m0",)])
                            op("v", lambda E: E.tensor_max(m0sc[0:1, 2:3], m0sc[0:1, 2:3], m0sc[0:1, 3:4]),
                               reads=[("m0",)], writes=[("m0",)])
                op("v", lambda E: E.tensor_add(m0sc[0:1, 4:5], m0sc[0:1, 0:1], m0sc[0:1, 2:3]),
                   reads=[("m0",)], writes=[("m0",)])
                op("v", lambda E: E.tensor_scalar_mul(m0sc[0:1, 4:5], m0sc[0:1, 4:5], -0.0625),
                   reads=[("m0",)], writes=[("m0",)])
                op("t", lambda E: E.matmul(ps1[:, 512:513], ones_r[:], m0sc[0:1, 4:5], start=True, stop=True),
                   reads=[("m0",), ("onesr",)], writes=[("ps1", 1)])
                op("v", lambda E: E.tensor_copy(negM0[:], ps1[:, 512:513]),
                   reads=[("ps1", 1)], writes=[("negM0",)])

                for ee in ENGS:
                    bld.barrier_into(ee)

            if LV >= 2:
                bld.mark("attn")
                with ExitStack() as ctxB2:
                    eB2 = ctxB2.enter_context
                    Psl = eB2(nc.sbuf_tensor("sb_Psl", [128, 2, 4, 512], BF16))
                    linv = eB2(nc.sbuf_tensor("sb_linv", [1, 1024], F32))
                    lnl = eB2(nc.sbuf_tensor("sb_lnl", [1, 1024], F32))
                    lrep = eB2(nc.sbuf_tensor("sb_lrep", [128, 2, 512], F32))
                    stage = eB2(nc.sbuf_tensor("sb_stage", [65, 2, 1024], F32))
                    psA = eB2(nc.psum_tensor("pm_psA", [128, 3072], F32))
                    psO = eB2(nc.psum_tensor("pm_psO", [128, 1024], F32))

                    for g in range(4):
                        ext = 8 * g + 8
                        for c in range(8):
                            kch = c % 2
                            kv_e = (2 * c) % 4
                            sg = (8 * g + c) % 2
                            batches, k0 = [], 0
                            while k0 < 8 * g:
                                sz = 2 if (len(batches) % 2 == 0) else 1
                                sz = min(sz, 8 * g - k0)
                                batches.append(list(range(k0, k0 + sz)))
                                k0 += sz
                            for n in range(8):
                                batches.append([8 * g + n])
                            for bi2, kcs in enumerate(batches):
                                slot = 0 if len(batches[bi2]) == 2 else (bi2 % 2)
                                if len(kcs) == 2:
                                    slot = 0 if (bi2 % 2 == 0) else 0
                                if len(kcs) == 2:
                                    slot, bank0 = 0, 0
                                else:
                                    slot = bi2 % 2
                                    bank0 = 0 if slot == 0 else 4
                                n0 = kcs[0] - 8 * g
                                mstart = max(0, n0 // 2) if kcs[0] >= 8 * g else 0
                                w = 512 - mstart * 128
                                for ki, kc in enumerate(kcs):
                                    for h2 in range(2):
                                        bk = bank0 + ki * 2 + h2
                                        op("t", lambda E, kch=kch, kc=kc, h2=h2, c=c, g=g, bk=bk, mstart=mstart, w=w: E.matmul(
                                            psA[:, bk * 512 + mstart * 128:bk * 512 + mstart * 128 + w],
                                            KT[h2 * 64:(h2 + 1) * 64, kch, kc * 128:(kc + 1) * 128],
                                            QT[h2 * 64:(h2 + 1) * 64, c, g * 512 + mstart * 128:(g + 1) * 512],
                                            start=True, stop=True, tile_position=(h2 * 64, 0)),
                                           reads=[("KT",), ("QT",)], writes=[("psA", bk)])
                                nb = len(kcs) * 2
                                if mstart > 0:
                                    for ti in range(nb):
                                        op("v", lambda E, slot=slot, ti=ti, mstart=mstart: E.memset(
                                            Psl[:, slot, ti, 0:mstart * 128], 0.0),
                                           writes=[("Pz", slot)])
                                op("c", lambda E, bank0=bank0, nb=nb, slot=slot, mstart=mstart, w=w: E.activation(
                                    Psl[:, slot, 0:nb, mstart * 128:512],
                                    psA[:, bank0 * 512:(bank0 + nb) * 512].rearrange(
                                        "p (b f) -> p b f", f=512)[:, :, mstart * 128:512] if mstart else
                                    psA[:, bank0 * 512:(bank0 + nb) * 512],
                                    AF.Exp, bias=negM0[:, 0:1], scale=0.125),
                                   reads=[("negM0",), ("Pz", slot)] + [("psA", bank0 + j) for j in range(nb)],
                                   writes=[("P", slot)])
                                for ki, kc in enumerate(kcs):
                                    if kc < 8 * g:
                                        continue
                                    n = kc - 8 * g
                                    m = n // 2
                                    for h2 in range(2):
                                        ti = ki * 2 + h2
                                        if n % 2 == 0:
                                            msk, mk = tri_b, ("tri",)
                                        else:
                                            msk, mk = zf_b, ("zfb",)
                                        op("v", lambda E, slot=slot, ti=ti, m=m, msk=msk: E.tensor_tensor(
                                            Psl[:, slot, ti, m * 128:(m + 1) * 128],
                                            Psl[:, slot, ti, m * 128:(m + 1) * 128],
                                            msk[:, :], ALU.mult),
                                           reads=[mk], writes=[("P", slot)])
                                for ki, kc in enumerate(kcs):
                                    for h2 in range(2):
                                        kv = kv_e + h2
                                        op("t", lambda E, kc=kc, h2=h2, kv=kv, slot=slot, ki=ki, ext=ext, mstart=mstart, g=g: E.matmul(
                                            psO[0:65, h2 * 512 + mstart * 128:(h2 + 1) * 512],
                                            VA[:, kc, kv * 65:kv * 65 + 65],
                                            Psl[:, slot, ki * 2 + h2, mstart * 128:512],
                                            start=(kc == 0), stop=(kc == ext - 1)),
                                           reads=[("P", slot), ("Pz", slot), ("VA", kc)], writes=[("psO", h2)])
                            op("v", lambda E, sg=sg: E.tensor_copy(stage[:, sg, :], psO[0:65, :]),
                               reads=[("psO", 0), ("psO", 1)], writes=[("stg", sg)])
                            op("c", lambda E, sg=sg: E.activation(lnl[0:1, :], stage[64:65, sg, :], AF.Ln),
                               reads=[("stg", sg)], writes=[("lnl",)])
                            op("c", lambda E: E.activation(linv[0:1, :], lnl[0:1, :], AF.Exp, scale=-1.0),
                               reads=[("lnl",)], writes=[("linv",)])
                            for h2 in range(2):
                                op("t", lambda E, h2=h2: E.matmul(
                                    psO[0:128, h2 * 512:(h2 + 1) * 512], ones_r[:],
                                    linv[0:1, h2 * 512:(h2 + 1) * 512],
                                    start=True, stop=True),
                                   reads=[("linv",), ("onesr",)], writes=[("psO", h2)])
                                op("v", lambda E, h2=h2: E.tensor_copy(lrep[:, h2, :], psO[:, h2 * 512:(h2 + 1) * 512]),
                                   reads=[("psO", h2)], writes=[("lrep", h2)])
                                op("v", lambda E, h2=h2, c=c, g=g, sg=sg: E.tensor_mul(
                                    fT[h2 * 64:h2 * 64 + 64, c, g * 512:(g + 1) * 512],
                                    stage[0:64, sg, h2 * 512:(h2 + 1) * 512], lrep[0:64, h2, :]),
                                   reads=[("stg", sg), ("lrep", h2)],
                                   writes=[("fT", 4 * g + mm) for mm in range(4)])

            if LV < 5:
                for ee in ENGS:
                    bld.barrier_into(ee)
                ns = 0
                op("g", lambda E: E.dma_start(out=dbg["qt"][:, :, :], in_=QT[:]), inc=16, sem=OUTSEM); ns += 16
                op("g", lambda E: E.dma_start(out=dbg["kt"][:, :, :], in_=KT[:]), inc=16, sem=OUTSEM); ns += 16
                op("g", lambda E: E.dma_start(out=dbg["va"][:, :, :], in_=VA[:]), inc=16, sem=OUTSEM); ns += 16
                dbg_ns = ns

            for ee in ENGS:
                bld.barrier_into(ee)
            ctxB.close()

            if LV >= 3:
                bld.mark("oproj")
                with ExitStack() as ctxC:
                    eC = ctxC.enter_context
                    wob = eC(nc.sbuf_tensor("sb_wob", [128, 8, D], BF16))
                    ln2sc = eC(nc.sbuf_tensor("sb_ln2sc", [128, 2, D], F32))
                    st2 = eC(nc.sbuf_tensor("sb_st2", [128, 2, 8], F32))
                    rsc = eC(nc.sbuf_tensor("sb_rsc", [1, 64], F32))
                    poolrow = eC(nc.sbuf_tensor("sb_poolrow", [1, D], F32))
                    b2r = eC(nc.sbuf_tensor("sb_b2r", [1, 2, D], F32))
                    b2g = eC(nc.sbuf_tensor("sb_b2g", [8, D], F32))
                    psC = eC(nc.psum_tensor("pm_psC", [128, 4096], F32))

                    for dc in range(8):
                        op("g", lambda E, dc=dc: E.dma_start(out=wob[:, dc, :], in_=d_wo[dc * 128:(dc + 1) * 128, :]),
                           writes=[("wob",)], inc=16)

                    def emit_oproj(j):
                        ob = (j % 2) * 2
                        for dh in range(2):
                            for cc in range(8):
                                op("t", lambda E, j=j, dh=dh, cc=cc, ob=ob: E.matmul(
                                    psC[:, (ob + dh) * 512:(ob + dh + 1) * 512],
                                    fT[:, cc, j * 128:(j + 1) * 128], wob[:, cc, dh * 512:(dh + 1) * 512],
                                    start=(cc == 0), stop=(cc == 7)),
                                   reads=[("fT", j), ("wob",)], writes=[("psC", ob + dh)])
                        for dh in range(2):
                            op("v", lambda E, j=j, dh=dh, ob=ob: E.tensor_add(
                                r1[:, j, dh * 512:(dh + 1) * 512], r1[:, j, dh * 512:(dh + 1) * 512],
                                psC[:, (ob + dh) * 512:(ob + dh + 1) * 512]),
                               reads=[("psC", ob + dh), ("r1", j)], writes=[("r1", j)])
                        jp = j % 2
                        L2 = ln2sc[:, jp, :]
                        S2 = st2[:, jp, :]
                        op("c", lambda E, j=j, L2=L2, jp=jp: E.activation(L2, r1[:, j, :], AF.Square),
                           reads=[("r1", j)], writes=[("l2s", jp)])
                        op("v", lambda E, S2=S2, L2=L2, jp=jp: E.reduce_sum(S2[:, 1:2], L2, axis=AXX),
                           reads=[("l2s", jp)], writes=[("s2k", jp)])
                        op("v", lambda E, j=j, S2=S2, jp=jp: E.reduce_sum(S2[:, 0:1], r1[:, j, :], axis=AXX),
                           reads=[("r1", j)], writes=[("s2k", jp)])
                        op("v", lambda E, S2=S2, jp=jp: E.tensor_scalar_mul(S2[:, 2:3], S2[:, 0:1], 1.0 / D),
                           reads=[("s2k", jp)], writes=[("s2k", jp)])
                        op("v", lambda E, S2=S2, jp=jp: E.tensor_scalar_mul(S2[:, 3:4], S2[:, 1:2], 1.0 / D),
                           reads=[("s2k", jp)], writes=[("s2k", jp)])
                        op("v", lambda E, S2=S2, jp=jp: E.tensor_mul(S2[:, 4:5], S2[:, 2:3], S2[:, 2:3]),
                           reads=[("s2k", jp)], writes=[("s2k", jp)])
                        op("v", lambda E, S2=S2, jp=jp: E.tensor_sub(S2[:, 3:4], S2[:, 3:4], S2[:, 4:5]),
                           reads=[("s2k", jp)], writes=[("s2k", jp)])
                        op("v", lambda E, S2=S2, jp=jp: E.tensor_scalar_add(S2[:, 5:6], S2[:, 3:4], EPS1),
                           reads=[("s2k", jp)], writes=[("s2k", jp)])
                        op("c", lambda E, S2=S2, jp=jp: E.activation(S2[:, 6:7], S2[:, 5:6], AF.Ln),
                           reads=[("s2k", jp)], writes=[("s2k", jp)])
                        op("c", lambda E, S2=S2, jp=jp: E.activation(S2[:, 6:7], S2[:, 6:7], AF.Exp, scale=-0.5),
                           reads=[("s2k", jp)], writes=[("s2k", jp)])
                        op("v", lambda E, j=j, S2=S2, L2=L2, jp=jp: E.tensor_scalar(
                            L2, r1[:, j, :], S2[:, 2:3], S2[:, 6:7], ALU.subtract, ALU.mult),
                           reads=[("r1", j), ("s2k", jp)], writes=[("l2s", jp)])

                    def emit_ln2pe(j):
                        jp = j % 2
                        L2 = ln2sc[:, jp, :]
                        for c8 in range(8):
                            tb = 4 + c8 % 2
                            op("t", lambda E, c8=c8, tb=tb, L2=L2, jp=jp: E.transpose(
                                psC[:, tb * 512:tb * 512 + 128], L2[:, c8 * 128:(c8 + 1) * 128], id128[:]),
                               reads=[("l2s", jp), ("id",)], writes=[("psC", tb)])
                            op("c", lambda E, c8=c8, tb=tb, j=j: E.copy(
                                fT[:, c8, j * 128:(j + 1) * 128], psC[:, tb * 512:tb * 512 + 128]),
                               reads=[("psC", tb)], writes=[("fT", j)])
                        for dh in range(2):
                            op("t", lambda E, j=j, dh=dh, L2=L2, jp=jp: E.matmul(
                                psC[0:1, (6 + dh) * 512:(6 + dh) * 512 + 512], onesf[:],
                                L2[:, dh * 512:(dh + 1) * 512], start=(j == 0), stop=(j == NSLOT - 1)),
                               reads=[("l2s", jp), ("onesf",)], writes=[("psC", 6 + dh)])

                    for j in range(NSLOT + 1):
                        if j < NSLOT:
                            emit_oproj(j)
                        if j >= 1:
                            emit_ln2pe(j - 1)

                    if LV >= 4:
                        bld.mark("route")
                        op("v", lambda E: E.tensor_copy(poolrow[0:1, 0:512], psC[0:1, 6 * 512:6 * 512 + 512]),
                           reads=[("psC", 6)], writes=[("poolrow",)])
                        op("v", lambda E: E.tensor_copy(poolrow[0:1, 512:1024], psC[0:1, 7 * 512:7 * 512 + 512]),
                           reads=[("psC", 7)], writes=[("poolrow",)])
                        op("s", lambda E: E.dma_start(out=cc_in[:, :], in_=poolrow[:]),
                           reads=[("poolrow",)], writes=[("ccin",)], inc=16)
                        op("g", lambda E: E.collective_compute(
                            "AllReduce", ALU.add,
                            replica_groups=[[0, 1], [2, 3], [4, 5], [6, 7]],
                            ins=[cc_in.ap().opt()], outs=[cc_out.ap().opt()]),
                           reads=[("ccin",)], writes=[("ccout",)], inc=1, sem=CCSEM)
                        op("g", lambda E: E.dma_start(
                            out=poolT[:], in_=cc_out.ap().rearrange("one (p c) -> (one p) c", c=8)),
                           reads=[("ccout",)], writes=[("poolT",)], inc=16, sem=RSEM)
                        op("v", lambda E: E.tensor_scalar_mul(poolT[:], poolT[:], 1.0 / N),
                           reads=[("poolT",)], writes=[("poolT",)])
                        for cb in range(8):
                            op("t", lambda E, cb=cb: E.matmul(
                                psC[0:1, 6 * 512:6 * 512 + 8], poolT[:, cb:cb + 1], wr_s[:, cb * 8:(cb + 1) * 8],
                                start=(cb == 0), stop=(cb == 7)),
                               reads=[("poolT",), ("wr",)], writes=[("psC", 6)])
                        op("v", lambda E: E.tensor_copy(rsc[0:1, 0:8], psC[0:1, 6 * 512:6 * 512 + 8]),
                           reads=[("psC", 6)], writes=[("rsc",)])
                        op("v", lambda E: E.reduce_max(rsc[0:1, 8:9], rsc[0:1, 0:8], axis=AXX),
                           reads=[("rsc",)], writes=[("rsc",)])
                        op("v", lambda E: E.tensor_single_scalar(rsc[0:1, 16:24], rsc[0:1, 0:8], rsc[0:1, 8:9], ALU.is_ge),
                           reads=[("rsc",)], writes=[("rsc",)])
                        op("v", lambda E: E.tensor_mul(rsc[0:1, 24:32], rsc[0:1, 16:24], io8[0:1, :]),
                           reads=[("rsc",), ("io8",)], writes=[("rsc",)])
                        op("v", lambda E: E.reduce_sum(rsc[0:1, 56:57], rsc[0:1, 24:32], axis=AXX),
                           reads=[("rsc",)], writes=[("rsc",)])
                        op("v", lambda E: E.scalar_tensor_tensor(
                            rsc[0:1, 32:40], rsc[0:1, 16:24], -1e30, rsc[0:1, 0:8], ALU.mult, ALU.add),
                           reads=[("rsc",)], writes=[("rsc",)])
                        op("v", lambda E: E.reduce_max(rsc[0:1, 10:11], rsc[0:1, 32:40], axis=AXX),
                           reads=[("rsc",)], writes=[("rsc",)])
                        op("v", lambda E: E.tensor_single_scalar(rsc[0:1, 40:48], rsc[0:1, 32:40], rsc[0:1, 10:11], ALU.is_ge),
                           reads=[("rsc",)], writes=[("rsc",)])
                        op("v", lambda E: E.tensor_mul(rsc[0:1, 48:56], rsc[0:1, 40:48], io8[0:1, :]),
                           reads=[("rsc",), ("io8",)], writes=[("rsc",)])
                        op("v", lambda E: E.reduce_sum(rsc[0:1, 57:58], rsc[0:1, 48:56], axis=AXX),
                           reads=[("rsc",)], writes=[("rsc",)])
                        op("v", lambda E: E.tensor_sub(rsc[0:1, 12:13], rsc[0:1, 10:11], rsc[0:1, 8:9]),
                           reads=[("rsc",)], writes=[("rsc",)])
                        op("c", lambda E: E.activation(rsc[0:1, 13:14], rsc[0:1, 12:13], AF.Exp),
                           reads=[("rsc",)], writes=[("rsc",)])
                        op("v", lambda E: E.tensor_scalar_add(rsc[0:1, 14:15], rsc[0:1, 13:14], 1.0),
                           reads=[("rsc",)], writes=[("rsc",)])
                        op("v", lambda E: E.reciprocal(rsc[0:1, 58:59], rsc[0:1, 14:15]),
                           reads=[("rsc",)], writes=[("rsc",)])
                        op("v", lambda E: E.tensor_mul(rsc[0:1, 59:60], rsc[0:1, 13:14], rsc[0:1, 58:59]),
                           reads=[("rsc",)], writes=[("rsc",)])
                        op("t", lambda E: E.matmul(psC[:, 7 * 512:7 * 512 + 4], ones_r[:], rsc[0:1, 56:60],
                                                   start=True, stop=True),
                           reads=[("rsc",), ("onesr",)], writes=[("psC", 7)])
                        op("v", lambda E: E.tensor_copy(rt[:, 0:4], psC[:, 7 * 512:7 * 512 + 4]),
                           reads=[("psC", 7)], writes=[("rt",)])
                        for sx in range(2):
                            op("v", lambda E, sx=sx: E.tensor_scalar_mul(rt[:, 4:5], rt[:, sx:sx + 1], 1024.0),
                               reads=[("rt",)], writes=[("rt",)])
                            op("v", lambda E: E.tensor_single_scalar(idxf[:, 0:8], iotas[:, 0:8], rt[:, 4:5], ALU.add),
                               reads=[("rt",), ("iotas",)], writes=[("idxf",)])
                            op("v", lambda E, sx=sx: E.tensor_copy(idxW1[:, sx, :], idxf[:, 0:8]),
                               reads=[("idxf",)], writes=[("idxW1",)])
                            op("v", lambda E, sx=sx: E.tensor_scalar_mul(rt[:, 5:6], rt[:, sx:sx + 1], 4096.0),
                               reads=[("rt",)], writes=[("rt",)])
                            op("v", lambda E: E.tensor_single_scalar(idxf[:, 0:32], iotas[:, 0:32], rt[:, 5:6], ALU.add),
                               reads=[("rt",), ("iotas",)], writes=[("idxf",)])
                            op("v", lambda E, sx=sx: E.tensor_copy(idxW2[:, sx, :], idxf[:, 0:32]),
                               reads=[("idxf",)], writes=[("idxW2",)])
                            op("v", lambda E, sx=sx: E.tensor_scalar_mul(rt[:, 6:7], rt[:, sx:sx + 1], 32.0),
                               reads=[("rt",)], writes=[("rt",)])
                            op("v", lambda E: E.tensor_single_scalar(idxf[0:32, 0:1], iotas[0:32, 0:1], rt[0:32, 6:7], ALU.add),
                               reads=[("rt",), ("iotas",)], writes=[("idxf",)])
                            op("v", lambda E, sx=sx: E.tensor_copy(idxb1[:, sx:sx + 1], idxf[0:32, 0:1]),
                               reads=[("idxf",)], writes=[("idxb1",)])
                            op("v", lambda E: E.tensor_scalar_mul(idxf[0:8, 1:2], iotas[0:8, 0:1], 1.0e6),
                               reads=[("iotas",)], writes=[("idxf",)])
                            op("v", lambda E, sx=sx: E.tensor_single_scalar(
                                idxf[0:8, 1:2], idxf[0:8, 1:2], rt[0:8, sx:sx + 1], ALU.add),
                               reads=[("rt",)], writes=[("idxf",)])
                            op("v", lambda E, sx=sx: E.tensor_copy(idxb2[:, sx:sx + 1], idxf[0:8, 1:2]),
                               reads=[("idxf",)], writes=[("idxb2",)])
                        for sx in range(2):
                            op("g", lambda E, sx=sx: E.indirect_dma_start(
                                b2g[:, :], None, d_b2.ap(), IndirectOffsetOnAxis(idxb2[:, sx:sx + 1], 0),
                                bounds_check=NE - 1, oob_is_err=False),
                               reads=[("idxb2",)], writes=[("b2g",)], inc=16, sem=RSEM)
                            wcol = 58 + sx
                            op("v", lambda E, sx=sx, wcol=wcol: E.tensor_single_scalar(
                                b2r[0:1, sx, :], b2g[0:1, :], rsc[0:1, wcol:wcol + 1], ALU.mult),
                               reads=[("b2g",), ("rsc",)], writes=[("b2r",)])
                        op("v", lambda E: E.tensor_add(b2r[0:1, 0, :], b2r[0:1, 0, :], b2r[0:1, 1, :]),
                           reads=[("b2r",)], writes=[("b2r",)])
                        for dh in range(2):
                            op("t", lambda E, dh=dh: E.matmul(
                                psC[:, (6 + dh) * 512:(6 + dh) * 512 + 512], ones_r[:],
                                b2r[0:1, 0, dh * 512:(dh + 1) * 512], start=True, stop=True),
                               reads=[("b2r",), ("onesr",)], writes=[("psC", 6 + dh)])
                            op("v", lambda E, dh=dh: E.tensor_copy(
                                b2c[:, dh * 512:(dh + 1) * 512], psC[:, (6 + dh) * 512:(6 + dh) * 512 + 512]),
                               reads=[("psC", 6 + dh)], writes=[("b2c",)])

            if LV >= 5:
                for ee in ENGS:
                    bld.barrier_into(ee)
                bld.mark("ffn")
                with ExitStack() as ctxD:
                    eD = ctxD.enter_context
                    W1sb = eD(nc.sbuf_tensor("sb_W1sb", [128, 8, FF], BF16))
                    hT = eD(nc.sbuf_tensor("sb_hT", [128, 32, 512], BF16))
                    W2sb = eD(nc.sbuf_tensor("sb_W2sb", [128, 2, D], BF16))
                    b1g = eD(nc.sbuf_tensor("sb_b1g", [32, 128], F32))
                    b1T = eD(nc.sbuf_tensor("sb_b1T", [128, 32], F32))
                    psF = eD(nc.psum_tensor("pm_psF", [128, 4096], F32))

                    for sx in range(2):
                        gatecol = rt[:, 2 + sx:3 + sx]
                        for dc in range(8):
                            op("g", lambda E, sx=sx, dc=dc: E.indirect_dma_start(
                                W1sb[:, dc, :], None, W1v, IndirectOffsetOnAxis(idxW1[:, sx, dc:dc + 1], 0)),
                               reads=[("idxW1",)], writes=[("W1sb",)], inc=16, sem=FSEM)
                        op("g", lambda E, sx=sx: E.indirect_dma_start(
                            b1g[:, :], None, b1v, IndirectOffsetOnAxis(idxb1[:, sx:sx + 1], 0)),
                           reads=[("idxb1",)], writes=[("b1g",)], inc=16, sem=RSEM)
                        op("t", lambda E: E.transpose(psF[:, 7 * 512:7 * 512 + 32], b1g[:, :], id128[0:32, 0:32]),
                           reads=[("b1g",), ("id",)], writes=[("psF", 7)])
                        op("v", lambda E: E.tensor_copy(b1T[:], psF[:, 7 * 512:7 * 512 + 32]),
                           reads=[("psF", 7)], writes=[("b1T",)])

                        for v4 in range(4):
                            for ft in range(32):
                                fb = ft % 2
                                for dc in range(8):
                                    op("t", lambda E, ft=ft, dc=dc, v4=v4, fb=fb: E.matmul(
                                        psF[:, fb * 512:(fb + 1) * 512], W1sb[:, dc, ft * 128:(ft + 1) * 128],
                                        fT[:, dc, v4 * 512:(v4 + 1) * 512], start=(dc == 0), stop=(dc == 7)),
                                       reads=[("W1sb",)] + [("fT", v4 * 4 + q) for q in range(4)],
                                       writes=[("psF", fb)])
                                op("c", lambda E, ft=ft, fb=fb: E.activation(
                                    hT[:, ft, :], psF[:, fb * 512:(fb + 1) * 512], AF.Gelu,
                                    bias=b1T[:, ft:ft + 1]),
                                   reads=[("psF", fb), ("b1T",)], writes=[("hT", ft)])
                            for fc in range(32):
                                par2 = fc % 2
                                op("g", lambda E, sx=sx, fc=fc, par2=par2: E.indirect_dma_start(
                                    W2sb[:, par2, :], None, W2v, IndirectOffsetOnAxis(idxW2[:, sx, fc:fc + 1], 0)),
                                   reads=[("idxW2",)], writes=[("W2sb", par2)], inc=16, sem=W2SEM[par2])
                                for tt in range(4):
                                    for dh in range(2):
                                        bk = tt * 2 + dh
                                        op("t", lambda E, fc=fc, tt=tt, dh=dh, par2=par2, bk=bk: E.matmul(
                                            psF[:, bk * 512:(bk + 1) * 512],
                                            hT[:, fc, tt * 128:(tt + 1) * 128],
                                            W2sb[:, par2, dh * 512:(dh + 1) * 512],
                                            start=(fc == 0), stop=(fc == 31)),
                                           reads=[("hT", fc), ("W2sb", par2)], writes=[("psF", bk)])
                            for tt in range(4):
                                for dh in range(2):
                                    bk = tt * 2 + dh
                                    jj = v4 * 4 + tt
                                    op("v", lambda E, bk=bk, jj=jj, dh=dh, gatecol=gatecol: E.scalar_tensor_tensor(
                                        r1[:, jj, dh * 512:(dh + 1) * 512], psF[:, bk * 512:(bk + 1) * 512],
                                        gatecol, r1[:, jj, dh * 512:(dh + 1) * 512], ALU.mult, ALU.add),
                                       reads=[("psF", bk), ("rt",), ("r1", jj)], writes=[("r1", jj)])

                    for j in range(NSLOT):
                        op("v", lambda E, j=j: E.tensor_add(r1[:, j, :], r1[:, j, :], b2c[:]),
                           reads=[("b2c",), ("r1", j)], writes=[("r1", j)])
                        op("s", lambda E, j=j: E.dma_start(out=d_out[j * 128:(j + 1) * 128, :], in_=r1[:, j, :]),
                           reads=[("r1", j)], inc=16, sem=OUTSEM)
                    raw("s", OUTSEM, 16 * NSLOT)

            if LV < 5:
                for ee in ENGS:
                    bld.barrier_into(ee)
                ns = dbg_ns
                for j in range(NSLOT):
                    op("s", lambda E, j=j: E.dma_start(out=dbg["r1"][j * 128:(j + 1) * 128, :], in_=r1[:, j, :]),
                       inc=16, sem=OUTSEM)
                    ns += 16
                if LV >= 2:
                    op("g", lambda E: E.dma_start(out=dbg["ft"][:, :, :], in_=fT[:]), inc=16, sem=OUTSEM); ns += 16
                if LV >= 4:
                    op("s", lambda E: E.dma_start(out=dbg["rt"][:, 0:4], in_=rt[:, 0:4]), inc=16, sem=OUTSEM); ns += 16
                    op("s", lambda E: E.dma_start(out=dbg["rt"][:, 4:5], in_=poolT[:, 0:1], allow_slow_non_contiguous=True), inc=16, sem=OUTSEM); ns += 16
                raw("s", OUTSEM, ns)

            def _run_eng(E, key):
                bounds = [(nm, idx[key]) for nm, idx in bld.marks] + [("end", len(bld.progs[key]))]
                for (nm, lo), (_, hi) in zip(bounds, bounds[1:]):
                    if hi <= lo:
                        continue
                    import os as _os
                    if _os.environ.get("KSCOPES", "1") == "1":
                        with nc.named_scope(nm):
                            for f in bld.progs[key][lo:hi]:
                                f(E)
                    else:
                        for f in bld.progs[key][lo:hi]:
                            f(E)

            with nc.Block() as block:
                @block.tensor
                def _(E):
                    _run_eng(E, "t")

                @block.vector
                def _(E):
                    _run_eng(E, "v")

                @block.scalar
                def _(E):
                    _run_eng(E, "c")

                @block.sync
                def _(E):
                    _run_eng(E, "s")

                @block.gpsimd
                def _(E):
                    _run_eng(E, "g")

    return nc


_NC_CACHE = {}


def _get_nc(upto="full"):
    if upto not in _NC_CACHE:
        _NC_CACHE[upto] = _build(upto)
    return _NC_CACHE[upto]


def _host_prep(inputs):
    x = np.ascontiguousarray(np.asarray(inputs["L0_attn_x"], dtype=np.float32))
    wq = np.ascontiguousarray(np.asarray(inputs["L0_attn_w_q"], dtype=np.float32))
    wk = np.ascontiguousarray(np.asarray(inputs["L0_attn_w_k"], dtype=np.float32))
    wv = np.ascontiguousarray(np.asarray(inputs["L0_attn_w_v"], dtype=np.float32))
    wo = np.ascontiguousarray(np.asarray(inputs["L0_attn_w_o"], dtype=np.float32))
    nw = np.asarray(inputs["L0_attn_norm_w"], dtype=np.float32)
    Wr = np.ascontiguousarray(np.asarray(inputs["L0_ffn_Wr"], dtype=np.float32))
    W1 = np.ascontiguousarray(np.asarray(inputs["L0_ffn_W1"], dtype=np.float32))
    b1 = np.ascontiguousarray(np.asarray(inputs["L0_ffn_b1"], dtype=np.float32))
    W2 = np.ascontiguousarray(np.asarray(inputs["L0_ffn_W2"], dtype=np.float32))
    b2 = np.ascontiguousarray(np.asarray(inputs["L0_ffn_b2"], dtype=np.float32))

    i = np.arange(DH)
    theta = np.power(10000.0, -2.0 * (i // 2).astype(np.float64) / DH)
    pos = np.arange(N, dtype=np.float64)
    ang = pos[:, None] * theta[None, :]
    sign = np.where(np.arange(N) % 2 == 0, -1.0, 1.0)
    rf = (np.cos(ang) + np.sin(ang) * sign[:, None]).astype(np.float32)
    rfT_base = np.concatenate([rf.T, rf.T], axis=0)

    nwT = np.ascontiguousarray(nw.reshape(8, 128).T)
    wr_r = np.ascontiguousarray(Wr.reshape(128, 64))

    in_maps = []
    for core in range(8):
        b, par = core // 2, core % 2
        xl = x[b]
        rfT = rfT_base
        if par:
            perm = np.arange(NT).reshape(NT // 2, 2)[:, ::-1].reshape(NT)
            xl = xl.reshape(NT, 128, D)[perm].reshape(N, D)
            rfT = rfT_base.reshape(128, NT, 128)[:, perm].reshape(128, N)
        zf = np.zeros((128, 128), np.float32) if par == 0 else np.ones((128, 128), np.float32)
        in_maps.append({
            "x": np.ascontiguousarray(xl), "wq": wq, "wk": wk, "wv": wv, "wo": wo,
            "nwT": nwT, "rfT": np.ascontiguousarray(rfT), "zf": zf, "wr": wr_r,
            "W1": W1, "b1": b1, "W2": W2, "b2": b2,
        })
    return in_maps


def _assemble(results, key="out"):
    out = np.empty((B, N, D), dtype=np.float32)
    for core in range(8):
        b, par = core // 2, core % 2
        ol = results[core][key]
        for j in range(NSLOT):
            at = 2 * j + par
            out[b, at * 128:(at + 1) * 128, :] = ol[j * 128:(j + 1) * 128, :]
    return out


def kernel(**inputs):
    nc = _get_nc("full")
    in_maps = _host_prep(inputs)
    res = run_bass_kernel_spmd(nc, in_maps, core_ids=list(range(8)))
    return _assemble(res.results)
